# version 11
# speedup vs baseline: 1.0289x; 1.0091x over previous
import contextlib
import numpy as np
import ml_dtypes
import concourse.bass as bass
import concourse.mybir as mybir
from concourse.bass_utils import run_bass_kernel_spmd

F32 = mybir.dt.float32
BF16 = mybir.dt.bfloat16
AF = mybir.ActivationFunctionType
ALU = mybir.AluOpType
AX = mybir.AxisListType
BF = ml_dtypes.bfloat16

T = 2048
D = 1024
NKC = D // 128
H0 = 16
DH0 = 64
CG = 256
DI = 2048
NDC = DI // 128
NS = 16
TCS = 512
HID = 512
H1 = 8
DH1 = 64
WIN = 128
NQ_SEL = 1024

ENGINES = ("sync", "tensor", "vector", "scalar", "gpsimd")

DEBUG_DUMPS = []


class Prog:

    def __init__(self, nc):
        self.nc = nc
        self.ops = {e: [] for e in ENGINES}
        self.sems = {}
        self.counts = {}
        self.last_wait = {e: {} for e in ENGINES}

    def new_sem(self, st, name):
        self.sems[name] = st.enter_context(self.nc.semaphore(name))
        self.counts[name] = 0

    def emit(self, eng, fn, waits=(), inc=None, incn=1):
        wl = []
        for s, v in waits:
            if s.startswith("dq_"):
                v = self.counts[s]
            if v > 0 and self.last_wait[eng].get(s, 0) < v:
                self.last_wait[eng][s] = v
                wl.append((s, v))
        tok = None
        if inc is not None:
            self.counts[inc] += incn
            tok = (inc, self.counts[inc])
        sems = self.sems

        def closure(h, wl=wl, fn=fn, inc=inc, incn=incn):
            for s, v in wl:
                h.wait_ge(sems[s], v)
            r = fn(h)
            if inc is not None:
                r.then_inc(sems[inc], incn)
        self.ops[eng].append(closure)
        return tok

    def wait_only(self, eng, waits):
        self.emit(eng, lambda h: None, waits=waits)

    bank_tok = None

    def bwait(self, bank):
        if self.bank_tok is None:
            self.bank_tok = {}
        t = self.bank_tok.get(bank)
        return [t] if t else []

    def bread(self, bank, tok):
        if self.bank_tok is None:
            self.bank_tok = {}
        self.bank_tok[bank] = tok

    def dma(self, eng, out, in_, waits=()):
        sem = "dq_" + eng
        return self.emit(eng, lambda h: h.dma_start(out=out, in_=in_),
                         waits=waits, inc=sem, incn=16)


def ceil_div(a, b):
    return -(-a // b)


def build_program(dump_names=()):
    nc = bass.Bass("TRN2", target_bir_lowering=False, detect_race_conditions=False)
    dramf = {}

    def din(name, shape, dtype=F32):
        dramf[name] = nc.dram_tensor(name, list(shape), dtype, kind="ExternalInput").ap()
        return dramf[name]

    def dout(name, shape, dtype=F32):
        dramf[name] = nc.dram_tensor(name, list(shape), dtype, kind="ExternalOutput").ap()
        return dramf[name]

    x_in = din("x_fm", (128, NKC, T))
    out_fm = dout("out_fm", (128, NKC, T))
    for w in ("Wq", "Wk", "Wv", "Wg", "Wo"):
        din(w, (128, NKC, D))
    din("Wgt", (128, NKC, H0))
    din("gate_w", (128, NKC, 8))
    din("w13", (8, 128, NKC, 1024))
    din("w2", (8, 128, 4, D))
    din("W_inT", (128, NKC, 2 * DI))
    din("conv_w", (128, NDC, 4))
    din("conv_b", (128, NDC))
    din("W_xT", (128, NDC, 33))
    din("W_dt", (128, NDC))
    din("b_dt", (128, NDC))
    din("A_log", (128, NDC, NS))
    din("D_param", (128, NDC))
    din("W_outT", (128, NDC, D))
    din("gamma_light", (128, NKC))
    din("gamma_heavy", (128, NKC))
    din("wq_light", (128, NKC, HID))
    din("wkv_light", (128, NKC, 2 * HID))
    din("wo_light", (128, 4, D))
    din("q_rt", (128, NKC))
    din("WkT_heavy", (128, NKC, HID))
    din("WvT_heavy", (128, NKC, HID))
    din("wq_heavy", (128, NKC, HID))
    din("wo_heavy", (128, 4, D))
    din("ident", (128, 128), BF16)
    din("identf", (128, 128))
    din("onesb", (128, 8), BF16)
    din("onesr", (1, 128))
    din("onesrb", (1, 128), BF16)
    din("consts", (128, 16))
    din("trilm", (128, 2, CG), BF16)
    x_old_hbm = nc.dram_tensor("x_old_hbm", [128, NKC, T], F32, kind="Internal").ap()
    x_new_hbm = nc.dram_tensor("x_new_hbm", [128, NKC, T], F32, kind="Internal").ap()

    dumps = {}

    def register_dump(name, shape, dtype=F32):
        if name in dump_names:
            dumps[name] = dout("dump_" + name, shape, dtype)

    register_dump("xn", (128, NKC, T), BF16)
    register_dump("q_tok", (128, 16, D), BF16)
    register_dump("qraw", (128, 4096), BF16)
    register_dump("wslot", (128, 1024), BF16)
    register_dump("vtok", (128, 4096), BF16)
    register_dump("rows", (128, 2048))
    register_dump("ptile", (128, 1024), BF16)
    register_dump("o_tok", (128, 16, D), BF16)
    register_dump("x1", (128, NKC, T))
    register_dump("moe_xs", (128, NKC, T), BF16)
    register_dump("moe_scale", (8, 512))
    register_dump("xd15", (128, 2048), BF16)
    register_dump("route", (128, 144))
    register_dump("srow", (4, 2048))
    register_dump("xn2b", (128, NKC, T), BF16)
    register_dump("dbg1", (128, 8))
    register_dump("dbg2", (128, 16))
    register_dump("dbg3", (4, 2048))
    register_dump("u_ssm", (128, NDC, T), BF16)
    register_dump("ssm_in", (64, T))
    register_dump("delta0", (128, T))
    register_dump("y_ssm", (128, NKC, T))
    register_dump("xn2", (128, NKC, T), BF16)
    register_dump("mask_q", (128, 16))
    register_dump("light_o", (128, 16, HID), BF16)
    register_dump("x2", (128, NKC, T))

    with contextlib.ExitStack() as st:
        sb = lambda name, shape, dtype: st.enter_context(nc.sbuf_tensor(name, list(shape), dtype))
        xres = sb("xres", (128, NKC, T), F32)
        actA = sb("actA", (128, 8, T), BF16)
        actB = sb("actB", (128, 8, T), BF16)
        actC = sb("actC", (128, 8, T), BF16)
        wbuf = sb("wbuf", (128, 5, T), BF16)
        misc = sb("misc", (128, 4096), F32)
        pslab = sb("pslab", (128, 2048), BF16)
        sscr = pslab
        fscr = sb("fscr", (128, 1024), F32)
        ident = sb("identsb", (128, 128), BF16)
        identf = sb("identfsb", (128, 128), F32)
        onesb = sb("onesbsb", (128, 8), BF16)
        onesr = sb("onesrsb", (1, 128), F32)
        onesrb = sb("onesrbsb", (1, 128), BF16)
        consts = sb("constsb", (128, 16), F32)
        trilm = sb("trilmsb", (128, 2, CG), BF16)
        psum = [st.enter_context(nc.psum_tensor(f"ps{i}", [128, 512], F32)) for i in range(8)]

        p = Prog(nc)
        for s in ("dq_sync", "dq_gpsimd", "pe", "dv", "sc", "gp"):
            p.new_sem(st, s)
        block = st.enter_context(nc.Block())

        C_ONE, C_NEG1, C_EPS_LN, C_EPS_RMS, C_NEG1E30, C_QSCALE, C_NGATE, C_INVD, C_SQRTD, C_EPS12, C_LOGK = range(11)
        cc = lambda i: consts[:, i:i + 1]

        tok_x = p.dma("sync", xres[:], x_in)
        p.dma("sync", ident[:], dramf["ident"])
        p.dma("sync", identf[:], dramf["identf"])
        p.dma("sync", onesb[:], dramf["onesb"])
        p.dma("sync", onesr[:], dramf["onesr"])
        p.dma("sync", onesrb[:], dramf["onesrb"])
        p.dma("sync", consts[:], dramf["consts"])
        p.dma("sync", trilm[:], dramf["trilm"])
        if "srow" in dumps:
            p.dma("sync", dramf["dump_srow"][2:3, 1024:2048], dramf["q_rt"])
        base_loads = [("dq_sync", p.counts["dq_sync"])]

        def layernorm(xsrc_f32, dst_bf16, stat_off, waits, stage_bf16):
            toks = []
            for tc in range(4):
                t0 = tc * 512
                ct = []
                for kc in range(NKC):
                    ct.append(p.emit("vector", lambda h, kc=kc, t0=t0: h.tensor_copy(
                        stage_bf16[:, kc, t0:t0 + 512], xsrc_f32[:, kc, t0:t0 + 512]),
                        waits=waits, inc="dv"))
                sq_tok = [None, None]
                for kc in range(NKC):
                    par = kc % 2
                    s0 = 512 * par
                    w_sq = [sq_tok[par]] if sq_tok[par] else []
                    tsq = p.emit("scalar", lambda h, kc=kc, t0=t0, s0=s0: h.activation(
                        sscr[:, s0:s0 + 512], stage_bf16[:, kc, t0:t0 + 512], AF.Square),
                        waits=[ct[kc]] + w_sq, inc="sc")
                    p.emit("tensor", lambda h, kc=kc, t0=t0: h.matmul(
                        psum[0][0:1, 0:512], onesb[:, 0:1], stage_bf16[:, kc, t0:t0 + 512],
                        start=(kc == 0), stop=(kc == NKC - 1), skip_group_check=True),
                        waits=[ct[kc]], inc="pe")
                    sq_tok[par] = p.emit("tensor", lambda h, kc=kc, s0=s0: h.matmul(
                        psum[1][0:1, 0:512], onesb[:, 0:1], sscr[:, s0:s0 + 512],
                        start=(kc == 0), stop=(kc == NKC - 1), skip_group_check=True),
                        waits=[tsq], inc="pe")
                pe_tok = ("pe", p.counts["pe"])
                m0 = stat_off + t0
                r0 = stat_off + T + t0
                t1_ = p.emit("vector", lambda h, m0=m0: h.tensor_scalar(
                    misc[0:1, m0:m0 + 512], psum[0][0:1, 0:512], 1.0 / D, None, ALU.mult),
                    waits=[pe_tok], inc="dv")
                t2_ = p.emit("vector", lambda h: h.tensor_scalar(
                    fscr[0:1, 0:512], psum[1][0:1, 0:512], 1.0 / D, None, ALU.mult),
                    waits=[pe_tok], inc="dv")
                t3_ = p.emit("scalar", lambda h, m0=m0: h.activation(
                    fscr[0:1, 512:1024], misc[0:1, m0:m0 + 512], AF.Square),
                    waits=[t1_], inc="sc")
                tv = p.emit("vector", lambda h: h.tensor_sub(
                    fscr[0:1, 0:512], fscr[0:1, 0:512], fscr[0:1, 512:1024]),
                    waits=[t2_, t3_], inc="dv")
                tq = p.emit("scalar", lambda h: h.activation(
                    fscr[0:1, 512:1024], fscr[0:1, 0:512], AF.Sqrt,
                    bias=consts[0:1, C_EPS_LN:C_EPS_LN + 1]), waits=[tv], inc="sc")
                tr = p.emit("vector", lambda h, r0=r0: h.reciprocal(
                    misc[0:1, r0:r0 + 512], fscr[0:1, 512:1024]), waits=[tq], inc="dv")
                tbm = p.emit("tensor", lambda h, m0=m0: h.matmul(
                    psum[2][:, 0:512], onesr[:],
                    misc[0:1, m0:m0 + 512], start=True, stop=True),
                    waits=[t1_], inc="pe")
                tbr = p.emit("tensor", lambda h, r0=r0: h.matmul(
                    psum[3][:, 0:512], onesr[:],
                    misc[0:1, r0:r0 + 512], start=True, stop=True),
                    waits=[tr], inc="pe")
                for kc in range(NKC):
                    p.emit("vector", lambda h, kc=kc, t0=t0: h.tensor_sub(
                        fscr[:, 0:512], xsrc_f32[:, kc, t0:t0 + 512], psum[2][:, 0:512]),
                        waits=[tbm], inc="dv")
                    tm_ = p.emit("vector", lambda h, kc=kc, t0=t0: h.tensor_mul(
                        dst_bf16[:, kc, t0:t0 + 512], fscr[:, 0:512], psum[3][:, 0:512]),
                        waits=[tbr], inc="dv")
                    toks.append(tm_)
                waits = list(waits) + [t1_, t2_, ("dv", p.counts["dv"])]
            return toks

        ln1_toks = layernorm(xres, actA, 0, [tok_x] + base_loads, actC)
        ln1_done = [("dv", p.counts["dv"])]

        if "xn" in dumps:
            p.dma("sync", dumps["xn"], actA[:], waits=ln1_done)

        aB = actB.rearrange("p a b -> p (a b)")
        aC = actC.rearrange("p a b -> p (a b)")
        wb = wbuf.rearrange("p a b -> p (a b)")
        S_f = misc
        Sbf = aB
        psbf = [ps.ap().bitcast(BF16) for ps in psum]

        tz0 = p.emit("vector", lambda h: h.memset(misc[:, 0:512], 0.0), inc="dv")
        tz1 = p.emit("vector", lambda h: h.memset(aB[:, 12288:12800], 0.0), inc="dv")
        tz2 = p.emit("vector", lambda h: h.memset(misc[0:16, 3728:3984], 0.0), inc="dv")
        zinit = [tz0, tz1, tz2]

        def wload(dst_ap, src_ap, waits=()):
            return p.dma("gpsimd", dst_ap, src_ap, waits=waits)

        TC0 = 512
        sbf_tok = {hh: tz1 for hh in range(H0)}
        prev_tc_done = ln1_done + zinit
        for tc in range(4):
            t0 = tc * TC0
            dump_toks = []
            xn = actA
            tc_start = prev_tc_done

            def kc_slot(i):
                return wb[:, (i % 8) * 1024:(i % 8) * 1024 + 1024]

            vg_results = {}
            prev_phase_mm = []
            for wname, dcol in (("Wv", None), ("Wg", None)):
                ldtoks = []
                for kc in range(NKC):
                    ldtoks.append(wload(kc_slot(kc), dramf[wname][:, kc, :],
                                        waits=tc_start + prev_phase_mm))
                mm_last = []
                for kc in range(NKC):
                    for tt in range(4):
                        ta = t0 + tt * 128
                        for hf in range(2):
                            bank = tt * 2 + hf
                            w_ = [ldtoks[kc]] + tc_start
                            if kc == 0:
                                w_ += p.bwait(bank)
                            tokmm = p.emit("tensor", lambda h, kc=kc, ta=ta, hf=hf, bank=bank: h.matmul(
                                psum[bank][:, 0:512],
                                xn[:, kc, ta:ta + 128],
                                kc_slot(kc)[:, hf * 512:hf * 512 + 512],
                                start=(kc == 0), stop=(kc == NKC - 1), skip_group_check=True),
                                waits=w_, inc="pe" if kc == NKC - 1 else None)
                            if kc == NKC - 1:
                                mm_last.append(tokmm)
                vg_results[wname] = mm_last
                prev_phase_mm = mm_last
                if wname == "Wv":
                    v_toks = []
                    for tt in range(4):
                        e0 = p.emit("scalar", lambda h, tt=tt: h.copy(
                            pslab[:, 0:512], psum[tt * 2][:, 0:512]),
                            waits=mm_last + v_toks[-1:], inc="sc")
                        e1_ = p.emit("scalar", lambda h, tt=tt: h.copy(
                            pslab[:, 512:1024], psum[tt * 2 + 1][:, 0:512]), waits=mm_last, inc="sc")
                        p.bread(tt * 2, e0)
                        p.bread(tt * 2 + 1, e1_)
                        sq = p.emit("scalar", lambda h: h.activation(
                            pslab[:, 1024:2048], pslab[:, 0:1024], AF.Square,
                            accum_out=fscr[:, 0:1]), waits=[e0, e1_], inc="sc")
                        rs = p.emit("scalar", lambda h: h.activation(
                            fscr[:, 1:2], fscr[:, 0:1], AF.Sqrt, scale=1.0 / D,
                            bias=cc(C_EPS_RMS)), waits=[sq], inc="sc")
                        rr = p.emit("vector", lambda h: h.reciprocal(
                            fscr[:, 2:3], fscr[:, 1:2]), waits=[rs], inc="dv")
                        vt = p.emit("vector", lambda h, tt=tt: h.tensor_scalar(
                            aC[:, 4096 + tt * 1024:4096 + (tt + 1) * 1024],
                            pslab[:, 0:1024], fscr[:, 2:3], None, ALU.mult),
                            waits=[rr], inc="dv")
                        v_toks.append(vt)
                    if tc == 0 and "vtok" in dumps:
                        dump_toks.append(p.dma("sync", dumps["vtok"], aC[:, 4096:8192], waits=v_toks))
                    prev_phase_mm = v_toks
                    prev_phase_mm = mm_last
            g_toks = []
            for tt in range(4):
                for hf in range(2):
                    gtk = p.emit("scalar", lambda h, tt=tt, hf=hf: h.copy(
                        aB[:, 4096 + tt * 1024 + hf * 512:4096 + tt * 1024 + hf * 512 + 512],
                        psum[tt * 2 + hf][:, 0:512]), waits=vg_results["Wg"], inc="sc")
                    p.bread(tt * 2 + hf, gtk)
                    g_toks.append(gtk)

            qk_norm = []
            prev_w_mms = vg_results["Wg"]
            for wname, arena in (("Wq", aB), ("Wk", aC)):
                ldtoks = []
                raw_toks = []
                this_mms = []
                for blk in range(NKC):
                    ldtoks.append(wload(
                        wb[:, blk * 1024:blk * 1024 + 1024].rearrange("p (a b) -> p a b", a=8),
                        dramf[wname][:, :, blk * 128:(blk + 1) * 128],
                        waits=prev_w_mms if blk < 8 else []))
                for blk in range(NKC):
                    bank = blk % 2
                    for kc in range(NKC):
                        w_ = [ldtoks[blk]]
                        if kc == 0:
                            w_ += p.bwait(bank) + g_toks
                        tokmm = p.emit("tensor", lambda h, blk=blk, kc=kc, bank=bank, t0=t0: h.matmul(
                            psum[bank][:, 0:512],
                            wb[:, blk * 1024 + kc * 128:blk * 1024 + kc * 128 + 128],
                            xn[:, kc, t0:t0 + 512],
                            start=(kc == 0), stop=(kc == NKC - 1), skip_group_check=True),
                            waits=w_, inc="pe" if kc == NKC - 1 else None)
                    this_mms.append(tokmm)
                    rt = p.emit("scalar", lambda h, blk=blk, bank=bank, arena=arena: h.copy(
                        arena[:, blk * 512:(blk + 1) * 512], psum[bank][:, 0:512]),
                        waits=[tokmm], inc="sc")
                    p.bread(bank, rt)
                    raw_toks.append(rt)
                if tc == 0 and wname == "Wq" and "qraw" in dumps:
                    dump_toks.append(p.dma("sync", dumps["qraw"], aB[:, 0:4096], waits=raw_toks))
                if tc == 0 and wname == "Wq" and "wslot" in dumps:
                    wt_ = p.dma("sync", dumps["wslot"], wb[:, 0:1024], waits=[ldtoks[0]])
                    this_mms.append(wt_)
                    dump_toks.append(wt_)
                prev_w_mms = this_mms
                sq_prev = {}
                for blk in range(NKC):
                    par = blk % 2
                    w_ = [raw_toks[blk]]
                    if par in sq_prev:
                        w_.append(sq_prev[par])
                    sqt = p.emit("scalar", lambda h, blk=blk, par=par, arena=arena: h.activation(
                        pslab[:, par * 512:par * 512 + 512],
                        arena[:, blk * 512:(blk + 1) * 512], AF.Square),
                        waits=w_, inc="sc")
                    w2_ = [sqt]
                    if blk == 0:
                        w2_ += p.bwait(2)
                    mmt = p.emit("tensor", lambda h, blk=blk, par=par: h.matmul(
                        psum[2][0:1, 0:512], onesb[:, 0:1], pslab[:, par * 512:par * 512 + 512],
                        start=(blk == 0), stop=(blk == NKC - 1), skip_group_check=True),
                        waits=w2_, inc="pe")
                    sq_prev[par] = mmt
                r1 = p.emit("scalar", lambda h: h.activation(
                    fscr[0:1, 0:512], psum[2][0:1, 0:512], AF.Sqrt, scale=1.0 / D,
                    bias=consts[0:1, C_EPS_RMS:C_EPS_RMS + 1]),
                    waits=[sq_prev[0], sq_prev[1]], inc="sc")
                p.bread(2, r1)
                r2 = p.emit("vector", lambda h: h.reciprocal(
                    fscr[0:1, 512:1024], fscr[0:1, 0:512]), waits=[r1], inc="dv")
                if wname == "Wq":
                    r2 = p.emit("vector", lambda h: h.tensor_scalar(
                        fscr[0:1, 512:1024], fscr[0:1, 512:1024], 1.0 / 8.0, None,
                        ALU.mult), waits=[r2], inc="dv")
                r3 = p.emit("tensor", lambda h: h.matmul(
                    psum[3][:, 0:512], onesr[:], fscr[0:1, 512:1024],
                    start=True, stop=True), waits=[r2] + p.bwait(3), inc="pe")
                lastn = None
                for blk in range(NKC):
                    lastn = p.emit("vector", lambda h, blk=blk, arena=arena: h.tensor_mul(
                        arena[:, 8192 + blk * 512:8192 + (blk + 1) * 512],
                        arena[:, blk * 512:(blk + 1) * 512], psum[3][:, 0:512]),
                        waits=[r3], inc="dv")
                    qk_norm.append(lastn)
                p.bread(3, lastn)
            qk_done = [("dv", p.counts["dv"])]

            ldg = wload(wb[:, 0:128].rearrange("p (a b) -> p a b", a=8),
                        dramf["Wgt"], waits=prev_w_mms)
            for kc in range(NKC):
                w_ = [ldg]
                if kc == 0:
                    w_ += p.bwait(4)
                tokmm = p.emit("tensor", lambda h, kc=kc, t0=t0: h.matmul(
                    psum[4][0:16, 0:512], wb[:, kc * 16:kc * 16 + 16],
                    xn[:, kc, t0:t0 + 512],
                    start=(kc == 0), stop=(kc == NKC - 1), skip_group_check=True),
                    waits=w_, inc="pe" if kc == NKC - 1 else None)
            a1 = p.emit("scalar", lambda h: h.activation(
                fscr[0:16, 0:512], psum[4][0:16, 0:512], AF.Abs), waits=[tokmm], inc="sc")
            a2 = p.emit("scalar", lambda h: h.activation(
                fscr[0:16, 512:1024], fscr[0:16, 0:512], AF.Exp, scale=-1.0), waits=[a1], inc="sc")
            a3 = p.emit("scalar", lambda h: h.activation(
                fscr[0:16, 512:1024], fscr[0:16, 512:1024], AF.Ln,
                bias=consts[0:16, C_ONE:C_ONE + 1]), waits=[a2], inc="sc")
            a4 = p.emit("vector", lambda h: h.tensor_scalar(
                fscr[0:16, 0:512], psum[4][0:16, 0:512], -1.0, 0.0, ALU.mult, ALU.max),
                waits=[a2], inc="dv")
            p.bread(4, a4)
            a5 = p.emit("vector", lambda h: h.tensor_add(
                fscr[0:16, 0:512], fscr[0:16, 0:512], fscr[0:16, 512:1024]),
                waits=[a3, a4], inc="dv")
            a6 = p.emit("vector", lambda h: h.tensor_scalar(
                fscr[0:16, 0:512], fscr[0:16, 0:512], -1.0 / 16.0, None, ALU.mult),
                waits=[a5], inc="dv")
            bt = []
            for nn in range(2):
                bt.append(p.emit("vector", lambda h, nn=nn: h.tensor_tensor_scan(
                    misc[0:16, 2048 + nn * 256:2048 + (nn + 1) * 256],
                    misc[0:16, 3728:3984],
                    fscr[0:16, nn * 256:(nn + 1) * 256],
                    0.0, ALU.add, ALU.add), waits=[a6] + zinit, inc="dv"))
            e1t = p.emit("scalar", lambda h: h.activation(
                misc[0:16, 2560:3072], misc[0:16, 2048:2560], AF.Exp), waits=bt, inc="sc")
            e2ts = []
            for nn in range(2):
                w1_ = p.emit("vector", lambda h, nn=nn: h.tensor_scalar(
                    fscr[0:16, 512 + nn * 256:512 + (nn + 1) * 256],
                    misc[0:16, 2048 + nn * 256:2048 + (nn + 1) * 256],
                    misc[0:16, 2048 + nn * 256 + 255:2048 + nn * 256 + 256],
                    None, ALU.subtract), waits=bt, inc="dv")
                e2ts.append(p.emit("scalar", lambda h, nn=nn: h.activation(
                    misc[0:16, 3072 + nn * 256:3072 + (nn + 1) * 256],
                    fscr[0:16, 512 + nn * 256:512 + (nn + 1) * 256],
                    AF.Exp, scale=-1.0), waits=[w1_], inc="sc"))
            bct_toks = []
            for st4 in range(4):
                tr1 = p.emit("tensor", lambda h, st4=st4: h.transpose(
                    psum[5][:, 0:16], misc[0:16, 2048 + st4 * 128:2048 + st4 * 128 + 128],
                    identf[0:16, 0:16]), waits=bt + p.bwait(5), inc="pe")
                ev1 = p.emit("scalar", lambda h, st4=st4: h.copy(
                    misc[:, 3584 + st4 * 16:3584 + st4 * 16 + 16], psum[5][:, 0:16]),
                    waits=[tr1], inc="sc")
                p.bread(5, ev1)
                tr2 = p.emit("tensor", lambda h, st4=st4: h.transpose(
                    psum[5][:, 0:16], misc[0:16, 3072 + st4 * 128:3072 + st4 * 128 + 128],
                    identf[0:16, 0:16]), waits=e2ts + [ev1], inc="pe")
                ev2 = p.emit("scalar", lambda h, st4=st4: h.copy(
                    misc[:, 3648 + st4 * 16:3648 + st4 * 16 + 16], psum[5][:, 0:16]),
                    waits=[tr2], inc="sc")
                p.bread(5, ev2)
                bct_toks += [ev1, ev2]
            rows_done = bct_toks + [e1t] + e2ts

            if tc == 0 and "rows" in dumps:
                dump_toks.append(p.dma("sync", dumps["rows"], misc[:, 2048:4096],
                                       waits=rows_done))
            dump_hold_q = dump_toks
            if "q_tok" in dumps:
                dump_hold_q.append(p.dma(
                    "sync", dumps["q_tok"].rearrange("p a b -> p (a b)")[:, tc * 4096:(tc + 1) * 4096],
                    aB[:, 8192:12288], waits=qk_done))

            for nn in range(2):
                nc0 = nn * 256
                t_tr = p.emit("tensor", lambda h, nn=nn: h.transpose(
                    psum[6][0:1, 0:16],
                    misc[0:16, 2560 + nn * 256 + 255:2560 + nn * 256 + 256],
                    identf[0:16, 0:16]), waits=rows_done + p.bwait(6), inc="pe")
                t_ev = p.emit("scalar", lambda h: h.copy(
                    misc[0:1, 3984:4000], psum[6][0:1, 0:16]), waits=[t_tr], inc="sc")
                p.bread(6, t_ev)
                t_rep = p.emit("tensor", lambda h: h.matmul(
                    psum[6][:, 16:32], onesr[0:1, 0:128], misc[0:1, 3984:4000],
                    start=True, stop=True), waits=[t_ev] + p.bwait(6), inc="pe")
                t_e3 = p.emit("scalar", lambda h: h.copy(
                    misc[:, 3712:3728], psum[6][:, 16:32]), waits=[t_rep], inc="sc")
                p.bread(6, t_e3)

                for blk in range(NKC):
                    qb = 6 + blk % 2
                    t_r1 = None
                    for hh2 in range(2):
                        sel1 = identf[0:16, 2 * blk + hh2:2 * blk + hh2 + 1].to_broadcast((16, 64))
                        t_r1 = p.emit("tensor", lambda h, nn=nn, sel1=sel1, hh2=hh2, qb=qb: h.matmul(
                            psum[qb][hh2 * 64:hh2 * 64 + 64, 32:288], sel1,
                            misc[0:16, 2560 + nn * 256:2560 + (nn + 1) * 256],
                            start=True, stop=True, skip_group_check=True),
                            waits=rows_done + (p.bwait(qb) if hh2 == 0 else []), inc="pe" if hh2 == 1 else None)
                    t_qd = p.emit("vector", lambda h, blk=blk, nc0=nc0, qb=qb: h.tensor_mul(
                        pslab[:, 1536:1792],
                        aB[:, 8192 + blk * 512 + nc0:8192 + blk * 512 + nc0 + 256],
                        psum[qb][:, 32:288]),
                        waits=[t_r1] + qk_done + p.bwait("qd"), inc="dv")
                    p.bread(qb, t_qd)
                    for hh in range(2):
                        hglob = 2 * blk + hh
                        par = hglob % 2
                        hp0 = hh * 64
                        selh = identf[0:16, hglob:hglob + 1].to_broadcast((16, 128))
                        t_rep2 = p.emit("tensor", lambda h, nn=nn, par=par, selh=selh: h.matmul(
                            psum[par][:, 0:256], selh,
                            misc[0:16, 2048 + nn * 256:2048 + (nn + 1) * 256],
                            start=True, stop=True), waits=rows_done + p.bwait(par), inc="pe")
                        for ss in range(2):
                            st4 = nn * 2 + ss
                            t_ea = p.emit("vector", lambda h, par=par, ss=ss, st4=st4, hglob=hglob: h.scalar_tensor_tensor(
                                fscr[:, ss * 256:(ss + 1) * 256],
                                psum[par][:, 0:256],
                                misc[:, 3584 + st4 * 16 + hglob:3584 + st4 * 16 + hglob + 1],
                                trilm[:, ss, :],
                                ALU.subtract, ALU.add),
                                waits=[t_rep2] + p.bwait(("warg", ss)), inc="dv")
                            if ss == 1:
                                p.bread(par, t_ea)
                            t_w = p.emit("scalar", lambda h, ss=ss: h.activation(
                                pslab[:, ss * 256:(ss + 1) * 256], fscr[:, ss * 256:(ss + 1) * 256],
                                AF.Exp, scale=-1.0),
                                waits=[t_ea] + p.bwait(("wtile", ss)), inc="sc")
                            p.bread(("warg", ss), t_w)
                            t_s = p.emit("tensor", lambda h, blk=blk, hp0=hp0, nc0=nc0, ss=ss, par=par: h.matmul(
                                psum[2 + par][:, 0:256],
                                aB[hp0:hp0 + 64, 8192 + blk * 512 + nc0 + ss * 128:
                                   8192 + blk * 512 + nc0 + ss * 128 + 128],
                                aC[hp0:hp0 + 64, 8192 + blk * 512 + nc0:8192 + blk * 512 + nc0 + 256],
                                start=True, stop=True), waits=qk_done + p.bwait(2 + par), inc="pe")
                            t_p = p.emit("vector", lambda h, ss=ss, par=par: h.tensor_mul(
                                pslab[:, 512 + ss * 256:512 + (ss + 1) * 256],
                                psum[2 + par][:, 0:256],
                                pslab[:, ss * 256:(ss + 1) * 256]),
                                waits=[t_w, t_s] + p.bwait(("P", ss)), inc="dv")
                            p.bread(2 + par, t_p)
                            p.bread(("wtile", ss), t_p)
                            ptt = []
                            for kvh in range(2):
                                t_t = p.emit("tensor", lambda h, ss=ss, kvh=kvh: h.transpose(
                                    psbf[4][:, ss * 256 + kvh * 128:ss * 256 + kvh * 128 + 128],
                                    pslab[:, 512 + ss * 256 + kvh * 128:512 + ss * 256 + kvh * 128 + 128],
                                    ident[:]), waits=[t_p] + (p.bwait(4) if kvh == 0 else []), inc="pe")
                                ptt.append(t_t)
                            t_pv = p.emit("vector", lambda h, ss=ss: h.tensor_copy(
                                pslab[:, 1024 + ss * 256:1024 + ss * 256 + 256],
                                psbf[4][:, ss * 256:ss * 256 + 256]),
                                waits=[ptt[-1]] + p.bwait(("PT", ss)), inc="dv")
                            p.bread(4, t_pv)
                            p.bread(("P", ss), t_pv)
                            for kvh in range(2):
                                loc_tt = (nc0 + kvh * 128) // 128
                                w_ = [t_pv] + v_toks
                                if kvh == 0 and ss == 0:
                                    w_ += p.bwait((2 + par, "o"))
                                t_o = p.emit("tensor", lambda h, ss=ss, kvh=kvh, loc_tt=loc_tt, hglob=hglob, par=par: h.matmul(
                                    psum[2 + par][:, 320 + ss * 64:320 + ss * 64 + 64],
                                    pslab[:, 1024 + ss * 256 + kvh * 128:1024 + ss * 256 + kvh * 128 + 128],
                                    aC[:, 4096 + loc_tt * 1024 + hglob * 64:4096 + loc_tt * 1024 + hglob * 64 + 64],
                                    start=(kvh == 0), stop=False, skip_group_check=True),
                                    waits=w_, inc=None)
                            t_oc = p.emit("tensor", lambda h, ss=ss, hp0=hp0, hglob=hglob, par=par: h.matmul(
                                psum[2 + par][:, 320 + ss * 64:320 + ss * 64 + 64],
                                pslab[hp0:hp0 + 64, 1536 + ss * 128:1536 + ss * 128 + 128],
                                Sbf[hp0:hp0 + 64, 12288 + (hglob // 2) * 64:12288 + (hglob // 2) * 64 + 64],
                                start=False, stop=True, skip_group_check=True),
                                waits=[t_qd, sbf_tok[hglob]], inc="pe")
                            p.bread(("PT", ss), t_oc)
                            p.bread("qd", t_oc)
                            oc_last = t_oc
                            if tc == 0 and nn == 0 and hglob == 0 and ss == 0 and "ptile" in dumps:
                                dump_toks.append(p.dma("sync", dumps["ptile"], pslab[:, 0:1024],
                                                       waits=[t_p, t_qd, ("sc", p.counts["sc"])]))
                        kv_mm = None
                        for kvh in range(2):
                            t_kt = p.emit("tensor", lambda h, hp0=hp0, blk=blk, nc0=nc0, kvh=kvh: h.transpose(
                                psbf[5][:, 0:64],
                                aC[hp0:hp0 + 64, 8192 + blk * 512 + nc0 + kvh * 128:
                                   8192 + blk * 512 + nc0 + kvh * 128 + 128],
                                ident[hp0:hp0 + 64, hp0:hp0 + 64]),
                                waits=qk_done + p.bwait(5), inc="pe")
                            t_kd = p.emit("vector", lambda h, nn=nn, kvh=kvh, hglob=hglob: h.tensor_scalar(
                                pslab[:, 1792 + kvh * 64:1792 + kvh * 64 + 64],
                                psbf[5][:, 0:64],
                                misc[:, 3648 + (nn * 2 + kvh) * 16 + hglob:
                                     3648 + (nn * 2 + kvh) * 16 + hglob + 1],
                                None, ALU.mult),
                                waits=[t_kt] + p.bwait(("kd", kvh)), inc="dv")
                            p.bread(5, t_kd)
                        for kvh in range(2):
                            loc_tt = (nc0 + kvh * 128) // 128
                            w_ = [("dv", p.counts["dv"])] if kvh == 0 else []
                            if kvh == 0:
                                w_ += p.bwait(2 + par)
                            kv_mm = p.emit("tensor", lambda h, kvh=kvh, loc_tt=loc_tt, hglob=hglob, par=par, hp0=hp0: h.matmul(
                                psum[2 + par][hp0:hp0 + 64, 256:320],
                                pslab[:, 1792 + kvh * 64:1792 + kvh * 64 + 64],
                                aC[:, 4096 + loc_tt * 1024 + hglob * 64:4096 + loc_tt * 1024 + hglob * 64 + 64],
                                start=(kvh == 0), stop=(kvh == 1), skip_group_check=True),
                                waits=w_, inc="pe" if kvh == 1 else None)
                        scol = (hglob // 2) * 64
                        t_su = p.emit("vector", lambda h, hglob=hglob, par=par, hp0=hp0, scol=scol: h.scalar_tensor_tensor(
                            misc[hp0:hp0 + 64, scol:scol + 64],
                            misc[hp0:hp0 + 64, scol:scol + 64],
                            misc[hp0:hp0 + 64, 3712 + hglob:3712 + hglob + 1],
                            psum[2 + par][hp0:hp0 + 64, 256:320],
                            ALU.mult, ALU.add), waits=[kv_mm, t_e3] + zinit, inc="dv")
                        p.bread(2 + par, t_su)
                        sbf_tok[hglob] = p.emit("vector", lambda h, hglob=hglob, hp0=hp0, scol=scol: h.tensor_copy(
                            aB[hp0:hp0 + 64, 12288 + scol:12288 + scol + 64],
                            misc[hp0:hp0 + 64, scol:scol + 64]), waits=[t_su], inc="dv")
                        loc0 = nc0 // 128
                        t_oe = p.emit("scalar", lambda h, loc0=loc0, hglob=hglob, par=par: h.copy(
                            aC[:, 12288:16384].rearrange("p (x y) -> p x y", x=4)
                            [:, loc0:loc0 + 2, hglob * 64:hglob * 64 + 64],
                            psum[2 + par][:, 320:448].rearrange("p (x y) -> p x y", x=2)),
                            waits=[oc_last, kv_mm, t_su], inc="sc")
                        p.bread((2 + par, "o"), t_oe)
                        p.bread(2 + par, t_oe)
            stageB_done = [("dv", p.counts["dv"]), ("sc", p.counts["sc"])]
            dump_hold_o = []
            if "o_tok" in dumps:
                dump_hold_o.append(p.dma(
                    "sync", dumps["o_tok"].rearrange("p a b -> p (a b)")[:, tc * 4096:(tc + 1) * 4096],
                    aC[:, 12288:16384], waits=stageB_done))

            ot_toks = []
            for tt in range(4):
                sq_toks = []
                for hh in range(H0):
                    sq_toks.append(p.emit("scalar", lambda h, tt=tt, hh=hh: h.activation(
                        pslab[:, 1024 + hh * 64:1024 + hh * 64 + 64],
                        aC[:, 12288 + tt * 1024 + hh * 64:12288 + tt * 1024 + hh * 64 + 64],
                        AF.Square, accum_out=fscr[:, hh:hh + 1]),
                        waits=stageB_done + (ot_toks[-1:] if ot_toks else []), inc="sc"))
                r1 = p.emit("scalar", lambda h: h.activation(
                    fscr[:, 16:32], fscr[:, 0:16], AF.Sqrt, scale=1.0 / DH0,
                    bias=cc(C_EPS_RMS)), waits=sq_toks, inc="sc")
                r2 = p.emit("vector", lambda h: h.reciprocal(
                    fscr[:, 32:48], fscr[:, 16:32]), waits=[r1], inc="dv")
                og_toks = []
                for hh in range(H0):
                    og_toks.append(p.emit("vector", lambda h, tt=tt, hh=hh: h.tensor_scalar(
                        pslab[:, hh * 64:hh * 64 + 64],
                        aC[:, 12288 + tt * 1024 + hh * 64:12288 + tt * 1024 + hh * 64 + 64],
                        fscr[:, 32 + hh:32 + hh + 1], None, ALU.mult), waits=[r2], inc="dv"))
                sg = p.emit("scalar", lambda h, tt=tt: h.activation(
                    pslab[:, 1024:2048], aB[:, 4096 + tt * 1024:4096 + (tt + 1) * 1024],
                    AF.Silu), waits=sq_toks, inc="sc")
                og2 = p.emit("vector", lambda h, tt=tt: h.tensor_mul(
                    aC[:, 12288 + tt * 1024:12288 + (tt + 1) * 1024],
                    pslab[:, 0:1024], pslab[:, 1024:2048]),
                    waits=og_toks + [sg] + dump_hold_o, inc="dv")
                ot_toks.append(og2)
            ogt_toks = []
            for tt in range(4):
                tts = []
                for blk in range(NKC):
                    w_ = [ot_toks[tt]]
                    if blk == 0:
                        w_ = w_ + p.bwait(4)
                    tts.append(p.emit("tensor", lambda h, tt=tt, blk=blk: h.transpose(
                        psbf[4][:, blk * 128:blk * 128 + 128],
                        aC[:, 12288 + tt * 1024 + blk * 128:12288 + tt * 1024 + blk * 128 + 128],
                        ident[:]), waits=w_, inc="pe"))
                ev = p.emit("vector", lambda h, tt=tt: h.tensor_copy(
                    aB[:, 8192 + tt * 128:8192 + tt * 128 + 4096]
                    .rearrange("p (x y) -> p x y", x=8)[:, :, 0:128],
                    psbf[4][:, 0:1024].rearrange("p (x y) -> p x y", x=8)),
                    waits=[tts[-1]] + dump_hold_q, inc="dv")
                p.bread(4, ev)
                ogt_toks.append(ev)
            resid_toks = []
            slot_consumer = {}
            for blk in range(NKC):
                slot = blk % 2
                w_ = list(ogt_toks[-1:])
                if slot in slot_consumer:
                    w_.append(slot_consumer[slot])
                ldt = wload(
                    wb[:, 8192 + slot * 1024:8192 + slot * 1024 + 1024].rearrange(
                        "p (a b) -> p a b", a=8),
                    dramf["Wo"][:, :, blk * 128:(blk + 1) * 128], waits=w_)
                bank = blk % 2
                for kc in range(NKC):
                    w2_ = [ldt] + ogt_toks
                    if kc == 0:
                        w2_ += p.bwait(bank)
                    t_mm = p.emit("tensor", lambda h, blk=blk, kc=kc, bank=bank, slot=slot: h.matmul(
                        psum[bank][:, 0:512],
                        wb[:, 8192 + slot * 1024 + kc * 128:8192 + slot * 1024 + kc * 128 + 128],
                        aB[:, 8192 + kc * 512:8192 + kc * 512 + 512],
                        start=(kc == 0), stop=(kc == NKC - 1), skip_group_check=True),
                        waits=w2_, inc="pe" if kc == NKC - 1 else None)
                slot_consumer[slot] = t_mm
                rt = p.emit("vector", lambda h, blk=blk, bank=bank, t0=t0: h.tensor_add(
                    xres[:, blk, t0:t0 + 512], xres[:, blk, t0:t0 + 512],
                    psum[bank][:, 0:512]), waits=[t_mm], inc="dv")
                p.bread(bank, rt)
                resid_toks.append(rt)

            prev_tc_done = resid_toks + dump_toks + dump_hold_q + dump_hold_o + [
                ("dv", p.counts["dv"]), ("sc", p.counts["sc"])]

        attn_done = prev_tc_done

        moe_prev = attn_done
        w13 = dramf["w13"]
        w2d = dramf["w2"]
        for tc in range(4):
            t0 = tc * 512
            ldgate = wload(wb[:, 0:64].rearrange("p (a b) -> p a b", a=8),
                           dramf["gate_w"], waits=moe_prev)
            for kc in range(NKC):
                w_ = [ldgate] + (moe_prev if kc == 0 else []) + (p.bwait(6) if kc == 0 else [])
                gmm = p.emit("tensor", lambda h, kc=kc, t0=t0: h.matmul(
                    psum[6][0:8, 0:512], wb[:, kc * 8:kc * 8 + 8],
                    actA[:, kc, t0:t0 + 512],
                    start=(kc == 0), stop=(kc == NKC - 1), skip_group_check=True),
                    waits=w_, inc="pe" if kc == NKC - 1 else None)
            gev = p.emit("scalar", lambda h: h.copy(fscr[0:8, 0:512], psum[6][0:8, 0:512]),
                         waits=[gmm], inc="sc")
            p.bread(6, gev)
            sc_toks = []
            for tt in range(4):
                t_t = p.emit("tensor", lambda h, tt=tt: h.transpose(
                    psum[7][:, 0:8], fscr[0:8, tt * 128:tt * 128 + 128],
                    identf[0:8, 0:8]), waits=[gev] + p.bwait(7), inc="pe")
                lev = p.emit("scalar", lambda h, tt=tt: h.copy(
                    fscr[:, 512 + tt * 8:512 + tt * 8 + 8], psum[7][:, 0:8]),
                    waits=[t_t], inc="sc")
                p.bread(7, lev)
                c0 = 512 + tt * 8
                m1 = p.emit("vector", lambda h, c0=c0: h.reduce_max(
                    fscr[:, 544:545], fscr[:, c0:c0 + 8], axis=AX.X, negate=True),
                    waits=[lev], inc="dv")
                ex = p.emit("scalar", lambda h, c0=c0: h.activation(
                    fscr[:, 552 + 0:552 + 8], fscr[:, c0:c0 + 8], AF.Exp,
                    bias=fscr[:, 544:545], accum_out=fscr[:, 545:546]),
                    waits=[m1], inc="sc")
                rc = p.emit("vector", lambda h: h.reciprocal(
                    fscr[:, 546:547], fscr[:, 545:546]), waits=[ex], inc="dv")
                pr = p.emit("vector", lambda h: h.tensor_scalar(
                    fscr[:, 552:560], fscr[:, 552:560], fscr[:, 546:547], None, ALU.mult),
                    waits=[rc], inc="dv")
                mm1 = p.emit("vector", lambda h: h.reduce_max(
                    fscr[:, 544:545], fscr[:, 552:560], axis=AX.X), waits=[pr], inc="dv")
                mk1 = p.emit("vector", lambda h: h.tensor_scalar(
                    fscr[:, 560:568], fscr[:, 552:560], fscr[:, 544:545], None, ALU.is_ge),
                    waits=[mm1], inc="dv")
                p2 = p.emit("vector", lambda h: h.scalar_tensor_tensor(
                    fscr[:, 568:576], fscr[:, 560:568], cc(C_NEG1E30), fscr[:, 552:560],
                    ALU.mult, ALU.add), waits=[mk1], inc="dv")
                mm2 = p.emit("vector", lambda h: h.reduce_max(
                    fscr[:, 545:546], fscr[:, 568:576], axis=AX.X), waits=[p2], inc="dv")
                sel = p.emit("vector", lambda h: h.tensor_scalar(
                    fscr[:, 560:568], fscr[:, 552:560], fscr[:, 545:546], None, ALU.is_ge),
                    waits=[mm2], inc="dv")
                scl = p.emit("vector", lambda h: h.tensor_mul(
                    fscr[:, 552:560], fscr[:, 552:560], fscr[:, 560:568]),
                    waits=[sel], inc="dv")
                t_b = p.emit("tensor", lambda h, tt=tt: h.transpose(
                    psum[7][0:8, 8:136], fscr[:, 552:560], identf[:, 0:128]),
                    waits=[scl] + p.bwait(7), inc="pe")
                sev = p.emit("scalar", lambda h, tt=tt, t0=t0: h.copy(
                    misc[0:8, 2048 + t0 + tt * 128:2048 + t0 + tt * 128 + 128],
                    psum[7][0:8, 8:136]), waits=[t_b], inc="sc")
                p.bread(7, sev)
                sc_toks.append(sev)
            if tc == 0 and "moe_scale" in dumps:
                sc_toks.append(p.dma("sync", dumps["moe_scale"], misc[0:8, 2048:2560],
                                     waits=sc_toks))

            h_toks_all = []
            for e in range(8):
                xs0 = (e % 2) * 4096
                ld13k = []
                for kc in range(NKC):
                    ld13k.append(wload(
                        wb[:, kc * 1024:kc * 1024 + 1024], w13[e][:, kc, :],
                        waits=p.bwait(("w13", kc)) + (sc_toks if e == 0 else [])))
                rb = 6 + e % 2
                sel1 = identf[0:8, e:e + 1].to_broadcast((8, 128))
                t_r = p.emit("tensor", lambda h, t0=t0, sel1=sel1, rb=rb: h.matmul(
                    psum[rb][:, 0:512], sel1,
                    misc[0:8, 2048 + t0:2048 + t0 + 512],
                    start=True, stop=True), waits=sc_toks + p.bwait(rb), inc="pe")
                re_ = p.emit("scalar", lambda h, rb=rb, e=e: h.copy(
                    pslab[:, (e % 2) * 512:(e % 2) * 512 + 512], psum[rb][:, 0:512]),
                    waits=[t_r] + p.bwait(("sr", e % 2)), inc="sc")
                p.bread(rb, re_)
                xs_toks = []
                for kc in range(NKC):
                    xt = p.emit("vector", lambda h, kc=kc, t0=t0, xs0=xs0, e=e: h.tensor_mul(
                        aC[:, xs0 + kc * 512:xs0 + kc * 512 + 512],
                        actA[:, kc, t0:t0 + 512],
                        pslab[:, (e % 2) * 512:(e % 2) * 512 + 512]),
                        waits=[re_] + p.bwait(("xs", e % 2, kc)), inc="dv")
                    xs_toks.append(xt)
                p.bread(("sr", e % 2), xs_toks[-1])
                if tc == 0 and e == 0 and "moe_xs" in dumps:
                    xs_toks.append(p.dma(
                        "sync", dramf["dump_moe_xs"].rearrange("p a b -> p (a b)")[:, 0:4096],
                        aC[:, 0:4096], waits=xs_toks))
                ev_toks = []
                kc_last = {}
                for hblk in range(4):
                    b1 = hblk % 2
                    b3 = 2 + hblk % 2
                    for kc in range(NKC):
                        w_ = [ld13k[kc], xs_toks[kc]]
                        m1_ = p.emit("tensor", lambda h, hblk=hblk, kc=kc, b1=b1, xs0=xs0: h.matmul(
                            psum[b1][:, 0:512],
                            wb[:, kc * 1024 + hblk * 128:kc * 1024 + hblk * 128 + 128],
                            aC[:, xs0 + kc * 512:xs0 + kc * 512 + 512],
                            start=(kc == 0), stop=(kc == NKC - 1), skip_group_check=True),
                            waits=w_ + (p.bwait(b1) if kc == 0 else []),
                            inc="pe" if kc == NKC - 1 else None)
                        m3_ = p.emit("tensor", lambda h, hblk=hblk, kc=kc, b3=b3, xs0=xs0: h.matmul(
                            psum[b3][:, 0:512],
                            wb[:, kc * 1024 + 512 + hblk * 128:kc * 1024 + 512 + hblk * 128 + 128],
                            aC[:, xs0 + kc * 512:xs0 + kc * 512 + 512],
                            start=(kc == 0), stop=(kc == NKC - 1), skip_group_check=True),
                            waits=(p.bwait(b3) if kc == 0 else []) + w_,
                            inc="pe" if hblk == 3 or kc == NKC - 1 else None)
                        if hblk == 3:
                            kc_last[kc] = m3_
                    sl = p.emit("scalar", lambda h, hblk=hblk, b1=b1: h.activation(
                        pslab[:, 1024 + (hblk % 2) * 512:1024 + (hblk % 2) * 512 + 512],
                        psum[b1][:, 0:512], AF.Silu),
                        waits=[m1_] + p.bwait(("sil", hblk % 2)), inc="sc")
                    p.bread(b1, sl)
                    ht = p.emit("vector", lambda h, hblk=hblk, b3=b3, e=e: h.tensor_mul(
                        aB[:, e * 2048 + hblk * 512:e * 2048 + hblk * 512 + 512],
                        psum[b3][:, 0:512],
                        pslab[:, 1024 + (hblk % 2) * 512:1024 + (hblk % 2) * 512 + 512]),
                        waits=[m3_, sl], inc="dv")
                    p.bread(b3, ht)
                    p.bread(("sil", hblk % 2), ht)
                    ev_toks.append(ht)
                for kc in range(NKC):
                    p.bread(("w13", kc), kc_last[kc])
                for kc in range(NKC):
                    p.bread(("xs", e % 2, kc), ev_toks[-1])
                h_toks_all += ev_toks
            y_toks = []
            last_mm = {}
            for e in range(8):
                for hk in range(4):
                    par = (e * 4 + hk) % 2
                    ldt = wload(wb[:, 8192 + par * 1024:8192 + par * 1024 + 1024],
                                w2d[e][:, hk, :],
                                waits=p.bwait(("w2s", par)) + (h_toks_all[-1:] if e == 0 and hk == 0 else []))
                    for blk in range(NKC):
                        w_ = [ldt]
                        if e == 0 and hk == 0:
                            w_ += p.bwait(blk) + h_toks_all[-1:]
                        last_mm[blk] = p.emit("tensor", lambda h, blk=blk, e=e, hk=hk, par=par: h.matmul(
                            psum[blk][:, 0:512],
                            wb[:, 8192 + par * 1024 + blk * 128:8192 + par * 1024 + blk * 128 + 128],
                            aB[:, e * 2048 + hk * 512:e * 2048 + hk * 512 + 512],
                            start=(e == 0 and hk == 0), stop=(e == 7 and hk == 3),
                            skip_group_check=True),
                            waits=w_, inc="pe" if blk == NKC - 1 or (e == 7 and hk == 3) else None)
                    p.bread(("w2s", par), last_mm[NKC - 1])
            for blk in range(NKC):
                rt = p.emit("vector", lambda h, blk=blk, t0=t0: h.tensor_add(
                    xres[:, blk, t0:t0 + 512], xres[:, blk, t0:t0 + 512],
                    psum[blk][:, 0:512]), waits=[last_mm[blk]], inc="dv")
                p.bread(blk, rt)
                y_toks.append(rt)
            moe_prev = y_toks + [("dv", p.counts["dv"]), ("sc", p.counts["sc"])]

        moe_done = moe_prev
        if "x1" in dumps:
            moe_done = moe_done + [p.dma("sync", dumps["x1"], xres[:], waits=moe_done)]

        ln2_toks = layernorm(xres, actA, 0, moe_done, actC)
        ln2_done = [("dv", p.counts["dv"])]
        tst = p.dma("sync", misc[64:66, 0:2048], misc[0:1, 0:4096], waits=ln2_done)
        if "xn2" in dumps:
            ln2_done = ln2_done + [p.dma("sync", dumps["xn2"], actA[:], waits=ln2_done)]

        t_pw = []
        t_pw.append(p.dma("sync", misc[:, 2560:2624].rearrange("p (a b) -> p a b", a=16),
                          dramf["conv_w"], waits=ln2_done))
        t_pw.append(p.dma("sync", misc[:, 2624:2640], dramf["conv_b"], waits=ln2_done))
        t_pw.append(p.dma("sync", misc[:, 2640:2656], dramf["W_dt"], waits=ln2_done))
        t_pw.append(p.dma("sync", misc[:, 2656:2672], dramf["b_dt"], waits=ln2_done))
        t_pw.append(p.dma("sync", misc[:, 2672:2688], dramf["D_param"], waits=ln2_done))
        t_pw.append(p.dma("sync", misc[:, 2688:2944].rearrange("p (a b) -> p a b", a=16),
                          dramf["A_log"], waits=ln2_done))
        t_A = p.emit("scalar", lambda h: h.activation(
            misc[:, 2688:2944], misc[:, 2688:2944], AF.Exp), waits=t_pw, inc="sc")
        t_A = p.emit("vector", lambda h: h.tensor_scalar(
            misc[:, 2688:2944], misc[:, 2688:2944], -1.0, None, ALU.mult),
            waits=[t_A], inc="dv")
        ld_wx = wload(wb[:, 9216:9744].rearrange("p (a b) -> p a b", a=16),
                      dramf["W_xT"], waits=ln2_done)

        u_hbm = nc.dram_tensor("u_hbm", [128, NDC, T], BF16, kind="Internal").ap()
        sz_hbm = nc.dram_tensor("sz_hbm", [128, NDC, T], BF16, kind="Internal").ap()

        xn2 = actA
        uu_toks = []
        rows_mm = {}
        for blk in range(NDC):
            slot = blk % 4
            ldt = wload(wb[:, slot * 1024:slot * 1024 + 1024].rearrange(
                "p (a b) -> p a b", a=8),
                dramf["W_inT"][:, :, blk * 128:(blk + 1) * 128],
                waits=p.bwait(("wi", slot)) + (ln2_done if blk < 4 else []))
            xd0 = (blk % 2) * 2048
            ev_toks = []
            for tq in range(4):
                bank = 4 + ((blk + tq) % 4)
                for kc in range(NKC):
                    w_ = [ldt]
                    if kc == 0:
                        w_ += p.bwait(bank) + ln2_done
                    mm = p.emit("tensor", lambda h, kc=kc, tq=tq, bank=bank, slot=slot: h.matmul(
                        psum[bank][:, 0:512],
                        wb[:, slot * 1024 + kc * 128:slot * 1024 + kc * 128 + 128],
                        xn2[:, kc, tq * 512:tq * 512 + 512],
                        start=(kc == 0), stop=(kc == NKC - 1), skip_group_check=True),
                        waits=w_, inc="pe" if kc == NKC - 1 else None)
                ev = p.emit("scalar", lambda h, tq=tq, bank=bank, xd0=xd0: h.copy(
                    aB[:, xd0 + tq * 512:xd0 + tq * 512 + 512],
                    psum[bank][:, 0:512]), waits=[mm] + p.bwait(("xd", blk % 2)), inc="sc")
                p.bread(bank, ev)
                ev_toks.append(ev)
            p.bread(("wi", slot), ev_toks[-1])
            if blk == 15 and "xd15" in dumps:
                xdt = p.dma("sync", dumps["xd15"], aB[:, xd0:xd0 + 2048], waits=ev_toks)
                ev_toks.append(xdt)
            u0 = 4096 + (blk % 2) * 2048
            tu = None
            for hf in range(2):
                hb = hf * 1024
                a0 = 2944
                prev = None
                for j in range(4):
                    lo = hb - 3 + j
                    srcs = max(lo, 0)
                    dsts = a0 + (srcs - lo)
                    n = 1024 - (srcs - lo)
                    if j == 0:
                        if srcs - lo > 0:
                            prev = p.emit("vector", lambda h: h.memset(
                                misc[:, 2944:2947], 0.0),
                                waits=ev_toks + [t_A] + p.bwait(("cacc",)), inc="dv")
                        prev = p.emit("vector", lambda h, xd0=xd0, blk=blk, srcs=srcs, dsts=dsts, n=n: h.tensor_scalar(
                            misc[:, dsts:dsts + n], aB[:, xd0 + srcs:xd0 + srcs + n],
                            misc[:, 2560 + blk * 4:2560 + blk * 4 + 1], None, ALU.mult),
                            waits=([prev] if prev else ev_toks + [t_A] + p.bwait(("cacc",))),
                            inc="dv")
                    else:
                        prev = p.emit("vector", lambda h, j=j, xd0=xd0, blk=blk, srcs=srcs, dsts=dsts, n=n: h.scalar_tensor_tensor(
                            misc[:, dsts:dsts + n], aB[:, xd0 + srcs:xd0 + srcs + n],
                            misc[:, 2560 + blk * 4 + j:2560 + blk * 4 + j + 1],
                            misc[:, dsts:dsts + n], ALU.mult, ALU.add),
                            waits=[prev], inc="dv")
                tu = p.emit("scalar", lambda h, u0=u0, hb=hb, blk=blk: h.activation(
                    aB[:, u0 + hb:u0 + hb + 1024], misc[:, 2944:3968], AF.Silu,
                    bias=misc[:, 2624 + blk:2624 + blk + 1]),
                    waits=[prev] + (p.bwait(("us", blk % 2)) + p.bwait(("usd", blk % 2)) if hf == 0 else []),
                    inc="sc")
                p.bread(("cacc",), tu)
            tdu = p.dma("sync", u_hbm[:, blk, :], aB[:, u0:u0 + 2048], waits=[tu])
            uu_toks.append(tdu)
            if "u_ssm" in dumps:
                udt = p.dma("sync", dramf["dump_u_ssm"][:, blk, :],
                            aB[:, u0:u0 + 2048], waits=[tu])
                uu_toks.append(udt)
                p.bread(("usd", blk % 2), udt)
            for tq in range(4):
                w_ = [tu, ld_wx]
                if blk == 0:
                    w_ += p.bwait(tq)
                rows_mm[tq] = p.emit("tensor", lambda h, blk=blk, tq=tq, u0=u0: h.matmul(
                    psum[tq][0:33, 0:512],
                    wb[:, 9216 + blk * 33:9216 + blk * 33 + 33],
                    aB[:, u0 + tq * 512:u0 + tq * 512 + 512],
                    start=(blk == 0), stop=(blk == NDC - 1), skip_group_check=True),
                    waits=w_, inc="pe")
            p.bread(("us", blk % 2), rows_mm[3])
            p.bread(("usd", blk % 2), tdu)
            p.bread(("xd", blk % 2), prev)
        row_toks = []
        for tq in range(4):
            rt = p.emit("scalar", lambda h, tq=tq: h.copy(
                misc[0:33, tq * 512:tq * 512 + 512], psum[tq][0:33, 0:512]),
                waits=[rows_mm[tq]] + p.bwait(("convdead",)), inc="sc")
            p.bread(tq, rt)
            row_toks.append(rt)
        if "ssm_in" in dumps:
            row_toks.append(p.dma("sync", dramf["dump_ssm_in"],
                                  misc[0:64, 0:2048], waits=row_toks))

        sz_toks = []
        for blk in range(NDC):
            slot = blk % 4
            ldt = wload(wb[:, slot * 1024:slot * 1024 + 1024].rearrange(
                "p (a b) -> p a b", a=8),
                dramf["W_inT"][:, :, DI + blk * 128:DI + (blk + 1) * 128],
                waits=p.bwait(("wi", slot)))
            z0 = 8192 + (blk % 2) * 2048
            ev_toks = []
            for tq in range(4):
                bank = 4 + ((blk + tq) % 4)
                for kc in range(NKC):
                    w_ = [ldt]
                    if kc == 0:
                        w_ += p.bwait(bank)
                    mm = p.emit("tensor", lambda h, kc=kc, tq=tq, bank=bank, slot=slot: h.matmul(
                        psum[bank][:, 0:512],
                        wb[:, slot * 1024 + kc * 128:slot * 1024 + kc * 128 + 128],
                        xn2[:, kc, tq * 512:tq * 512 + 512],
                        start=(kc == 0), stop=(kc == NKC - 1), skip_group_check=True),
                        waits=w_, inc="pe" if kc == NKC - 1 else None)
                ev = p.emit("scalar", lambda h, tq=tq, bank=bank, z0=z0: h.activation(
                    aB[:, z0 - 8192 + tq * 512:z0 - 8192 + tq * 512 + 512],
                    psum[bank][:, 0:512], AF.Silu),
                    waits=[mm] + p.bwait(("zs", blk % 2)) +
                    (row_toks + p.bwait(("xd", blk % 2)) if blk < 2 else []), inc="sc")
                p.bread(bank, ev)
                ev_toks.append(ev)
            p.bread(("wi", slot), ev_toks[-1])
            tdz = p.dma("sync", sz_hbm[:, blk, :], aB[:, (blk % 2) * 2048:(blk % 2) * 2048 + 2048],
                        waits=ev_toks)
            p.bread(("zs", blk % 2), tdz)
            sz_toks.append(tdz)

        t_spill = p.dma("sync", x_old_hbm, xres[:], waits=sz_toks + row_toks + uu_toks)

        xf32 = xres.rearrange("p a b -> p (a b)")
        xbf = xres.rearrange("p a b -> p (a b)").bitcast(BF16)
        aA = actA.rearrange("p a b -> p (a b)")
        t_state0 = p.emit("vector", lambda h: h.memset(misc[:, 3456:3712], 0.0),
                          waits=row_toks + [("sc", p.counts["sc"])], inc="dv")
        state_tok = {c: t_state0 for c in range(256)}
        cube_prev = [t_spill, t_state0] + ln2_done
        ssm_y_toks = []
        for tci in range(4):
            t0 = tci * 512
            t_lu = p.dma("sync", aB[:, 0:8192].rearrange("p (a b) -> p a b", a=16),
                         u_hbm[:, :, t0:t0 + 512], waits=cube_prev + p.bwait(("utc",)))
            t_lz = p.dma("sync", aB[:, 8192:16384].rearrange("p (a b) -> p a b", a=16),
                         sz_hbm[:, :, t0:t0 + 512], waits=cube_prev + p.bwait(("sztc",)))
            bc_toks = []
            for which, base in ((1, 0), (17, 8192)):
                for s in range(NS):
                    selr = identf[0:33, which + s:which + s + 1].to_broadcast((33, 128))
                    t_r = p.emit("tensor", lambda h, s=s, selr=selr, t0=t0: h.matmul(
                        psum[6 + s % 2][:, 0:512], selr, misc[0:33, t0:t0 + 512],
                        start=True, stop=True),
                        waits=row_toks + cube_prev + p.bwait(6 + s % 2), inc="pe")
                    t_e = p.emit("scalar", lambda h, s=s, base=base: h.copy(
                        aA[:, base + s * 512:base + s * 512 + 512],
                        psum[6 + s % 2][:, 0:512]),
                        waits=[t_r] + p.bwait(("bc", base, s)), inc="sc")
                    p.bread(6 + s % 2, t_e)
                    bc_toks.append(t_e)
            t_dr = p.emit("tensor", lambda h, t0=t0: h.matmul(
                psum[6][:, 0:512], onesr[:], misc[0:1, t0:t0 + 512],
                start=True, stop=True), waits=row_toks + p.bwait(6), inc="pe")
            t_de = p.emit("scalar", lambda h: h.copy(
                misc[:, 2048:2560], psum[6][:, 0:512]),
                waits=[t_dr] + p.bwait(("dtr",)), inc="sc")
            p.bread(6, t_de)

            wo_mm = {}

            def dprod_a(dc):
                zp = p.emit("vector", lambda h, dc=dc: h.tensor_scalar(
                    fscr[:, 0:512], misc[:, 2048:2560],
                    misc[:, 2640 + dc:2640 + dc + 1], misc[:, 2656 + dc:2656 + dc + 1],
                    ALU.mult, ALU.add),
                    waits=[t_de, t_A] + p.bwait(("zp",)) + p.bwait(("zab",)), inc="dv")
                abw = [zp] + p.bwait(("lns",))
                if tci == 0 and dc == 0:
                    abw = abw + cube_prev + [("pe", p.counts["pe"])]
                ab = p.emit("scalar", lambda h: h.activation(
                    pslab[:, 1536:2048], fscr[:, 0:512], AF.Abs), waits=abw, inc="sc")
                p.bread(("zab",), ab)
                ex = p.emit("scalar", lambda h: h.activation(
                    pslab[:, 1536:2048], pslab[:, 1536:2048], AF.Exp, scale=-1.0),
                    waits=[ab], inc="sc")
                ln_ = p.emit("scalar", lambda h: h.activation(
                    pslab[:, 1536:2048], pslab[:, 1536:2048], AF.Ln,
                    bias=cc(C_ONE)), waits=[ex], inc="sc")
                return zp, ln_

            def dprod_b(dc, zp, ln_):
                dl = p.emit("vector", lambda h: h.scalar_tensor_tensor(
                    misc[:, 2944:3456], fscr[:, 0:512], 0.0, pslab[:, 1536:2048],
                    ALU.max, ALU.add), waits=[ln_] + p.bwait(("dl",)), inc="dv")
                p.bread(("zp",), dl)
                p.bread(("lns",), dl)
                if tci == 0 and dc == 0 and "delta0" in dumps:
                    dtok = p.dma("sync", dramf["dump_delta0"][:, 0:512], misc[:, 2944:3456],
                                 waits=[dl])
                    p.bread(("dl",), dtok)
                tw = p.emit("vector", lambda h, dc=dc: h.tensor_mul(
                    pslab[:, 0:512], misc[:, 2944:3456],
                    aB[:, dc * 512:dc * 512 + 512]), waits=[dl, t_lu] + p.bwait(("wv",)), inc="dv")
                da_toks = []
                for s in range(NS):
                    da_toks.append(p.emit("scalar", lambda h, s=s, dc=dc: h.activation(
                        xf32[:, s * 512:s * 512 + 512], misc[:, 2944:3456], AF.Exp,
                        scale=misc[:, 2688 + dc * 16 + s:2688 + dc * 16 + s + 1]),
                        waits=[dl] + p.bwait(("dA", s)) +
                        (cube_prev if dc == 0 else []), inc="sc"))
                p.bread(("dl",), da_toks[-1])
                x2t = p.emit("vector", lambda h: h.tensor_tensor(
                    xbf[:, 16384:24576].rearrange("p (a b) -> p a b", a=16),
                    pslab[:, 0:512].unsqueeze(1).broadcast_to((128, 16, 512)),
                    aA[:, 0:8192].rearrange("p (a b) -> p a b", a=16),
                    ALU.mult), waits=[tw] + bc_toks, inc="dv")
                p.bread(("wv",), x2t)
                return da_toks, x2t, dl

            zpl = dprod_a(0)
            for dc in range(NDC):
                da_toks, x2t, dl = dprod_b(dc, *zpl)
                if dc + 1 < NDC:
                    zpl = dprod_a(dc + 1)
                sc_t = []
                for s in range(NS):
                    stc = dc * 16 + s
                    sc_t.append(p.emit("vector", lambda h, s=s, stc=stc: h.tensor_tensor_scan(
                        xbf[:, 24576 + s * 512:24576 + s * 512 + 512],
                        xf32[:, s * 512:s * 512 + 512],
                        xbf[:, 16384 + s * 512:16384 + s * 512 + 512],
                        misc[:, 3456 + stc:3456 + stc + 1],
                        ALU.mult, ALU.add),
                        waits=[x2t, da_toks[s], state_tok[stc]] + p.bwait(("H", s)), inc="dv"))
                    p.bread(("dA", s), sc_t[-1])
                st_up = p.emit("vector", lambda h, dc=dc: h.tensor_copy(
                    misc[:, 3456 + dc * 16:3456 + dc * 16 + 16],
                    xbf[:, 24576 + 511:24576 + 511 + 15 * 512 + 1:512] if False else
                    xbf[:, 24576:32768].rearrange("p (a b) -> p a b", a=16)[:, :, 511]),
                    waits=sc_t, inc="dv")
                for s in range(NS):
                    state_tok[dc * 16 + s] = st_up
                ldw = wload(wb[:, 8192 + (dc % 2) * 1024:8192 + (dc % 2) * 1024 + 1024],
                            dramf["W_outT"][:, dc, :], waits=p.bwait(("wo", dc % 2)))
                hct = p.emit("vector", lambda h: h.tensor_tensor(
                    xbf[:, 16384:24576].rearrange("p (a b) -> p a b", a=16),
                    xbf[:, 24576:32768].rearrange("p (a b) -> p a b", a=16),
                    aA[:, 8192:16384].rearrange("p (a b) -> p a b", a=16),
                    ALU.mult), waits=sc_t + [("dv", p.counts["dv"])] + bc_toks, inc="dv")
                tr1 = p.emit("vector", lambda h: h.tensor_add(
                    xbf[:, 24576:28672], xbf[:, 16384:20480], xbf[:, 20480:24576]),
                    waits=[hct], inc="dv")
                tr2 = p.emit("vector", lambda h: h.tensor_add(
                    xbf[:, 16384:18432], xbf[:, 24576:26624], xbf[:, 26624:28672]),
                    waits=[tr1], inc="dv")
                tr3 = p.emit("vector", lambda h: h.tensor_add(
                    xbf[:, 24576:25600], xbf[:, 16384:17408], xbf[:, 17408:18432]),
                    waits=[tr2], inc="dv")
                for s in range(NS):
                    p.bread(("H", s), tr3)
                tr4 = p.emit("vector", lambda h: h.tensor_add(
                    fscr[:, 512:1024], xbf[:, 24576:25088], xbf[:, 25088:25600]),
                    waits=[tr3], inc="dv")
                y1 = p.emit("vector", lambda h, dc=dc: h.scalar_tensor_tensor(
                    pslab[:, 512:1024], aB[:, dc * 512:dc * 512 + 512],
                    misc[:, 2672 + dc:2672 + dc + 1], fscr[:, 512:1024],
                    ALU.mult, ALU.add),
                    waits=[tr4, t_lu], inc="dv")
                yg = p.emit("vector", lambda h, dc=dc: h.tensor_mul(
                    pslab[:, 1024:1536], pslab[:, 512:1024],
                    aB[:, 8192 + dc * 512:8192 + dc * 512 + 512]),
                    waits=[y1, t_lz] + p.bwait(("yg",)), inc="dv")
                for blk in range(NKC):
                    w_ = [ldw, yg]
                    if dc == 0:
                        w_ += p.bwait(blk)
                    wo_mm[blk] = p.emit("tensor", lambda h, dc=dc, blk=blk: h.matmul(
                        psum[blk][:, 0:512],
                        wb[:, 8192 + (dc % 2) * 1024 + blk * 128:8192 + (dc % 2) * 1024 + blk * 128 + 128],
                        pslab[:, 1024:1536],
                        start=(dc == 0), stop=(dc == NDC - 1), skip_group_check=True),
                        waits=w_, inc="pe" if blk == NKC - 1 or dc == NDC - 1 else None)
                p.bread(("wo", dc % 2), wo_mm[NKC - 1])
                p.bread(("yg",), wo_mm[NKC - 1])
            t_lx = p.dma("sync", xf32[:, 0:4096].rearrange("p (a b) -> p a b", a=8),
                         x_old_hbm[:, :, t0:t0 + 512],
                         waits=[t_spill] + [wo_mm[b] for b in range(1)] + p.bwait(("xold",)))
            add_toks = []
            for blk in range(NKC):
                at = p.emit("vector", lambda h, blk=blk: h.tensor_add(
                    xf32[:, 4096 + blk * 512:4096 + blk * 512 + 512],
                    xf32[:, blk * 512:blk * 512 + 512], psum[blk][:, 0:512]),
                    waits=[t_lx, wo_mm[blk]] + p.bwait(("xnew", blk)), inc="dv")
                p.bread(blk, at)
                add_toks.append(at)
            t_sx = p.dma("sync", x_new_hbm[:, :, t0:t0 + 512],
                         xf32[:, 4096:8192].rearrange("p (a b) -> p a b", a=8),
                         waits=add_toks)
            if "y_ssm" in dumps:
                pass
            p.bread(("xold",), add_toks[-1])
            for blk in range(NKC):
                p.bread(("xnew", blk), t_sx)
            p.bread(("utc",), yg)
            p.bread(("sztc",), yg)
            p.bread(("bc", 0, 0), hct)
            for s in range(NS):
                p.bread(("bc", 0, s), hct)
                p.bread(("bc", 8192, s), hct)
            p.bread(("dtr",), dl)
            ssm_y_toks.append(t_sx)
            cube_prev = [t_sx, ("dv", p.counts["dv"]), ("sc", p.counts["sc"])]

        t_lold = p.dma("sync", xres[:], x_old_hbm, waits=cube_prev)
        rec_toks = []
        for tq in range(4):
            t0 = tq * 512
            selm = identf[64:66, 64:65].to_broadcast((2, 128))
            selr = identf[64:66, 65:66].to_broadcast((2, 128))
            tm = p.emit("tensor", lambda h, t0=t0, selm=selm: h.matmul(
                psum[6][:, 0:512], selm, misc[64:66, t0:t0 + 512],
                start=True, stop=True), waits=[tst, t_lold] + p.bwait(6), inc="pe")
            tr_ = p.emit("tensor", lambda h, t0=t0, selr=selr: h.matmul(
                psum[7][:, 0:512], selr, misc[64:66, t0:t0 + 512],
                start=True, stop=True), waits=[tst] + p.bwait(7), inc="pe")
            for kc in range(NKC):
                ts1 = p.emit("vector", lambda h, kc=kc, t0=t0: h.tensor_sub(
                    fscr[:, 0:512], xres[:, kc, t0:t0 + 512], psum[6][:, 0:512]),
                    waits=[tm, t_lold], inc="dv")
                ts2 = p.emit("vector", lambda h, kc=kc, t0=t0: h.tensor_mul(
                    actA[:, kc, t0:t0 + 512], fscr[:, 0:512], psum[7][:, 0:512]),
                    waits=[tr_], inc="dv")
                rec_toks.append(ts2)
            p.bread(6, rec_toks[-1])
            p.bread(7, rec_toks[-1])
        t_rx = p.dma("sync", xres[:], x_new_hbm, waits=rec_toks)
        if "x2" in dumps and False:
            pass
        ssm_done = [t_rx, ("dv", p.counts["dv"]), ("sc", p.counts["sc"])]
        if "y_ssm" in dumps:
            ssm_done = ssm_done + [p.dma("sync", dumps["y_ssm"], xres[:], waits=ssm_done)]
        if "xn2b" in dumps:
            ssm_done = ssm_done + [p.dma("sync", dumps["xn2b"], actA[:], waits=ssm_done)]

        aAx = actA.rearrange("p a b -> p (a b)")

        xh_toks = []
        sq_prev = {}
        for tq in range(4):
            t0 = tq * 512
            for kc in range(NKC):
                par = kc % 2
                w_ = ssm_done + ([sq_prev[par]] if par in sq_prev else [])
                sqt = p.emit("scalar", lambda h, kc=kc, t0=t0, par=par: h.activation(
                    pslab[:, par * 512:par * 512 + 512], actA[:, kc, t0:t0 + 512],
                    AF.Square), waits=w_, inc="sc")
                mmt = p.emit("tensor", lambda h, kc=kc, par=par, tq=tq: h.matmul(
                    psum[4][0:1, tq * 128:tq * 128 + 128] if False else
                    psum[4][0:1, 0:512], onesb[:, 0:1], pslab[:, par * 512:par * 512 + 512],
                    start=(kc == 0), stop=(kc == NKC - 1), skip_group_check=True),
                    waits=[sqt] + (p.bwait(4) if kc == 0 else []), inc="pe")
                sq_prev[par] = mmt
            r1 = p.emit("scalar", lambda h, t0=t0: h.activation(
                fscr[0:1, 0:512], psum[4][0:1, 0:512], AF.Sqrt), waits=[mmt], inc="sc")
            p.bread(4, r1)
            r2 = p.emit("vector", lambda h: h.tensor_scalar(
                fscr[0:1, 0:512], fscr[0:1, 0:512], 1e-12, None, ALU.max),
                waits=[r1], inc="dv")
            r3 = p.emit("vector", lambda h: h.reciprocal(
                fscr[0:1, 512:1024], fscr[0:1, 0:512]), waits=[r2], inc="dv")
            r4 = p.emit("vector", lambda h, t0=t0: h.tensor_scalar(
                misc[0:1, 3584 + t0 // 4 * 0 + 0:3584 + 512] if False else
                fscr[0:1, 512:1024], fscr[0:1, 512:1024], cc(C_SQRTD)[0:1, :], None,
                ALU.mult), waits=[r3], inc="dv")
            rb = p.emit("tensor", lambda h: h.matmul(
                psum[5][:, 0:512], onesr[:], fscr[0:1, 512:1024],
                start=True, stop=True), waits=[r4] + p.bwait(5), inc="pe")
            last = None
            for kc in range(NKC):
                last = p.emit("vector", lambda h, kc=kc, t0=t0: h.tensor_mul(
                    actB[:, kc, t0:t0 + 512], actA[:, kc, t0:t0 + 512],
                    psum[5][:, 0:512]), waits=[rb], inc="dv")
                xh_toks.append(last)
            p.bread(5, last)

        t_rt = p.dma("sync", misc[:, 3968:3976], dramf["q_rt"], waits=ssm_done)
        if "dbg1" in dumps:
            t_rt = p.dma("sync", dumps["dbg1"], misc[:, 3968:3976], waits=[t_rt] + ssm_done)
        if "dbg2" in dumps:
            t_rt = p.dma("sync", dumps["dbg2"], consts[:, :], waits=[t_rt])
        t_rtb = p.emit("vector", lambda h: h.tensor_copy(
            pslab[:, 1024:1032], misc[:, 3968:3976]), waits=[t_rt] + ssm_done, inc="dv")
        if "srow" in dumps:
            dtk = p.dma("sync", dramf["dump_srow"][2:3, 0:1024], misc[:, 3968:3976],
                        waits=[t_rt] + ssm_done)
            t_rtb = p.emit("vector", lambda h: h.tensor_copy(
                pslab[:, 1024:1032], misc[:, 3968:3976]), waits=[dtk], inc="dv")
        smm = None
        s_evs = []
        for tq in range(4):
            for kc in range(NKC):
                smm = p.emit("tensor", lambda h, tq=tq, kc=kc: h.matmul(
                    psum[6][0:1, 0:512],
                    pslab[:, 1024 + kc:1024 + kc + 1],
                    actA[:, kc, tq * 512:tq * 512 + 512],
                    start=(kc == 0), stop=(kc == NKC - 1), skip_group_check=True),
                    waits=[t_rtb] + ssm_done + (p.bwait(6) if kc == 0 else []), inc="pe")
            sev = p.emit("scalar", lambda h, tq=tq: h.copy(
                misc[0:1, 2048 + tq * 512:2048 + tq * 512 + 512], psum[6][0:1, 0:512]),
                waits=[smm], inc="sc")
            p.bread(6, sev)
            s_evs.append(sev)
        s_done = s_evs
        if "dbg3" in dumps:
            s_done = s_done + [p.dma("sync", dramf["dump_dbg3"][0:1, :],
                                     misc[0:1, 2048:4096], waits=s_done)]
        t_s16 = p.dma("sync", misc[0:16, 1024:1152], misc[0:1, 2048:4096], waits=s_done)
        t_smax = p.emit("vector", lambda h: h.reduce_max(
            fscr[0:1, 144:145], misc[0:1, 2048:4096], axis=AX.X), waits=s_done, inc="dv")
        t_na = p.emit("vector", lambda h: h.memset(
            fscr[0:1, 145:146],
            float(np.log(2048.0) - np.log(np.float32(NQ_SEL * 9.0 / 8.0)))),
            waits=s_done, inc="dv")
        tm0 = p.emit("vector", lambda h: h.tensor_tensor(
            fscr[0:1, 146:147], fscr[0:1, 144:145], fscr[0:1, 145:146],
            ALU.min), waits=[t_smax, t_na], inc="dv")
        tm0b = p.emit("vector", lambda h: h.tensor_scalar(
            fscr[0:1, 147:148], fscr[0:1, 146:147], -1.0, None, ALU.mult),
            waits=[tm0], inc="dv")
        trp0 = p.emit("tensor", lambda h: h.matmul(
            psum[7][0:16, 0:3], onesr[0:1, 0:16], fscr[0:1, 145:148],
            start=True, stop=True), waits=[tm0b] + p.bwait(7), inc="pe")
        tre0 = p.emit("scalar", lambda h: h.copy(
            fscr[0:16, 152:155], psum[7][0:16, 0:3]), waits=[trp0], inc="sc")
        p.bread(7, tre0)

        t_gl = p.dma("sync", fscr[:, 8:16], dramf["gamma_light"], waits=ssm_done)
        t_gh = p.dma("sync", fscr[:, 16:24], dramf["gamma_heavy"], waits=ssm_done)
        qk_toks = []
        vl_toks = []
        lql = wload(wb[:, 0:4096].rearrange("p (a b) -> p a b", a=8),
                    dramf["wq_light"], waits=ssm_done + p.bwait(("lw", 0)))
        wsc_toks = []
        for kc in range(NKC):
            wsc_toks.append(p.emit("vector", lambda h, kc=kc: h.tensor_scalar(
                wb[:, kc * 512:kc * 512 + 512], wb[:, kc * 512:kc * 512 + 512],
                fscr[:, 8 + kc:8 + kc + 1], 1.0 / 8.0, ALU.mult, ALU.mult),
                waits=[lql, t_gl], inc="dv"))
        for blk in range(4):
            for tq in range(4):
                bank = (blk + tq) % 4
                for kc in range(NKC):
                    w_ = wsc_toks + xh_toks
                    if kc == 0:
                        w_ = w_ + p.bwait(bank)
                    mmq = p.emit("tensor", lambda h, blk=blk, kc=kc, tq=tq, bank=bank: h.matmul(
                        psum[bank][:, 0:512],
                        wb[:, kc * 512 + blk * 128:kc * 512 + blk * 128 + 128],
                        actB[:, kc, tq * 512:tq * 512 + 512],
                        start=(kc == 0), stop=(kc == NKC - 1), skip_group_check=True),
                        waits=w_, inc="pe" if kc == NKC - 1 else None)
                ev = p.emit("scalar", lambda h, blk=blk, tq=tq, bank=bank: h.copy(
                    aC[:, blk * 2048 + tq * 512:blk * 2048 + tq * 512 + 512],
                    psum[bank][:, 0:512]), waits=[mmq], inc="sc")
                p.bread(bank, ev)
                qk_toks.append(ev)
        p.bread(("lw", 0), qk_toks[-1])
        lkv = wload(wb[:, 0:8192].rearrange("p (a b) -> p a b", a=8),
                    dramf["wkv_light"], waits=p.bwait(("lw", 0)) + ssm_done)
        wsc2 = []
        for kc in range(NKC):
            wsc2.append(p.emit("vector", lambda h, kc=kc: h.tensor_scalar(
                wb[:, kc * 1024:kc * 1024 + 1024], wb[:, kc * 1024:kc * 1024 + 1024],
                fscr[:, 8 + kc:8 + kc + 1], None, ALU.mult),
                waits=[lkv, t_gl], inc="dv"))
        for blk in range(4):
            for tq in range(4):
                bank = (blk + tq) % 4
                for kc in range(NKC):
                    w_ = wsc2 + xh_toks
                    if kc == 0:
                        w_ = w_ + p.bwait(bank)
                    mmk = p.emit("tensor", lambda h, blk=blk, kc=kc, tq=tq, bank=bank: h.matmul(
                        psum[bank][:, 0:512],
                        wb[:, kc * 1024 + blk * 128:kc * 1024 + blk * 128 + 128],
                        actB[:, kc, tq * 512:tq * 512 + 512],
                        start=(kc == 0), stop=(kc == NKC - 1), skip_group_check=True),
                        waits=w_, inc="pe" if kc == NKC - 1 else None)
                ev = p.emit("scalar", lambda h, blk=blk, tq=tq, bank=bank: h.copy(
                    aC[:, 8192 + blk * 2048 + tq * 512:8192 + blk * 2048 + tq * 512 + 512],
                    psum[bank][:, 0:512]), waits=[mmk], inc="sc")
                p.bread(bank, ev)
                qk_toks.append(ev)
        for tt in range(16):
            bank = 4 + tt % 2
            for kc in range(NKC):
                w_ = wsc2 + xh_toks + s_done + [t_s16]
                if kc == 0:
                    w_ = w_ + p.bwait(bank)
                mmv = p.emit("tensor", lambda h, tt=tt, kc=kc, bank=bank: h.matmul(
                    psum[bank][:, 0:512],
                    actB[:, kc, tt * 128:tt * 128 + 128],
                    wb[:, kc * 1024 + 512:kc * 1024 + 1024],
                    start=(kc == 0), stop=(kc == NKC - 1), skip_group_check=True),
                    waits=w_, inc="pe" if kc == NKC - 1 else None)
            evv = p.emit("scalar", lambda h, tt=tt, bank=bank: h.copy(
                aAx[:, tt * 512:tt * 512 + 512], psum[bank][:, 0:512]),
                waits=[mmv], inc="sc")
            p.bread(bank, evv)
            vl_toks.append(evv)
        p.bread(("lw", 0), vl_toks[-1])
        p.bread(("lw", 1), vl_toks[-1])

        light_o_toks = []
        coor_last = [tre0]
        for it in range(50):
            if it > 0:
                ta_ = p.emit("vector", lambda h: h.tensor_scalar(
                    misc[0:16, 1152:1280], misc[0:16, 1024:1152],
                    fscr[0:16, 152:153], None, ALU.min),
                    waits=coor_last + [t_s16], inc="dv")
                te_ = p.emit("scalar", lambda h: h.activation(
                    misc[0:16, 1280:1408], misc[0:16, 1152:1280], AF.Exp,
                    bias=fscr[0:16, 154:155], accum_out=fscr[0:16, 155:156]),
                    waits=[ta_], inc="sc")
                tps = p.emit("tensor", lambda h: h.matmul(
                    psum[7][0:1, 4:5], consts[0:16, C_ONE:C_ONE + 1],
                    fscr[0:16, 155:156], start=True, stop=True),
                    waits=[te_] + p.bwait((7, "b")), inc="pe")
                tln = p.emit("scalar", lambda h: h.activation(
                    fscr[0:1, 149:150], psum[7][0:1, 4:5], AF.Ln), waits=[tps], inc="sc")
                p.bread((7, "b"), tln)
                t1 = p.emit("vector", lambda h: h.tensor_tensor(
                    fscr[0:1, 148:149], fscr[0:1, 146:147], fscr[0:1, 149:150],
                    ALU.add), waits=[tln, ("dv", p.counts["dv"])], inc="dv")
                t2 = p.emit("vector", lambda h: h.tensor_scalar(
                    fscr[0:1, 145:146], fscr[0:1, 148:149],
                    consts[0:1, C_LOGK:C_LOGK + 1], None, ALU.subtract),
                    waits=[t1], inc="dv")
                t3 = p.emit("vector", lambda h: h.tensor_tensor(
                    fscr[0:1, 146:147], fscr[0:1, 144:145], fscr[0:1, 145:146],
                    ALU.min), waits=[t2], inc="dv")
                t4 = p.emit("vector", lambda h: h.tensor_scalar(
                    fscr[0:1, 147:148], fscr[0:1, 146:147], -1.0, None, ALU.mult),
                    waits=[t3], inc="dv")
                trp = p.emit("tensor", lambda h: h.matmul(
                    psum[7][0:16, 0:3], onesr[0:1, 0:16], fscr[0:1, 145:148],
                    start=True, stop=True), waits=[t4] + p.bwait(7), inc="pe")
                coor_last = [p.emit("scalar", lambda h: h.copy(
                    fscr[0:16, 152:155], psum[7][0:16, 0:3]), waits=[trp], inc="sc")]
                p.bread(7, coor_last[0])
            if it < 16:
                wn = it
                wt0 = wn * 128
                for hh in range(H1):
                    hp0 = (hh % 2) * 64
                    hblk = hh // 2
                    g2 = (it * H1 + hh) % 4
                    sb_ = g2
                    fc = 24 + g2 * 4
                    t_s = p.emit("tensor", lambda h, hp0=hp0, hblk=hblk, wt0=wt0, sb_=sb_: h.matmul(
                        psum[sb_][:, 0:128],
                        aC[hp0:hp0 + 64, hblk * 2048 + wt0:hblk * 2048 + wt0 + 128],
                        aC[hp0:hp0 + 64, 8192 + hblk * 2048 + wt0:8192 + hblk * 2048 + wt0 + 128],
                        start=True, stop=True), waits=qk_toks + p.bwait(sb_), inc="pe")
                    t_mx = p.emit("vector", lambda h, sb_=sb_, fc=fc: h.reduce_max(
                        fscr[:, fc:fc + 1], psum[sb_][:, 0:128], axis=AX.X, negate=True),
                        waits=[t_s] + p.bwait(("lmx", g2)), inc="dv")
                    t_ex = p.emit("scalar", lambda h, sb_=sb_, fc=fc, g2=g2: h.activation(
                        pslab[:, g2 * 128:g2 * 128 + 128],
                        psum[sb_][:, 0:128], AF.Exp,
                        bias=fscr[:, fc:fc + 1], accum_out=fscr[:, fc + 1:fc + 2]),
                        waits=[t_mx] + p.bwait(("lps", g2)), inc="sc")
                    p.bread(sb_, t_ex)
                    p.bread(("lmx", g2), t_ex)
                    t_rc = p.emit("vector", lambda h, fc=fc: h.reciprocal(
                        fscr[:, fc + 2:fc + 3], fscr[:, fc + 1:fc + 2]),
                        waits=[t_ex] + p.bwait(("lrc", g2)), inc="dv")
                    t_tp = p.emit("tensor", lambda h, sb_=sb_, g2=g2: h.transpose(
                        psbf[sb_][:, 256:384], pslab[:, g2 * 128:g2 * 128 + 128],
                        ident[:]), waits=[t_ex] + p.bwait((sb_, "pt")), inc="pe")
                    t_pe = p.emit("scalar", lambda h, sb_=sb_, g2=g2: h.copy(
                        pslab[:, 512 + g2 * 128:512 + g2 * 128 + 128],
                        psbf[sb_][:, 256:384]), waits=[t_tp] + p.bwait(("lpt", g2)), inc="sc")
                    p.bread((sb_, "pt"), t_pe)
                    p.bread(("lps", g2), t_pe)
                    t_av = p.emit("tensor", lambda h, wn=wn, hh=hh, sb_=sb_, g2=g2: h.matmul(
                        psum[sb_][:, 128:192],
                        pslab[:, 512 + g2 * 128:512 + g2 * 128 + 128],
                        aAx[:, wn * 512 + hh * 64:wn * 512 + hh * 64 + 64],
                        start=True, stop=True), waits=[t_pe] + vl_toks, inc="pe")
                    p.bread(("lpt", g2), t_av)
                    t_oe = p.emit("scalar", lambda h, sb_=sb_, fc=fc, g2=g2: h.activation(
                        pslab[:, 1024 + g2 * 64:1024 + g2 * 64 + 64],
                        psum[sb_][:, 128:192], AF.Copy,
                        scale=fscr[:, fc + 2:fc + 3]),
                        waits=[t_av, t_rc] + p.bwait(("lo", g2)), inc="sc")
                    p.bread(("lrc", g2), t_oe)
                    t_oc2 = p.emit("vector", lambda h, wn=wn, hh=hh, g2=g2: h.tensor_copy(
                        aAx[:, 8192 + wn * 512 + hh * 64:8192 + wn * 512 + hh * 64 + 64],
                        pslab[:, 1024 + g2 * 64:1024 + g2 * 64 + 64]),
                        waits=[t_oe], inc="dv")
                    p.bread(("lo", g2), t_oc2)
                    light_o_toks.append(t_oc2)
        coor_done = coor_last
        light_done = light_o_toks[-1:]

        lf = [("sc", p.counts["sc"]), ("dv", p.counts["dv"])]

        kx = p.emit("vector", lambda h: h.tensor_scalar(
            misc[0:1, 0:2048], misc[0:1, 2048:4096], fscr[0:1, 145:146], 0.0,
            ALU.subtract, ALU.min), waits=coor_done + s_done, inc="dv")
        t_kT = None
        for c_ in range(16):
            tt_ = p.emit("tensor", lambda h, c_=c_: h.transpose(
                psum[5][:, 0:1], misc[0:1, c_ * 128:c_ * 128 + 128],
                identf[0:1, 0:1]), waits=[kx] + p.bwait(5), inc="pe")
            t_kT = p.emit("scalar", lambda h, c_=c_: h.copy(
                fscr[:, 32 + c_:32 + c_ + 1], psum[5][:, 0:1]), waits=[tt_], inc="sc")
            p.bread(5, t_kT)
        if "dbg3" in dumps:
            kx = p.dma("sync", dramf["dump_dbg3"][1:2, :], misc[0:1, 0:2048], waits=[kx])
        zz_ = p.emit("vector", lambda h: h.memset(fscr[0:1, 512:1024], 0.0),
                     waits=[kx], inc="dv")
        pz_ = p.emit("vector", lambda h: h.memset(misc[0:1, 2048:2049], 0.0),
                     waits=[kx, t_kT] + s_done + [t_s16, t_smax], inc="dv")
        prev_pref = pz_
        for ch in range(4):
            c0 = ch * 512
            n = 512 if ch < 3 else 511
            satc = p.emit("vector", lambda h, c0=c0: h.tensor_scalar(
                fscr[0:1, 0:512], misc[0:1, c0:c0 + 512], 0.0, None, ALU.is_ge),
                waits=[prev_pref], inc="dv")
            prev_pref = p.emit("vector", lambda h, c0=c0, n=n: h.tensor_tensor_scan(
                misc[0:1, 2048 + c0 + 1:2048 + c0 + 1 + n],
                fscr[0:1, 512:512 + n], fscr[0:1, 0:n],
                misc[0:1, 2048 + c0:2048 + c0 + 1],
                ALU.add, ALU.add), waits=[satc, zz_], inc="dv")
        if "dbg3" in dumps:
            prev_pref = p.dma("sync", dramf["dump_dbg3"][2:3, :], misc[0:1, 2048:4096],
                              waits=[prev_pref])
        t_pT = None
        for c_ in range(16):
            tt_ = p.emit("tensor", lambda h, c_=c_: h.transpose(
                psum[5][:, 0:1], misc[0:1, 2048 + c_ * 128:2048 + c_ * 128 + 128],
                identf[0:1, 0:1]), waits=[prev_pref] + p.bwait(5), inc="pe")
            t_pT = p.emit("scalar", lambda h, c_=c_: h.copy(
                fscr[:, 48 + c_:48 + c_ + 1], psum[5][:, 0:1]), waits=[tt_], inc="sc")
            p.bread(5, t_pT)
        cnt_toks = []
        for itile in range(16):
            acc = None
            for ch in range(4):
                t_r = p.emit("tensor", lambda h, ch=ch: h.matmul(
                    psum[4][:, 0:512], onesr[:], misc[0:1, ch * 512:ch * 512 + 512],
                    start=True, stop=True), waits=[kx] + p.bwait(4), inc="pe")
                t_c = p.emit("vector", lambda h, itile=itile, ch=ch: h.tensor_scalar(
                    fscr[:, 64 + ch * 512 // 4:64 + ch * 512 // 4] if False else
                    pslab[:, 0:512].bitcast(F32)[:, 0:256] if False else
                    fscr[:, 512:1024][:, ch * 128:ch * 128 + 128] if False else
                    fscr[:, 512:1024],
                    psum[4][:, 0:512], fscr[:, 32 + itile:32 + itile + 1], None,
                    ALU.is_gt), waits=[t_r, t_kT], inc="dv")
                p.bread(4, t_c)
                acc = p.emit("vector", lambda h, itile=itile, ch=ch: h.reduce_sum(
                    fscr[:, 64 + itile:64 + itile + 1] if ch == 0 else
                    fscr[:, 80 + itile:80 + itile + 1],
                    fscr[:, 512:1024], axis=AX.X), waits=[t_c], inc="dv")
                if ch > 0:
                    acc = p.emit("vector", lambda h, itile=itile: h.tensor_add(
                        fscr[:, 64 + itile:64 + itile + 1],
                        fscr[:, 64 + itile:64 + itile + 1],
                        fscr[:, 80 + itile:80 + itile + 1]), waits=[acc], inc="dv")
            eqi = p.emit("vector", lambda h, itile=itile: h.tensor_scalar(
                fscr[:, 96 + itile:96 + itile + 1] if False else fscr[:, 80 + itile:80 + itile + 1],
                fscr[:, 32 + itile:32 + itile + 1], 0.0, None, ALU.is_ge),
                waits=[acc, t_kT], inc="dv")
            tot = p.emit("vector", lambda h, itile=itile: h.scalar_tensor_tensor(
                fscr[:, 64 + itile:64 + itile + 1],
                fscr[:, 80 + itile:80 + itile + 1],
                fscr[:, 48 + itile:48 + itile + 1],
                fscr[:, 64 + itile:64 + itile + 1], ALU.mult, ALU.add),
                waits=[eqi, t_pT], inc="dv")
            msk = p.emit("vector", lambda h, itile=itile: h.tensor_scalar(
                fscr[:, 96 + itile:96 + itile + 1],
                fscr[:, 64 + itile:64 + itile + 1], float(NQ_SEL), None, ALU.is_lt),
                waits=[tot], inc="dv")
            cnt_toks.append(msk)
        if "mask_q" in dumps:
            cnt_toks.append(p.dma("sync", dumps["mask_q"], fscr[:, 96:112], waits=cnt_toks))
        if "route" in dumps:
            cnt_toks.append(p.dma("sync", dumps["route"], fscr[:, 0:144], waits=cnt_toks))
        if "srow" in dumps:
            cnt_toks.append(p.dma("sync", dramf["dump_srow"][0:1, :], misc[0:1, 2048:4096],
                                  waits=[prev_pref]))
            cnt_toks.append(p.dma("sync", dramf["dump_srow"][1:2, :], misc[0:1, 0:2048],
                                  waits=[kx] + cnt_toks))
        if "light_o" in dumps:
            cnt_toks.append(p.dma("sync",
                dramf["dump_light_o"].rearrange("p a b -> p (a b)"),
                aAx[:, 8192:16384], waits=light_o_toks))

        lo_fm_toks = []
        for tt in range(16):
            for blk in range(4):
                t_t = p.emit("tensor", lambda h, tt=tt, blk=blk: h.transpose(
                    psbf[4][:, 0:128],
                    aAx[:, 8192 + tt * 512 + blk * 128:8192 + tt * 512 + blk * 128 + 128],
                    ident[:]), waits=light_done + lf + p.bwait(4), inc="pe")
                ev = p.emit("scalar", lambda h, tt=tt, blk=blk: h.copy(
                    aAx[:, blk * 2048 + tt * 128:blk * 2048 + tt * 128 + 128],
                    psbf[4][:, 0:128]), waits=[t_t], inc="sc")
                p.bread(4, ev)
                lo_fm_toks.append(ev)
        ld_wol = wload(wb[:, 0:4096].rearrange("p (a b) -> p a b", a=4),
                       dramf["wo_light"], waits=lf + p.bwait(("lw", 0)))
        wol_adds = []
        for tq in range(4):
            last = {}
            for blk in range(NKC):
                for hk in range(4):
                    w_ = [ld_wol] + lo_fm_toks
                    if hk == 0:
                        w_ = w_ + p.bwait(blk)
                    last[blk] = p.emit("tensor", lambda h, blk=blk, hk=hk, tq=tq: h.matmul(
                        psum[blk][:, 0:512],
                        wb[:, hk * 1024 + blk * 128:hk * 1024 + blk * 128 + 128],
                        aAx[:, hk * 2048 + tq * 512:hk * 2048 + tq * 512 + 512],
                        start=(hk == 0), stop=(hk == 3), skip_group_check=True),
                        waits=w_, inc="pe" if hk == 3 else None)
                at = p.emit("vector", lambda h, blk=blk, tq=tq: h.tensor_add(
                    xres[:, blk, tq * 512:tq * 512 + 512],
                    xres[:, blk, tq * 512:tq * 512 + 512],
                    psum[blk][:, 0:512]), waits=[last[blk]], inc="dv")
                p.bread(blk, at)
                wol_adds.append(at)
        p.bread(("lw", 0), wol_adds[-1])

        hv_toks = []
        for wname, gscale, dst, tok_major in (
                ("wq_heavy", "q", 0, False), ("WkT_heavy", "k", 8192, False),
                ("WvT_heavy", "v", None, True)):
            ldh = wload(wb[:, 4096:8192].rearrange("p (a b) -> p a b", a=8),
                        dramf[wname], waits=lf + p.bwait(("hw",)))
            sc_toks2 = []
            for kc in range(NKC):
                if gscale == "q":
                    tsw = p.emit("vector", lambda h, kc=kc: h.tensor_scalar(
                        wb[:, 4096 + kc * 512:4096 + kc * 512 + 512],
                        wb[:, 4096 + kc * 512:4096 + kc * 512 + 512],
                        fscr[:, 16 + kc:16 + kc + 1], 1.0 / 8.0, ALU.mult, ALU.mult),
                        waits=[ldh, t_gh], inc="dv")
                else:
                    tsw = p.emit("vector", lambda h, kc=kc: h.tensor_scalar(
                        wb[:, 4096 + kc * 512:4096 + kc * 512 + 512],
                        wb[:, 4096 + kc * 512:4096 + kc * 512 + 512],
                        fscr[:, 16 + kc:16 + kc + 1], None, ALU.mult),
                        waits=[ldh, t_gh], inc="dv")
                sc_toks2.append(tsw)
            this = []
            if not tok_major:
                for blk in range(4):
                    for tq in range(4):
                        bank = 4 + (blk + tq) % 2
                        for kc in range(NKC):
                            w_ = sc_toks2 + xh_toks
                            if kc == 0:
                                w_ = w_ + p.bwait(bank)
                            mmq = p.emit("tensor", lambda h, blk=blk, kc=kc, tq=tq, bank=bank: h.matmul(
                                psum[bank][:, 0:512],
                                wb[:, 4096 + kc * 512 + blk * 128:4096 + kc * 512 + blk * 128 + 128],
                                actB[:, kc, tq * 512:tq * 512 + 512],
                                start=(kc == 0), stop=(kc == NKC - 1), skip_group_check=True),
                                waits=w_, inc="pe" if kc == NKC - 1 else None)
                        ev = p.emit("scalar", lambda h, dst=dst, blk=blk, tq=tq, bank=bank: h.copy(
                            aC[:, dst + blk * 2048 + tq * 512:dst + blk * 2048 + tq * 512 + 512],
                            psum[bank][:, 0:512]), waits=[mmq] + (wol_adds if dst == 0 else []),
                            inc="sc")
                        p.bread(bank, ev)
                        this.append(ev)
            else:
                for tt in range(16):
                    bank = 4 + tt % 2
                    for kc in range(NKC):
                        w_ = sc_toks2 + xh_toks
                        if kc == 0:
                            w_ = w_ + p.bwait(bank)
                        mmv = p.emit("tensor", lambda h, tt=tt, kc=kc, bank=bank: h.matmul(
                            psum[bank][:, 0:512],
                            actB[:, kc, tt * 128:tt * 128 + 128],
                            wb[:, 4096 + kc * 512:4096 + kc * 512 + 512],
                            start=(kc == 0), stop=(kc == NKC - 1), skip_group_check=True),
                            waits=w_, inc="pe" if kc == NKC - 1 else None)
                    evv = p.emit("scalar", lambda h, tt=tt, bank=bank: h.copy(
                        aAx[:, 8192 + tt * 512:8192 + tt * 512 + 512], psum[bank][:, 0:512]),
                        waits=[mmv] + lo_fm_toks, inc="sc")
                    p.bread(bank, evv)
                    this.append(evv)
            p.bread(("hw",), this[-1])
            hv_toks += this

        def hP_ap(gp_, c0, c1):
            return pslab[:, c0:c1] if gp_ == 0 else actB[:, 0, c0:c1]

        def hT_ap(gp_, c0, c1):
            return actB[:, 1 + gp_, c0:c1]

        ho_toks = []
        hgrp = {}

        def hfront(gi):
            hh = gi // 16
            qt = gi % 16
            gp_ = gi % 2
            b0 = gp_ * 4
            hp0 = (hh % 2) * 64
            hblk = hh // 2
            smm_t = []
            for ch in range(4):
                w_ = hv_toks + p.bwait(b0 + ch)
                smm_t.append(p.emit("tensor", lambda h, ch=ch, hp0=hp0, hblk=hblk, qt=qt, b0=b0: h.matmul(
                    psum[b0 + ch][:, 0:512],
                    aC[hp0:hp0 + 64, hblk * 2048 + qt * 128:hblk * 2048 + qt * 128 + 128],
                    aC[hp0:hp0 + 64, 8192 + hblk * 2048 + ch * 512:8192 + hblk * 2048 + ch * 512 + 512],
                    start=True, stop=True), waits=w_, inc="pe"))
            mx_t = []
            for ch in range(4):
                mx_t.append(p.emit("vector", lambda h, ch=ch, b0=b0, gp_=gp_: h.reduce_max(
                    fscr[:, 112 + gp_ * 8 + ch:112 + gp_ * 8 + ch + 1],
                    psum[b0 + ch][:, 0:512], axis=AX.X, negate=True),
                    waits=[smm_t[ch]], inc="dv"))
            mxf = p.emit("vector", lambda h, gp_=gp_: h.tensor_reduce(
                fscr[:, 128 + gp_:128 + gp_ + 1], fscr[:, 112 + gp_ * 8:112 + gp_ * 8 + 4],
                AX.X, ALU.min), waits=mx_t + p.bwait(("hmxf", gp_)), inc="dv")
            ex_t = []
            for ch in range(4):
                ex_t.append(p.emit("scalar", lambda h, ch=ch, b0=b0, gp_=gp_: h.activation(
                    hP_ap(gp_, ch * 512, ch * 512 + 512), psum[b0 + ch][:, 0:512], AF.Exp,
                    bias=fscr[:, 128 + gp_:128 + gp_ + 1],
                    accum_out=fscr[:, 130 + gp_ * 4 + ch:130 + gp_ * 4 + ch + 1]),
                    waits=[mxf] + p.bwait(("hP", gp_, ch)) +
                    (p.bwait(("hac", gp_)) if ch == 0 else []), inc="sc"))
                p.bread(b0 + ch, ex_t[-1])
            p.bread(("hmxf", gp_), ex_t[-1])
            sm1 = p.emit("vector", lambda h, gp_=gp_: h.reduce_sum(
                fscr[:, 138 + gp_:138 + gp_ + 1], fscr[:, 130 + gp_ * 4:130 + gp_ * 4 + 4],
                axis=AX.X), waits=ex_t, inc="dv")
            p.bread(("hac", gp_), sm1)
            rc1 = p.emit("vector", lambda h, gp_=gp_: h.reciprocal(
                fscr[:, 140 + gp_:140 + gp_ + 1], fscr[:, 138 + gp_:138 + gp_ + 1]),
                waits=[sm1], inc="dv")
            scl = p.emit("vector", lambda h, qt=qt, gp_=gp_: h.tensor_mul(
                fscr[:, 142 + gp_:142 + gp_ + 1], fscr[:, 140 + gp_:140 + gp_ + 1],
                fscr[:, 96 + qt:96 + qt + 1]),
                waits=[rc1] + cnt_toks + p.bwait(("hscl", gp_)), inc="dv")
            hgrp[gi] = (ex_t, scl)

        def hback(gi):
            hh = gi // 16
            qt = gi % 16
            gp_ = gi % 2
            b0 = gp_ * 4
            ex_t, scl = hgrp.pop(gi)
            ev_half = []
            for half in range(2):
                tts = []
                for j in range(8):
                    kvt = half * 8 + j
                    w_ = [ex_t[kvt // 4]]
                    if j == 0:
                        w_ = w_ + p.bwait(b0 + half)
                    tts.append(p.emit("tensor", lambda h, j=j, kvt=kvt, b0=b0, half=half, gp_=gp_: h.transpose(
                        psbf[b0 + half][:, j * 128:j * 128 + 128],
                        hP_ap(gp_, kvt * 128, kvt * 128 + 128), ident[:]),
                        waits=w_, inc="pe"))
                for ch in (2 * half, 2 * half + 1):
                    p.bread(("hP", gp_, ch), tts[-1])
                ev = p.emit("vector", lambda h, b0=b0, half=half, gp_=gp_: h.tensor_copy(
                    hT_ap(gp_, half * 1024, half * 1024 + 1024),
                    psbf[b0 + half][:, 0:1024]),
                    waits=[tts[-1]] + p.bwait(("hT", gp_, half)), inc="dv")
                p.bread(b0 + half, ev)
                ev_half.append(ev)
            av_last = None
            for kvt in range(16):
                half = kvt // 8
                w_ = [ev_half[half]]
                if kvt == 0:
                    w_ = w_ + p.bwait(b0 + 2)
                av_last = p.emit("tensor", lambda h, kvt=kvt, hh=hh, b0=b0, gp_=gp_: h.matmul(
                    psum[b0 + 2][:, 0:64],
                    hT_ap(gp_, kvt * 128, kvt * 128 + 128),
                    aAx[:, 8192 + kvt * 512 + hh * 64:8192 + kvt * 512 + hh * 64 + 64],
                    start=(kvt == 0), stop=(kvt == 15), skip_group_check=True),
                    waits=w_, inc="pe" if kvt in (7, 15) else None)
                if kvt == 7:
                    p.bread(("hT", gp_, 0), av_last)
            p.bread(("hT", gp_, 1), av_last)
            t_oe = p.emit("scalar", lambda h, qt=qt, hh=hh, b0=b0, gp_=gp_: h.activation(
                wb[:, qt * 512 + hh * 64:qt * 512 + hh * 64 + 64],
                psum[b0 + 2][:, 0:64], AF.Copy, scale=fscr[:, 142 + gp_:142 + gp_ + 1]),
                waits=[av_last, scl] + (p.bwait(("lw", 0)) if gi == 0 else []) +
                (p.bwait(("lw", 1)) if gi == 0 else []), inc="sc")
            p.bread(("hscl", gp_), t_oe)
            p.bread(b0 + 2, t_oe)
            ho_toks.append(t_oe)

        for gi in range(129):
            if gi < 128:
                hfront(gi)
            if gi >= 1:
                hback(gi - 1)

        ho_fm_toks = []
        for blk in range(4):
            for tg in range(2):
                bank = 4 + (blk * 2 + tg) % 2
                tts = []
                for j in range(8):
                    tt = tg * 8 + j
                    w_ = ho_toks if (blk == 0 and tg == 0 and j == 0) else []
                    if j == 0:
                        w_ = w_ + p.bwait(bank)
                    tts.append(p.emit("tensor", lambda h, tt=tt, blk=blk, j=j, bank=bank: h.transpose(
                        psbf[bank][:, j * 128:j * 128 + 128],
                        wb[:, tt * 512 + blk * 128:tt * 512 + blk * 128 + 128],
                        ident[:]), waits=w_, inc="pe"))
                ev = p.emit("vector", lambda h, blk=blk, tg=tg, bank=bank: h.tensor_copy(
                    aC[:, blk * 2048 + tg * 1024:blk * 2048 + tg * 1024 + 1024],
                    psbf[bank][:, 0:1024]),
                    waits=[tts[-1]] + ([("sc", p.counts["sc"]), ("dv", p.counts["dv"])]
                                       if blk == 0 and tg == 0 else []), inc="dv")
                p.bread(bank, ev)
                ho_fm_toks.append(ev)
        woh_adds = []
        ld_prev = {}
        for tq in range(4):
            last = {}
            for hk in range(4):
                par = hk % 2
                ldt = wload(wb[:, 8192 + par * 1024:8192 + par * 1024 + 1024],
                            dramf["wo_heavy"][:, hk, :],
                            waits=p.bwait(("whs", par)) + (ho_fm_toks[-1:] if tq == 0 and hk < 2 else []))
                for blk in range(NKC):
                    w_ = [ldt] + ho_fm_toks
                    if hk == 0:
                        w_ = w_ + p.bwait(blk)
                    last[blk] = p.emit("tensor", lambda h, blk=blk, hk=hk, tq=tq, par=par: h.matmul(
                        psum[blk][:, 0:512],
                        wb[:, 8192 + par * 1024 + blk * 128:8192 + par * 1024 + blk * 128 + 128],
                        aC[:, hk * 2048 + tq * 512:hk * 2048 + tq * 512 + 512],
                        start=(hk == 0), stop=(hk == 3), skip_group_check=True),
                        waits=w_, inc="pe" if hk == 3 or blk == NKC - 1 else None)
                p.bread(("whs", par), last[NKC - 1])
            for blk in range(NKC):
                at = p.emit("vector", lambda h, blk=blk, tq=tq: h.tensor_add(
                    xres[:, blk, tq * 512:tq * 512 + 512],
                    xres[:, blk, tq * 512:tq * 512 + 512],
                    psum[blk][:, 0:512]), waits=[last[blk]], inc="dv")
                p.bread(blk, at)
                woh_adds.append(at)
        ffn_done = woh_adds + wol_adds + [("dv", p.counts["dv"]), ("sc", p.counts["sc"])]
        if "x2" in dumps:
            ffn_done = ffn_done + [p.dma("sync", dumps["x2"], xres[:], waits=ffn_done)]

        fin = p.dma("sync", out_fm, xres[:], waits=ffn_done)
        p.wait_only("sync", [fin, ("dq_sync", p.counts["dq_sync"]),
                             ("dq_gpsimd", p.counts["dq_gpsimd"])])

        @block.sync
        def _(h):
            for f in p.ops["sync"]:
                f(h)

        @block.tensor
        def _(h):
            for f in p.ops["tensor"]:
                f(h)

        @block.vector
        def _(h):
            for f in p.ops["vector"]:
                f(h)

        @block.scalar
        def _(h):
            for f in p.ops["scalar"]:
                f(h)

        @block.gpsimd
        def _(h):
            for f in p.ops["gpsimd"]:
                f(h)

    return nc, dumps



def make_consts():
    c = np.zeros((128, 16), np.float32)
    c[:, 0] = 1.0
    c[:, 1] = -1.0
    c[:, 2] = 1e-5
    c[:, 3] = 1e-6
    c[:, 4] = -1e30
    c[:, 5] = 1.0 / 8.0
    c[:, 6] = -1.0 / 16.0
    c[:, 7] = 1.0 / 1024.0
    c[:, 8] = 32.0
    c[:, 9] = 1e-12
    c[:, 10] = float(np.log(np.float32(NQ_SEL * 9.0 / 8.0)))
    return c


def make_tril():
    m = np.zeros((128, 2, CG), BF)
    i = np.arange(128)[:, None]
    j = np.arange(CG)[None, :]
    m[:, 0, :] = np.where(j > i, 1e30, 0.0)
    m[:, 1, :] = np.where(j > i + 128, 1e30, 0.0)
    return m


def fm(w, p=128):
    din, dout = w.shape
    return np.ascontiguousarray(w.reshape(din // p, p, dout).transpose(1, 0, 2))


def vec_fm(v, p=128):
    n = v.shape[0]
    return np.ascontiguousarray(v.reshape(n // p, p).T)


def prep_inputs(kw):
    g = {}
    g["ident"] = np.eye(128, dtype=BF)
    g["identf"] = np.eye(128, dtype=np.float32)
    g["onesb"] = np.ones((128, 8), dtype=BF)
    g["onesr"] = np.ones((1, 128), dtype=np.float32)
    g["onesrb"] = np.ones((1, 128), dtype=BF)
    g["consts"] = make_consts()
    g["trilm"] = make_tril()
    g["Wq"] = fm(np.asarray(kw["L0_attn_Wq"]))
    g["Wk"] = fm(np.asarray(kw["L0_attn_Wk"]))
    g["Wv"] = fm(np.asarray(kw["L0_attn_Wv"]))
    g["Wg"] = fm(np.asarray(kw["L0_attn_Wg"]))
    g["Wo"] = fm(np.asarray(kw["L0_attn_Wo"]))
    g["Wgt"] = fm(np.asarray(kw["L0_attn_Wgt"]))
    g["gate_w"] = fm(np.asarray(kw["L0_ffn_gate_w"]))
    w1s = [fm(w) for w in np.asarray(kw["L0_ffn_w1"])]
    w3s = [fm(w) for w in np.asarray(kw["L0_ffn_w3"])]
    g["w13"] = np.ascontiguousarray(np.stack(
        [np.concatenate([a, b], axis=2) for a, b in zip(w1s, w3s)]))
    g["w2"] = np.ascontiguousarray(np.stack([fm(w) for w in np.asarray(kw["L0_ffn_w2"])]))
    g["W_inT"] = fm(np.ascontiguousarray(np.asarray(kw["L1_attn_W_in"]).T))
    g["conv_w"] = np.ascontiguousarray(
        np.asarray(kw["L1_attn_conv_w"])[:, 0, :].reshape(NDC, 128, 4).transpose(1, 0, 2))
    g["conv_b"] = vec_fm(np.asarray(kw["L1_attn_conv_b"]))
    g["W_xT"] = fm(np.ascontiguousarray(np.asarray(kw["L1_attn_W_x"]).T))
    g["W_dt"] = vec_fm(np.asarray(kw["L1_attn_W_dt"])[:, 0])
    g["b_dt"] = vec_fm(np.asarray(kw["L1_attn_b_dt"]))
    g["A_log"] = np.ascontiguousarray(
        np.asarray(kw["L1_attn_A_log"]).reshape(NDC, 128, NS).transpose(1, 0, 2))
    g["D_param"] = vec_fm(np.asarray(kw["L1_attn_D_param"]))
    g["W_outT"] = fm(np.ascontiguousarray(np.asarray(kw["L1_attn_W_out"]).T))
    g["gamma_light"] = vec_fm(np.asarray(kw["L1_ffn_gamma_light"]))
    g["gamma_heavy"] = vec_fm(np.asarray(kw["L1_ffn_gamma_heavy"]))
    g["wq_light"] = fm(np.asarray(kw["L1_ffn_wq_light"]))
    g["wkv_light"] = fm(np.asarray(kw["L1_ffn_wkv_light"]))
    g["wo_light"] = fm(np.asarray(kw["L1_ffn_wo_light"]))
    g["q_rt"] = vec_fm(np.asarray(kw["L1_ffn_q_routing_token"]))
    wkvh = np.asarray(kw["L1_ffn_wkv_heavy"]).reshape(H1, 2, DH1, D)
    g["WkT_heavy"] = fm(np.ascontiguousarray(
        wkvh[:, 0].transpose(2, 0, 1).reshape(D, HID)))
    g["WvT_heavy"] = fm(np.ascontiguousarray(
        wkvh[:, 1].transpose(2, 0, 1).reshape(D, HID)))
    g["wq_heavy"] = fm(np.asarray(kw["L1_ffn_wq_heavy"]))
    g["wo_heavy"] = fm(np.asarray(kw["L1_ffn_wo_heavy"]))

    x = np.asarray(kw["L0_attn_x"])
    xs = [np.ascontiguousarray(x[c].T.reshape(NKC, 128, T).transpose(1, 0, 2))
          for c in range(8)]
    return g, xs


def run(kw, dump_names=(), cores=8):
    nc, dumps = build_program(dump_names)
    g, xs = prep_inputs(kw)
    in_maps = [dict(g, x_fm=xs[c]) for c in range(cores)]
    res = run_bass_kernel_spmd(nc, in_maps, core_ids=list(range(cores)))
    return res


def kernel(**kw):
    res = run(kw)
    outs = []
    for c in range(8):
        ofm = np.asarray(res.results[c]["out_fm"], dtype=np.float32)
        outs.append(ofm.transpose(1, 0, 2).reshape(D, T).T)
    return np.stack(outs).astype(np.float32)


# version 13
# speedup vs baseline: 1.0359x; 1.0068x over previous
import contextlib
import numpy as np
import ml_dtypes
import concourse.bass as bass
import concourse.mybir as mybir
from concourse.bass_utils import run_bass_kernel_spmd

F32 = mybir.dt.float32
BF16 = mybir.dt.bfloat16
AF = mybir.ActivationFunctionType
ALU = mybir.AluOpType
AX = mybir.AxisListType
BF = ml_dtypes.bfloat16

T = 2048
D = 1024
NKC = D // 128
H0 = 16
DH0 = 64
CG = 256
DI = 2048
NDC = DI // 128
NS = 16
TCS = 512
HID = 512
H1 = 8
DH1 = 64
WIN = 128
NQ_SEL = 1024

ENGINES = ("sync", "tensor", "vector", "scalar", "gpsimd")

DEBUG_DUMPS = []


class Prog:

    def __init__(self, nc):
        self.nc = nc
        self.ops = {e: [] for e in ENGINES}
        self.sems = {}
        self.counts = {}
        self.last_wait = {e: {} for e in ENGINES}

    def new_sem(self, st, name):
        self.sems[name] = st.enter_context(self.nc.semaphore(name))
        self.counts[name] = 0

    def emit(self, eng, fn, waits=(), inc=None, incn=1):
        wl = []
        for s, v in waits:
            if s.startswith("dq_"):
                v = self.counts[s]
            if v > 0 and self.last_wait[eng].get(s, 0) < v:
                self.last_wait[eng][s] = v
                wl.append((s, v))
        tok = None
        if inc is not None:
            self.counts[inc] += incn
            tok = (inc, self.counts[inc])
        sems = self.sems

        def closure(h, wl=wl, fn=fn, inc=inc, incn=incn):
            for s, v in wl:
                h.wait_ge(sems[s], v)
            r = fn(h)
            if inc is not None:
                r.then_inc(sems[inc], incn)
        self.ops[eng].append(closure)
        return tok

    def wait_only(self, eng, waits):
        self.emit(eng, lambda h: None, waits=waits)

    bank_tok = None

    def bwait(self, bank):
        if self.bank_tok is None:
            self.bank_tok = {}
        t = self.bank_tok.get(bank)
        return [t] if t else []

    def bread(self, bank, tok):
        if self.bank_tok is None:
            self.bank_tok = {}
        self.bank_tok[bank] = tok

    def dma(self, eng, out, in_, waits=()):
        sem = "dq_" + eng
        return self.emit(eng, lambda h: h.dma_start(out=out, in_=in_),
                         waits=waits, inc=sem, incn=16)


def ceil_div(a, b):
    return -(-a // b)


def build_program(dump_names=()):
    nc = bass.Bass("TRN2", target_bir_lowering=False, detect_race_conditions=False)
    dramf = {}

    def din(name, shape, dtype=F32):
        dramf[name] = nc.dram_tensor(name, list(shape), dtype, kind="ExternalInput").ap()
        return dramf[name]

    def dout(name, shape, dtype=F32):
        dramf[name] = nc.dram_tensor(name, list(shape), dtype, kind="ExternalOutput").ap()
        return dramf[name]

    x_in = din("x_fm", (128, NKC, T))
    out_fm = dout("out_fm", (128, NKC, T))
    for w in ("Wq", "Wk", "Wv", "Wg", "Wo"):
        din(w, (128, NKC, D))
    din("Wgt", (128, NKC, H0))
    din("gate_w", (128, NKC, 8))
    din("w13", (8, 128, NKC, 1024))
    din("w2", (8, 128, 4, D))
    din("W_inT", (128, NKC, 2 * DI))
    din("conv_w", (128, NDC, 4))
    din("conv_b", (128, NDC))
    din("W_xT", (128, NDC, 33))
    din("W_dt", (128, NDC))
    din("b_dt", (128, NDC))
    din("A_log", (128, NDC, NS))
    din("D_param", (128, NDC))
    din("W_outT", (128, NDC, D))
    din("gamma_light", (128, NKC))
    din("gamma_heavy", (128, NKC))
    din("wq_light", (128, NKC, HID))
    din("wkv_light", (128, NKC, 2 * HID))
    din("wo_light", (128, 4, D))
    din("q_rt", (128, NKC))
    din("WkT_heavy", (128, NKC, HID))
    din("WvT_heavy", (128, NKC, HID))
    din("wq_heavy", (128, NKC, HID))
    din("wo_heavy", (128, 4, D))
    din("ident", (128, 128), BF16)
    din("identf", (128, 128))
    din("onesb", (128, 8), BF16)
    din("onesr", (1, 128))
    din("onesrb", (1, 128), BF16)
    din("consts", (128, 16))
    din("trilm", (128, 2, CG), BF16)
    x_old_hbm = nc.dram_tensor("x_old_hbm", [128, NKC, T], F32, kind="Internal").ap()
    x_new_hbm = nc.dram_tensor("x_new_hbm", [128, NKC, T], F32, kind="Internal").ap()

    dumps = {}

    def register_dump(name, shape, dtype=F32):
        if name in dump_names:
            dumps[name] = dout("dump_" + name, shape, dtype)

    register_dump("xn", (128, NKC, T), BF16)
    register_dump("q_tok", (128, 16, D), BF16)
    register_dump("qraw", (128, 4096), BF16)
    register_dump("wslot", (128, 1024), BF16)
    register_dump("vtok", (128, 4096), BF16)
    register_dump("rows", (128, 2048))
    register_dump("ptile", (128, 1024), BF16)
    register_dump("o_tok", (128, 16, D), BF16)
    register_dump("x1", (128, NKC, T))
    register_dump("moe_xs", (128, NKC, T), BF16)
    register_dump("moe_scale", (8, 512))
    register_dump("xd15", (128, 2048), BF16)
    register_dump("route", (128, 144))
    register_dump("srow", (4, 2048))
    register_dump("xn2b", (128, NKC, T), BF16)
    register_dump("dbg1", (128, 8))
    register_dump("dbg2", (128, 16))
    register_dump("dbg3", (4, 2048))
    register_dump("u_ssm", (128, NDC, T), BF16)
    register_dump("ssm_in", (64, T))
    register_dump("delta0", (128, T))
    register_dump("y_ssm", (128, NKC, T))
    register_dump("xn2", (128, NKC, T), BF16)
    register_dump("mask_q", (128, 16))
    register_dump("light_o", (128, 16, HID), BF16)
    register_dump("x2", (128, NKC, T))

    with contextlib.ExitStack() as st:
        sb = lambda name, shape, dtype: st.enter_context(nc.sbuf_tensor(name, list(shape), dtype))
        xres = sb("xres", (128, NKC, T), F32)
        actA = sb("actA", (128, 8, T), BF16)
        actB = sb("actB", (128, 8, T), BF16)
        actC = sb("actC", (128, 8, T), BF16)
        wbuf = sb("wbuf", (128, 5, T), BF16)
        misc = sb("misc", (128, 4096), F32)
        pslab = sb("pslab", (128, 2048), BF16)
        sscr = pslab
        fscr = sb("fscr", (128, 1024), F32)
        ident = sb("identsb", (128, 128), BF16)
        identf = sb("identfsb", (128, 128), F32)
        onesb = sb("onesbsb", (128, 8), BF16)
        onesr = sb("onesrsb", (1, 128), F32)
        onesrb = sb("onesrbsb", (1, 128), BF16)
        consts = sb("constsb", (128, 16), F32)
        trilm = sb("trilmsb", (128, 2, CG), BF16)
        psum = [st.enter_context(nc.psum_tensor(f"ps{i}", [128, 512], F32)) for i in range(8)]

        p = Prog(nc)
        for s in ("dq_sync", "dq_gpsimd", "pe", "dv", "sc", "gp"):
            p.new_sem(st, s)
        block = st.enter_context(nc.Block())

        C_ONE, C_NEG1, C_EPS_LN, C_EPS_RMS, C_NEG1E30, C_QSCALE, C_NGATE, C_INVD, C_SQRTD, C_EPS12, C_LOGK = range(11)
        cc = lambda i: consts[:, i:i + 1]

        tok_x = p.dma("sync", xres[:], x_in)
        p.dma("sync", ident[:], dramf["ident"])
        p.dma("sync", identf[:], dramf["identf"])
        p.dma("sync", onesb[:], dramf["onesb"])
        p.dma("sync", onesr[:], dramf["onesr"])
        p.dma("sync", onesrb[:], dramf["onesrb"])
        p.dma("sync", consts[:], dramf["consts"])
        p.dma("sync", trilm[:], dramf["trilm"])
        if "srow" in dumps:
            p.dma("sync", dramf["dump_srow"][2:3, 1024:2048], dramf["q_rt"])
        base_loads = [("dq_sync", p.counts["dq_sync"])]

        def layernorm(xsrc_f32, dst_bf16, stat_off, waits, stage_bf16):
            toks = []
            for tc in range(4):
                t0 = tc * 512
                ct = []
                for kc in range(NKC):
                    ct.append(p.emit("vector", lambda h, kc=kc, t0=t0: h.tensor_copy(
                        stage_bf16[:, kc, t0:t0 + 512], xsrc_f32[:, kc, t0:t0 + 512]),
                        waits=waits, inc="dv"))
                sq_tok = [None, None]
                for kc in range(NKC):
                    par = kc % 2
                    s0 = 512 * par
                    w_sq = [sq_tok[par]] if sq_tok[par] else []
                    tsq = p.emit("scalar", lambda h, kc=kc, t0=t0, s0=s0: h.activation(
                        sscr[:, s0:s0 + 512], stage_bf16[:, kc, t0:t0 + 512], AF.Square),
                        waits=[ct[kc]] + w_sq, inc="sc")
                    p.emit("tensor", lambda h, kc=kc, t0=t0: h.matmul(
                        psum[0][0:1, 0:512], onesb[:, 0:1], stage_bf16[:, kc, t0:t0 + 512],
                        start=(kc == 0), stop=(kc == NKC - 1), skip_group_check=True),
                        waits=[ct[kc]], inc="pe")
                    sq_tok[par] = p.emit("tensor", lambda h, kc=kc, s0=s0: h.matmul(
                        psum[1][0:1, 0:512], onesb[:, 0:1], sscr[:, s0:s0 + 512],
                        start=(kc == 0), stop=(kc == NKC - 1), skip_group_check=True),
                        waits=[tsq], inc="pe")
                pe_tok = ("pe", p.counts["pe"])
                m0 = stat_off + t0
                r0 = stat_off + T + t0
                t1_ = p.emit("vector", lambda h, m0=m0: h.tensor_scalar(
                    misc[0:1, m0:m0 + 512], psum[0][0:1, 0:512], 1.0 / D, None, ALU.mult),
                    waits=[pe_tok], inc="dv")
                t2_ = p.emit("vector", lambda h: h.tensor_scalar(
                    fscr[0:1, 0:512], psum[1][0:1, 0:512], 1.0 / D, None, ALU.mult),
                    waits=[pe_tok], inc="dv")
                t3_ = p.emit("scalar", lambda h, m0=m0: h.activation(
                    fscr[0:1, 512:1024], misc[0:1, m0:m0 + 512], AF.Square),
                    waits=[t1_], inc="sc")
                tv = p.emit("vector", lambda h: h.tensor_sub(
                    fscr[0:1, 0:512], fscr[0:1, 0:512], fscr[0:1, 512:1024]),
                    waits=[t2_, t3_], inc="dv")
                tq = p.emit("scalar", lambda h: h.activation(
                    fscr[0:1, 512:1024], fscr[0:1, 0:512], AF.Sqrt,
                    bias=consts[0:1, C_EPS_LN:C_EPS_LN + 1]), waits=[tv], inc="sc")
                tr = p.emit("vector", lambda h, r0=r0: h.reciprocal(
                    misc[0:1, r0:r0 + 512], fscr[0:1, 512:1024]), waits=[tq], inc="dv")
                tbm = p.emit("tensor", lambda h, m0=m0: h.matmul(
                    psum[2][:, 0:512], onesr[:],
                    misc[0:1, m0:m0 + 512], start=True, stop=True),
                    waits=[t1_], inc="pe")
                tbr = p.emit("tensor", lambda h, r0=r0: h.matmul(
                    psum[3][:, 0:512], onesr[:],
                    misc[0:1, r0:r0 + 512], start=True, stop=True),
                    waits=[tr], inc="pe")
                for kc in range(NKC):
                    p.emit("vector", lambda h, kc=kc, t0=t0: h.tensor_sub(
                        fscr[:, 0:512], xsrc_f32[:, kc, t0:t0 + 512], psum[2][:, 0:512]),
                        waits=[tbm], inc="dv")
                    tm_ = p.emit("vector", lambda h, kc=kc, t0=t0: h.tensor_mul(
                        dst_bf16[:, kc, t0:t0 + 512], fscr[:, 0:512], psum[3][:, 0:512]),
                        waits=[tbr], inc="dv")
                    toks.append(tm_)
                waits = list(waits) + [t1_, t2_, ("dv", p.counts["dv"])]
            return toks

        ln1_toks = layernorm(xres, actA, 0, [tok_x] + base_loads, actC)
        ln1_done = [("dv", p.counts["dv"])]

        if "xn" in dumps:
            p.dma("sync", dumps["xn"], actA[:], waits=ln1_done)

        aB = actB.rearrange("p a b -> p (a b)")
        aC = actC.rearrange("p a b -> p (a b)")
        wb = wbuf.rearrange("p a b -> p (a b)")
        S_f = misc
        Sbf = aB
        psbf = [ps.ap().bitcast(BF16) for ps in psum]

        tz0 = p.emit("vector", lambda h: h.memset(misc[:, 0:512], 0.0), inc="dv")
        tz1 = p.emit("vector", lambda h: h.memset(aB[:, 12288:12800], 0.0), inc="dv")
        tz2 = p.emit("vector", lambda h: h.memset(misc[0:16, 3728:3984], 0.0), inc="dv")
        zinit = [tz0, tz1, tz2]

        def wload(dst_ap, src_ap, waits=()):
            return p.dma("gpsimd", dst_ap, src_ap, waits=waits)

        TC0 = 512
        sbf_tok = {hh: tz1 for hh in range(H0)}
        prev_tc_done = ln1_done + zinit
        for tc in range(4):
            t0 = tc * TC0
            dump_toks = []
            xn = actA
            tc_start = prev_tc_done

            def kc_slot(i):
                return wb[:, (i % 8) * 1024:(i % 8) * 1024 + 1024]

            vg_results = {}
            prev_phase_mm = []
            for wname, dcol in (("Wv", None), ("Wg", None)):
                ldtoks = []
                for kc in range(NKC):
                    ldtoks.append(wload(kc_slot(kc), dramf[wname][:, kc, :],
                                        waits=tc_start + prev_phase_mm))
                mm_last = []
                for kc in range(NKC):
                    for tt in range(4):
                        ta = t0 + tt * 128
                        for hf in range(2):
                            bank = tt * 2 + hf
                            w_ = [ldtoks[kc]] + tc_start
                            if kc == 0:
                                w_ += p.bwait(bank)
                            tokmm = p.emit("tensor", lambda h, kc=kc, ta=ta, hf=hf, bank=bank: h.matmul(
                                psum[bank][:, 0:512],
                                xn[:, kc, ta:ta + 128],
                                kc_slot(kc)[:, hf * 512:hf * 512 + 512],
                                start=(kc == 0), stop=(kc == NKC - 1), skip_group_check=True),
                                waits=w_, inc="pe" if kc == NKC - 1 else None)
                            if kc == NKC - 1:
                                mm_last.append(tokmm)
                vg_results[wname] = mm_last
                prev_phase_mm = mm_last
                if wname == "Wv":
                    v_toks = []
                    for tt in range(4):
                        e0 = p.emit("scalar", lambda h, tt=tt: h.copy(
                            pslab[:, 0:512], psum[tt * 2][:, 0:512]),
                            waits=mm_last + v_toks[-1:], inc="sc")
                        e1_ = p.emit("scalar", lambda h, tt=tt: h.copy(
                            pslab[:, 512:1024], psum[tt * 2 + 1][:, 0:512]), waits=mm_last, inc="sc")
                        p.bread(tt * 2, e0)
                        p.bread(tt * 2 + 1, e1_)
                        sq = p.emit("scalar", lambda h: h.activation(
                            pslab[:, 1024:2048], pslab[:, 0:1024], AF.Square,
                            accum_out=fscr[:, 0:1]), waits=[e0, e1_], inc="sc")
                        rs = p.emit("scalar", lambda h: h.activation(
                            fscr[:, 1:2], fscr[:, 0:1], AF.Sqrt, scale=1.0 / D,
                            bias=cc(C_EPS_RMS)), waits=[sq], inc="sc")
                        rr = p.emit("vector", lambda h: h.reciprocal(
                            fscr[:, 2:3], fscr[:, 1:2]), waits=[rs], inc="dv")
                        vt = p.emit("vector", lambda h, tt=tt: h.tensor_scalar(
                            aC[:, 4096 + tt * 1024:4096 + (tt + 1) * 1024],
                            pslab[:, 0:1024], fscr[:, 2:3], None, ALU.mult),
                            waits=[rr], inc="dv")
                        v_toks.append(vt)
                    if tc == 0 and "vtok" in dumps:
                        dump_toks.append(p.dma("sync", dumps["vtok"], aC[:, 4096:8192], waits=v_toks))
                    prev_phase_mm = v_toks
                    prev_phase_mm = mm_last
            g_toks = []
            for tt in range(4):
                for hf in range(2):
                    gtk = p.emit("scalar", lambda h, tt=tt, hf=hf: h.copy(
                        aB[:, 4096 + tt * 1024 + hf * 512:4096 + tt * 1024 + hf * 512 + 512],
                        psum[tt * 2 + hf][:, 0:512]), waits=vg_results["Wg"], inc="sc")
                    p.bread(tt * 2 + hf, gtk)
                    g_toks.append(gtk)

            qk_norm = []
            prev_w_mms = vg_results["Wg"]
            for wname, arena in (("Wq", aB), ("Wk", aC)):
                ldtoks = []
                raw_toks = []
                this_mms = []
                for blk in range(NKC):
                    ldtoks.append(wload(
                        wb[:, blk * 1024:blk * 1024 + 1024].rearrange("p (a b) -> p a b", a=8),
                        dramf[wname][:, :, blk * 128:(blk + 1) * 128],
                        waits=prev_w_mms if blk < 8 else []))
                for blk in range(NKC):
                    bank = blk % 2
                    for kc in range(NKC):
                        w_ = [ldtoks[blk]]
                        if kc == 0:
                            w_ += p.bwait(bank) + g_toks
                        tokmm = p.emit("tensor", lambda h, blk=blk, kc=kc, bank=bank, t0=t0: h.matmul(
                            psum[bank][:, 0:512],
                            wb[:, blk * 1024 + kc * 128:blk * 1024 + kc * 128 + 128],
                            xn[:, kc, t0:t0 + 512],
                            start=(kc == 0), stop=(kc == NKC - 1), skip_group_check=True),
                            waits=w_, inc="pe" if kc == NKC - 1 else None)
                    this_mms.append(tokmm)
                    rt = p.emit("scalar", lambda h, blk=blk, bank=bank, arena=arena: h.copy(
                        arena[:, blk * 512:(blk + 1) * 512], psum[bank][:, 0:512]),
                        waits=[tokmm], inc="sc")
                    p.bread(bank, rt)
                    raw_toks.append(rt)
                if tc == 0 and wname == "Wq" and "qraw" in dumps:
                    dump_toks.append(p.dma("sync", dumps["qraw"], aB[:, 0:4096], waits=raw_toks))
                if tc == 0 and wname == "Wq" and "wslot" in dumps:
                    wt_ = p.dma("sync", dumps["wslot"], wb[:, 0:1024], waits=[ldtoks[0]])
                    this_mms.append(wt_)
                    dump_toks.append(wt_)
                prev_w_mms = this_mms
                sq_prev = {}
                for blk in range(NKC):
                    par = blk % 2
                    w_ = [raw_toks[blk]]
                    if par in sq_prev:
                        w_.append(sq_prev[par])
                    sqt = p.emit("scalar", lambda h, blk=blk, par=par, arena=arena: h.activation(
                        pslab[:, par * 512:par * 512 + 512],
                        arena[:, blk * 512:(blk + 1) * 512], AF.Square),
                        waits=w_, inc="sc")
                    w2_ = [sqt]
                    if blk == 0:
                        w2_ += p.bwait(2)
                    mmt = p.emit("tensor", lambda h, blk=blk, par=par: h.matmul(
                        psum[2][0:1, 0:512], onesb[:, 0:1], pslab[:, par * 512:par * 512 + 512],
                        start=(blk == 0), stop=(blk == NKC - 1), skip_group_check=True),
                        waits=w2_, inc="pe")
                    sq_prev[par] = mmt
                r1 = p.emit("scalar", lambda h: h.activation(
                    fscr[0:1, 0:512], psum[2][0:1, 0:512], AF.Sqrt, scale=1.0 / D,
                    bias=consts[0:1, C_EPS_RMS:C_EPS_RMS + 1]),
                    waits=[sq_prev[0], sq_prev[1]], inc="sc")
                p.bread(2, r1)
                r2 = p.emit("vector", lambda h: h.reciprocal(
                    fscr[0:1, 512:1024], fscr[0:1, 0:512]), waits=[r1], inc="dv")
                if wname == "Wq":
                    r2 = p.emit("vector", lambda h: h.tensor_scalar(
                        fscr[0:1, 512:1024], fscr[0:1, 512:1024], 1.0 / 8.0, None,
                        ALU.mult), waits=[r2], inc="dv")
                r3 = p.emit("tensor", lambda h: h.matmul(
                    psum[3][:, 0:512], onesr[:], fscr[0:1, 512:1024],
                    start=True, stop=True), waits=[r2] + p.bwait(3), inc="pe")
                lastn = None
                for blk in range(NKC):
                    lastn = p.emit("vector", lambda h, blk=blk, arena=arena: h.tensor_mul(
                        arena[:, 8192 + blk * 512:8192 + (blk + 1) * 512],
                        arena[:, blk * 512:(blk + 1) * 512], psum[3][:, 0:512]),
                        waits=[r3], inc="dv")
                    qk_norm.append(lastn)
                p.bread(3, lastn)
            qk_done = [("dv", p.counts["dv"])]

            ldg = wload(wb[:, 0:128].rearrange("p (a b) -> p a b", a=8),
                        dramf["Wgt"], waits=prev_w_mms)
            for kc in range(NKC):
                w_ = [ldg]
                if kc == 0:
                    w_ += p.bwait(4)
                tokmm = p.emit("tensor", lambda h, kc=kc, t0=t0: h.matmul(
                    psum[4][0:16, 0:512], wb[:, kc * 16:kc * 16 + 16],
                    xn[:, kc, t0:t0 + 512],
                    start=(kc == 0), stop=(kc == NKC - 1), skip_group_check=True),
                    waits=w_, inc="pe" if kc == NKC - 1 else None)
            a1 = p.emit("scalar", lambda h: h.activation(
                fscr[0:16, 0:512], psum[4][0:16, 0:512], AF.Abs), waits=[tokmm], inc="sc")
            a2 = p.emit("scalar", lambda h: h.activation(
                fscr[0:16, 512:1024], fscr[0:16, 0:512], AF.Exp, scale=-1.0), waits=[a1], inc="sc")
            a3 = p.emit("scalar", lambda h: h.activation(
                fscr[0:16, 512:1024], fscr[0:16, 512:1024], AF.Ln,
                bias=consts[0:16, C_ONE:C_ONE + 1]), waits=[a2], inc="sc")
            a4 = p.emit("vector", lambda h: h.tensor_scalar(
                fscr[0:16, 0:512], psum[4][0:16, 0:512], -1.0, 0.0, ALU.mult, ALU.max),
                waits=[a2], inc="dv")
            p.bread(4, a4)
            a5 = p.emit("vector", lambda h: h.tensor_add(
                fscr[0:16, 0:512], fscr[0:16, 0:512], fscr[0:16, 512:1024]),
                waits=[a3, a4], inc="dv")
            a6 = p.emit("vector", lambda h: h.tensor_scalar(
                fscr[0:16, 0:512], fscr[0:16, 0:512], -1.0 / 16.0, None, ALU.mult),
                waits=[a5], inc="dv")
            bt = []
            for nn in range(2):
                bt.append(p.emit("vector", lambda h, nn=nn: h.tensor_tensor_scan(
                    misc[0:16, 2048 + nn * 256:2048 + (nn + 1) * 256],
                    misc[0:16, 3728:3984],
                    fscr[0:16, nn * 256:(nn + 1) * 256],
                    0.0, ALU.add, ALU.add), waits=[a6] + zinit, inc="dv"))
            e1t = p.emit("scalar", lambda h: h.activation(
                misc[0:16, 2560:3072], misc[0:16, 2048:2560], AF.Exp), waits=bt, inc="sc")
            e2ts = []
            for nn in range(2):
                w1_ = p.emit("vector", lambda h, nn=nn: h.tensor_scalar(
                    fscr[0:16, 512 + nn * 256:512 + (nn + 1) * 256],
                    misc[0:16, 2048 + nn * 256:2048 + (nn + 1) * 256],
                    misc[0:16, 2048 + nn * 256 + 255:2048 + nn * 256 + 256],
                    None, ALU.subtract), waits=bt, inc="dv")
                e2ts.append(p.emit("scalar", lambda h, nn=nn: h.activation(
                    misc[0:16, 3072 + nn * 256:3072 + (nn + 1) * 256],
                    fscr[0:16, 512 + nn * 256:512 + (nn + 1) * 256],
                    AF.Exp, scale=-1.0), waits=[w1_], inc="sc"))
            bct_toks = []
            for st4 in range(4):
                tr1 = p.emit("tensor", lambda h, st4=st4: h.transpose(
                    psum[5][:, 0:16], misc[0:16, 2048 + st4 * 128:2048 + st4 * 128 + 128],
                    identf[0:16, 0:16]), waits=bt + p.bwait(5), inc="pe")
                ev1 = p.emit("scalar", lambda h, st4=st4: h.copy(
                    misc[:, 3584 + st4 * 16:3584 + st4 * 16 + 16], psum[5][:, 0:16]),
                    waits=[tr1], inc="sc")
                p.bread(5, ev1)
                tr2 = p.emit("tensor", lambda h, st4=st4: h.transpose(
                    psum[5][:, 0:16], misc[0:16, 3072 + st4 * 128:3072 + st4 * 128 + 128],
                    identf[0:16, 0:16]), waits=e2ts + [ev1], inc="pe")
                ev2 = p.emit("scalar", lambda h, st4=st4: h.copy(
                    misc[:, 3648 + st4 * 16:3648 + st4 * 16 + 16], psum[5][:, 0:16]),
                    waits=[tr2], inc="sc")
                p.bread(5, ev2)
                bct_toks += [ev1, ev2]
            rows_done = bct_toks + [e1t] + e2ts

            if tc == 0 and "rows" in dumps:
                dump_toks.append(p.dma("sync", dumps["rows"], misc[:, 2048:4096],
                                       waits=rows_done))
            dump_hold_q = dump_toks
            if "q_tok" in dumps:
                dump_hold_q.append(p.dma(
                    "sync", dumps["q_tok"].rearrange("p a b -> p (a b)")[:, tc * 4096:(tc + 1) * 4096],
                    aB[:, 8192:12288], waits=qk_done))

            for nn in range(2):
                nc0 = nn * 256
                t_tr = p.emit("tensor", lambda h, nn=nn: h.transpose(
                    psum[6][0:1, 0:16],
                    misc[0:16, 2560 + nn * 256 + 255:2560 + nn * 256 + 256],
                    identf[0:16, 0:16]), waits=rows_done + p.bwait(6), inc="pe")
                t_ev = p.emit("scalar", lambda h: h.copy(
                    misc[0:1, 3984:4000], psum[6][0:1, 0:16]), waits=[t_tr], inc="sc")
                p.bread(6, t_ev)
                t_rep = p.emit("tensor", lambda h: h.matmul(
                    psum[6][:, 16:32], onesr[0:1, 0:128], misc[0:1, 3984:4000],
                    start=True, stop=True), waits=[t_ev] + p.bwait(6), inc="pe")
                t_e3 = p.emit("scalar", lambda h: h.copy(
                    misc[:, 3712:3728], psum[6][:, 16:32]), waits=[t_rep], inc="sc")
                p.bread(6, t_e3)

                for blk in range(NKC):
                    qb = 6 + blk % 2
                    t_r1 = None
                    for hh2 in range(2):
                        sel1 = identf[0:16, 2 * blk + hh2:2 * blk + hh2 + 1].to_broadcast((16, 64))
                        t_r1 = p.emit("tensor", lambda h, nn=nn, sel1=sel1, hh2=hh2, qb=qb: h.matmul(
                            psum[qb][hh2 * 64:hh2 * 64 + 64, 32:288], sel1,
                            misc[0:16, 2560 + nn * 256:2560 + (nn + 1) * 256],
                            start=True, stop=True, skip_group_check=True),
                            waits=rows_done + (p.bwait(qb) if hh2 == 0 else []), inc="pe" if hh2 == 1 else None)
                    t_qd = p.emit("vector", lambda h, blk=blk, nc0=nc0, qb=qb: h.tensor_mul(
                        pslab[:, 1536:1792],
                        aB[:, 8192 + blk * 512 + nc0:8192 + blk * 512 + nc0 + 256],
                        psum[qb][:, 32:288]),
                        waits=[t_r1] + qk_done + p.bwait("qd"), inc="dv")
                    p.bread(qb, t_qd)
                    for hh in range(2):
                        hglob = 2 * blk + hh
                        par = hglob % 2
                        hp0 = hh * 64
                        selh = identf[0:16, hglob:hglob + 1].to_broadcast((16, 128))
                        t_rep2 = p.emit("tensor", lambda h, nn=nn, par=par, selh=selh: h.matmul(
                            psum[par][:, 0:256], selh,
                            misc[0:16, 2048 + nn * 256:2048 + (nn + 1) * 256],
                            start=True, stop=True), waits=rows_done + p.bwait(par), inc="pe")
                        for ss in range(2):
                            st4 = nn * 2 + ss
                            t_ea = p.emit("vector", lambda h, par=par, ss=ss, st4=st4, hglob=hglob: h.scalar_tensor_tensor(
                                fscr[:, ss * 256:(ss + 1) * 256],
                                psum[par][:, 0:256],
                                misc[:, 3584 + st4 * 16 + hglob:3584 + st4 * 16 + hglob + 1],
                                trilm[:, ss, :],
                                ALU.subtract, ALU.add),
                                waits=[t_rep2] + p.bwait(("warg", ss)), inc="dv")
                            if ss == 1:
                                p.bread(par, t_ea)
                            t_w = p.emit("scalar", lambda h, ss=ss: h.activation(
                                pslab[:, ss * 256:(ss + 1) * 256], fscr[:, ss * 256:(ss + 1) * 256],
                                AF.Exp, scale=-1.0),
                                waits=[t_ea] + p.bwait(("wtile", ss)), inc="sc")
                            p.bread(("warg", ss), t_w)
                            t_s = p.emit("tensor", lambda h, blk=blk, hp0=hp0, nc0=nc0, ss=ss, par=par: h.matmul(
                                psum[2 + par][:, 0:256],
                                aB[hp0:hp0 + 64, 8192 + blk * 512 + nc0 + ss * 128:
                                   8192 + blk * 512 + nc0 + ss * 128 + 128],
                                aC[hp0:hp0 + 64, 8192 + blk * 512 + nc0:8192 + blk * 512 + nc0 + 256],
                                start=True, stop=True), waits=qk_done + p.bwait(2 + par), inc="pe")
                            t_p = p.emit("vector", lambda h, ss=ss, par=par: h.tensor_mul(
                                pslab[:, 512 + ss * 256:512 + (ss + 1) * 256],
                                psum[2 + par][:, 0:256],
                                pslab[:, ss * 256:(ss + 1) * 256]),
                                waits=[t_w, t_s] + p.bwait(("P", ss)), inc="dv")
                            p.bread(2 + par, t_p)
                            p.bread(("wtile", ss), t_p)
                            ptt = []
                            for kvh in range(2):
                                t_t = p.emit("tensor", lambda h, ss=ss, kvh=kvh: h.transpose(
                                    psbf[4][:, ss * 256 + kvh * 128:ss * 256 + kvh * 128 + 128],
                                    pslab[:, 512 + ss * 256 + kvh * 128:512 + ss * 256 + kvh * 128 + 128],
                                    ident[:]), waits=[t_p] + (p.bwait(4) if kvh == 0 else []), inc="pe")
                                ptt.append(t_t)
                            t_pv = p.emit("vector", lambda h, ss=ss: h.tensor_copy(
                                pslab[:, 1024 + ss * 256:1024 + ss * 256 + 256],
                                psbf[4][:, ss * 256:ss * 256 + 256]),
                                waits=[ptt[-1]] + p.bwait(("PT", ss)), inc="dv")
                            p.bread(4, t_pv)
                            p.bread(("P", ss), t_pv)
                            for kvh in range(2):
                                loc_tt = (nc0 + kvh * 128) // 128
                                w_ = [t_pv] + v_toks
                                if kvh == 0 and ss == 0:
                                    w_ += p.bwait((2 + par, "o"))
                                t_o = p.emit("tensor", lambda h, ss=ss, kvh=kvh, loc_tt=loc_tt, hglob=hglob, par=par: h.matmul(
                                    psum[2 + par][:, 320 + ss * 64:320 + ss * 64 + 64],
                                    pslab[:, 1024 + ss * 256 + kvh * 128:1024 + ss * 256 + kvh * 128 + 128],
                                    aC[:, 4096 + loc_tt * 1024 + hglob * 64:4096 + loc_tt * 1024 + hglob * 64 + 64],
                                    start=(kvh == 0), stop=False, skip_group_check=True),
                                    waits=w_, inc=None)
                            t_oc = p.emit("tensor", lambda h, ss=ss, hp0=hp0, hglob=hglob, par=par: h.matmul(
                                psum[2 + par][:, 320 + ss * 64:320 + ss * 64 + 64],
                                pslab[hp0:hp0 + 64, 1536 + ss * 128:1536 + ss * 128 + 128],
                                Sbf[hp0:hp0 + 64, 12288 + (hglob // 2) * 64:12288 + (hglob // 2) * 64 + 64],
                                start=False, stop=True, skip_group_check=True),
                                waits=[t_qd, sbf_tok[hglob]], inc="pe")
                            p.bread(("PT", ss), t_oc)
                            p.bread("qd", t_oc)
                            oc_last = t_oc
                            if tc == 0 and nn == 0 and hglob == 0 and ss == 0 and "ptile" in dumps:
                                dump_toks.append(p.dma("sync", dumps["ptile"], pslab[:, 0:1024],
                                                       waits=[t_p, t_qd, ("sc", p.counts["sc"])]))
                        kv_mm = None
                        for kvh in range(2):
                            t_kt = p.emit("tensor", lambda h, hp0=hp0, blk=blk, nc0=nc0, kvh=kvh: h.transpose(
                                psbf[5][:, 0:64],
                                aC[hp0:hp0 + 64, 8192 + blk * 512 + nc0 + kvh * 128:
                                   8192 + blk * 512 + nc0 + kvh * 128 + 128],
                                ident[hp0:hp0 + 64, hp0:hp0 + 64]),
                                waits=qk_done + p.bwait(5), inc="pe")
                            t_kd = p.emit("vector", lambda h, nn=nn, kvh=kvh, hglob=hglob: h.tensor_scalar(
                                pslab[:, 1792 + kvh * 64:1792 + kvh * 64 + 64],
                                psbf[5][:, 0:64],
                                misc[:, 3648 + (nn * 2 + kvh) * 16 + hglob:
                                     3648 + (nn * 2 + kvh) * 16 + hglob + 1],
                                None, ALU.mult),
                                waits=[t_kt] + p.bwait(("kd", kvh)), inc="dv")
                            p.bread(5, t_kd)
                        for kvh in range(2):
                            loc_tt = (nc0 + kvh * 128) // 128
                            w_ = [("dv", p.counts["dv"])] if kvh == 0 else []
                            if kvh == 0:
                                w_ += p.bwait(2 + par)
                            kv_mm = p.emit("tensor", lambda h, kvh=kvh, loc_tt=loc_tt, hglob=hglob, par=par, hp0=hp0: h.matmul(
                                psum[2 + par][hp0:hp0 + 64, 256:320],
                                pslab[:, 1792 + kvh * 64:1792 + kvh * 64 + 64],
                                aC[:, 4096 + loc_tt * 1024 + hglob * 64:4096 + loc_tt * 1024 + hglob * 64 + 64],
                                start=(kvh == 0), stop=(kvh == 1), skip_group_check=True),
                                waits=w_, inc="pe" if kvh == 1 else None)
                        scol = (hglob // 2) * 64
                        t_su = p.emit("vector", lambda h, hglob=hglob, par=par, hp0=hp0, scol=scol: h.scalar_tensor_tensor(
                            misc[hp0:hp0 + 64, scol:scol + 64],
                            misc[hp0:hp0 + 64, scol:scol + 64],
                            misc[hp0:hp0 + 64, 3712 + hglob:3712 + hglob + 1],
                            psum[2 + par][hp0:hp0 + 64, 256:320],
                            ALU.mult, ALU.add), waits=[kv_mm, t_e3] + zinit, inc="dv")
                        p.bread(2 + par, t_su)
                        sbf_tok[hglob] = p.emit("vector", lambda h, hglob=hglob, hp0=hp0, scol=scol: h.tensor_copy(
                            aB[hp0:hp0 + 64, 12288 + scol:12288 + scol + 64],
                            misc[hp0:hp0 + 64, scol:scol + 64]), waits=[t_su], inc="dv")
                        loc0 = nc0 // 128
                        t_oe = p.emit("scalar", lambda h, loc0=loc0, hglob=hglob, par=par: h.copy(
                            aC[:, 12288:16384].rearrange("p (x y) -> p x y", x=4)
                            [:, loc0:loc0 + 2, hglob * 64:hglob * 64 + 64],
                            psum[2 + par][:, 320:448].rearrange("p (x y) -> p x y", x=2)),
                            waits=[oc_last, kv_mm, t_su], inc="sc")
                        p.bread((2 + par, "o"), t_oe)
                        p.bread(2 + par, t_oe)
            stageB_done = [("dv", p.counts["dv"]), ("sc", p.counts["sc"])]
            dump_hold_o = []
            if "o_tok" in dumps:
                dump_hold_o.append(p.dma(
                    "sync", dumps["o_tok"].rearrange("p a b -> p (a b)")[:, tc * 4096:(tc + 1) * 4096],
                    aC[:, 12288:16384], waits=stageB_done))

            ot_toks = []
            for tt in range(4):
                sq_toks = []
                for hh in range(H0):
                    sq_toks.append(p.emit("scalar", lambda h, tt=tt, hh=hh: h.activation(
                        pslab[:, 1024 + hh * 64:1024 + hh * 64 + 64],
                        aC[:, 12288 + tt * 1024 + hh * 64:12288 + tt * 1024 + hh * 64 + 64],
                        AF.Square, accum_out=fscr[:, hh:hh + 1]),
                        waits=stageB_done + (ot_toks[-1:] if ot_toks else []), inc="sc"))
                r1 = p.emit("scalar", lambda h: h.activation(
                    fscr[:, 16:32], fscr[:, 0:16], AF.Sqrt, scale=1.0 / DH0,
                    bias=cc(C_EPS_RMS)), waits=sq_toks, inc="sc")
                r2 = p.emit("vector", lambda h: h.reciprocal(
                    fscr[:, 32:48], fscr[:, 16:32]), waits=[r1], inc="dv")
                og_toks = []
                for hh in range(H0):
                    og_toks.append(p.emit("vector", lambda h, tt=tt, hh=hh: h.tensor_scalar(
                        pslab[:, hh * 64:hh * 64 + 64],
                        aC[:, 12288 + tt * 1024 + hh * 64:12288 + tt * 1024 + hh * 64 + 64],
                        fscr[:, 32 + hh:32 + hh + 1], None, ALU.mult), waits=[r2], inc="dv"))
                sg = p.emit("scalar", lambda h, tt=tt: h.activation(
                    pslab[:, 1024:2048], aB[:, 4096 + tt * 1024:4096 + (tt + 1) * 1024],
                    AF.Silu), waits=sq_toks, inc="sc")
                og2 = p.emit("vector", lambda h, tt=tt: h.tensor_mul(
                    aC[:, 12288 + tt * 1024:12288 + (tt + 1) * 1024],
                    pslab[:, 0:1024], pslab[:, 1024:2048]),
                    waits=og_toks + [sg] + dump_hold_o, inc="dv")
                ot_toks.append(og2)
            ogt_toks = []
            for tt in range(4):
                tts = []
                for blk in range(NKC):
                    w_ = [ot_toks[tt]]
                    if blk == 0:
                        w_ = w_ + p.bwait(4)
                    tts.append(p.emit("tensor", lambda h, tt=tt, blk=blk: h.transpose(
                        psbf[4][:, blk * 128:blk * 128 + 128],
                        aC[:, 12288 + tt * 1024 + blk * 128:12288 + tt * 1024 + blk * 128 + 128],
                        ident[:]), waits=w_, inc="pe"))
                ev = p.emit("vector", lambda h, tt=tt: h.tensor_copy(
                    aB[:, 8192 + tt * 128:8192 + tt * 128 + 4096]
                    .rearrange("p (x y) -> p x y", x=8)[:, :, 0:128],
                    psbf[4][:, 0:1024].rearrange("p (x y) -> p x y", x=8)),
                    waits=[tts[-1]] + dump_hold_q, inc="dv")
                p.bread(4, ev)
                ogt_toks.append(ev)
            resid_toks = []
            slot_consumer = {}
            for blk in range(NKC):
                slot = blk % 2
                w_ = list(ogt_toks[-1:])
                if slot in slot_consumer:
                    w_.append(slot_consumer[slot])
                ldt = wload(
                    wb[:, 8192 + slot * 1024:8192 + slot * 1024 + 1024].rearrange(
                        "p (a b) -> p a b", a=8),
                    dramf["Wo"][:, :, blk * 128:(blk + 1) * 128], waits=w_)
                bank = blk % 2
                for kc in range(NKC):
                    w2_ = [ldt] + ogt_toks
                    if kc == 0:
                        w2_ += p.bwait(bank)
                    t_mm = p.emit("tensor", lambda h, blk=blk, kc=kc, bank=bank, slot=slot: h.matmul(
                        psum[bank][:, 0:512],
                        wb[:, 8192 + slot * 1024 + kc * 128:8192 + slot * 1024 + kc * 128 + 128],
                        aB[:, 8192 + kc * 512:8192 + kc * 512 + 512],
                        start=(kc == 0), stop=(kc == NKC - 1), skip_group_check=True),
                        waits=w2_, inc="pe" if kc == NKC - 1 else None)
                slot_consumer[slot] = t_mm
                rt = p.emit("vector", lambda h, blk=blk, bank=bank, t0=t0: h.tensor_add(
                    xres[:, blk, t0:t0 + 512], xres[:, blk, t0:t0 + 512],
                    psum[bank][:, 0:512]), waits=[t_mm], inc="dv")
                p.bread(bank, rt)
                resid_toks.append(rt)

            prev_tc_done = resid_toks + dump_toks + dump_hold_q + dump_hold_o + [
                ("dv", p.counts["dv"]), ("sc", p.counts["sc"])]

        attn_done = prev_tc_done

        moe_prev = attn_done
        w13 = dramf["w13"]
        w2d = dramf["w2"]
        for tc in range(4):
            t0 = tc * 512
            ldgate = wload(wb[:, 0:64].rearrange("p (a b) -> p a b", a=8),
                           dramf["gate_w"], waits=moe_prev)
            for kc in range(NKC):
                w_ = [ldgate] + (moe_prev if kc == 0 else []) + (p.bwait(6) if kc == 0 else [])
                gmm = p.emit("tensor", lambda h, kc=kc, t0=t0: h.matmul(
                    psum[6][0:8, 0:512], wb[:, kc * 8:kc * 8 + 8],
                    actA[:, kc, t0:t0 + 512],
                    start=(kc == 0), stop=(kc == NKC - 1), skip_group_check=True),
                    waits=w_, inc="pe" if kc == NKC - 1 else None)
            gev = p.emit("scalar", lambda h: h.copy(fscr[0:8, 0:512], psum[6][0:8, 0:512]),
                         waits=[gmm], inc="sc")
            p.bread(6, gev)
            sc_toks = []
            for tt in range(4):
                t_t = p.emit("tensor", lambda h, tt=tt: h.transpose(
                    psum[7][:, 0:8], fscr[0:8, tt * 128:tt * 128 + 128],
                    identf[0:8, 0:8]), waits=[gev] + p.bwait(7), inc="pe")
                lev = p.emit("scalar", lambda h, tt=tt: h.copy(
                    fscr[:, 512 + tt * 8:512 + tt * 8 + 8], psum[7][:, 0:8]),
                    waits=[t_t], inc="sc")
                p.bread(7, lev)
                c0 = 512 + tt * 8
                m1 = p.emit("vector", lambda h, c0=c0: h.reduce_max(
                    fscr[:, 544:545], fscr[:, c0:c0 + 8], axis=AX.X, negate=True),
                    waits=[lev], inc="dv")
                ex = p.emit("scalar", lambda h, c0=c0: h.activation(
                    fscr[:, 552 + 0:552 + 8], fscr[:, c0:c0 + 8], AF.Exp,
                    bias=fscr[:, 544:545], accum_out=fscr[:, 545:546]),
                    waits=[m1], inc="sc")
                rc = p.emit("vector", lambda h: h.reciprocal(
                    fscr[:, 546:547], fscr[:, 545:546]), waits=[ex], inc="dv")
                pr = p.emit("vector", lambda h: h.tensor_scalar(
                    fscr[:, 552:560], fscr[:, 552:560], fscr[:, 546:547], None, ALU.mult),
                    waits=[rc], inc="dv")
                mm1 = p.emit("vector", lambda h: h.reduce_max(
                    fscr[:, 544:545], fscr[:, 552:560], axis=AX.X), waits=[pr], inc="dv")
                mk1 = p.emit("vector", lambda h: h.tensor_scalar(
                    fscr[:, 560:568], fscr[:, 552:560], fscr[:, 544:545], None, ALU.is_ge),
                    waits=[mm1], inc="dv")
                p2 = p.emit("vector", lambda h: h.scalar_tensor_tensor(
                    fscr[:, 568:576], fscr[:, 560:568], cc(C_NEG1E30), fscr[:, 552:560],
                    ALU.mult, ALU.add), waits=[mk1], inc="dv")
                mm2 = p.emit("vector", lambda h: h.reduce_max(
                    fscr[:, 545:546], fscr[:, 568:576], axis=AX.X), waits=[p2], inc="dv")
                sel = p.emit("vector", lambda h: h.tensor_scalar(
                    fscr[:, 560:568], fscr[:, 552:560], fscr[:, 545:546], None, ALU.is_ge),
                    waits=[mm2], inc="dv")
                scl = p.emit("vector", lambda h: h.tensor_mul(
                    fscr[:, 552:560], fscr[:, 552:560], fscr[:, 560:568]),
                    waits=[sel], inc="dv")
                t_b = p.emit("tensor", lambda h, tt=tt: h.transpose(
                    psum[7][0:8, 8:136], fscr[:, 552:560], identf[:, 0:128]),
                    waits=[scl] + p.bwait(7), inc="pe")
                sev = p.emit("scalar", lambda h, tt=tt, t0=t0: h.copy(
                    misc[0:8, 2048 + t0 + tt * 128:2048 + t0 + tt * 128 + 128],
                    psum[7][0:8, 8:136]), waits=[t_b], inc="sc")
                p.bread(7, sev)
                sc_toks.append(sev)
            if tc == 0 and "moe_scale" in dumps:
                sc_toks.append(p.dma("sync", dumps["moe_scale"], misc[0:8, 2048:2560],
                                     waits=sc_toks))

            h_toks_all = []
            for e in range(8):
                xs0 = (e % 2) * 4096
                ld13k = []
                for kc in range(NKC):
                    ld13k.append(wload(
                        wb[:, kc * 1024:kc * 1024 + 1024], w13[e][:, kc, :],
                        waits=p.bwait(("w13", kc)) + (sc_toks if e == 0 else [])))
                rb = 6 + e % 2
                sel1 = identf[0:8, e:e + 1].to_broadcast((8, 128))
                t_r = p.emit("tensor", lambda h, t0=t0, sel1=sel1, rb=rb: h.matmul(
                    psum[rb][:, 0:512], sel1,
                    misc[0:8, 2048 + t0:2048 + t0 + 512],
                    start=True, stop=True), waits=sc_toks + p.bwait(rb), inc="pe")
                re_ = p.emit("scalar", lambda h, rb=rb, e=e: h.copy(
                    pslab[:, (e % 2) * 512:(e % 2) * 512 + 512], psum[rb][:, 0:512]),
                    waits=[t_r] + p.bwait(("sr", e % 2)), inc="sc")
                p.bread(rb, re_)
                xs_toks = []
                for kc in range(NKC):
                    xt = p.emit("vector", lambda h, kc=kc, t0=t0, xs0=xs0, e=e: h.tensor_mul(
                        aC[:, xs0 + kc * 512:xs0 + kc * 512 + 512],
                        actA[:, kc, t0:t0 + 512],
                        pslab[:, (e % 2) * 512:(e % 2) * 512 + 512]),
                        waits=[re_] + p.bwait(("xs", e % 2, kc)), inc="dv")
                    xs_toks.append(xt)
                p.bread(("sr", e % 2), xs_toks[-1])
                if tc == 0 and e == 0 and "moe_xs" in dumps:
                    xs_toks.append(p.dma(
                        "sync", dramf["dump_moe_xs"].rearrange("p a b -> p (a b)")[:, 0:4096],
                        aC[:, 0:4096], waits=xs_toks))
                ev_toks = []
                kc_last = {}
                for hblk in range(4):
                    b1 = hblk % 2
                    b3 = 2 + hblk % 2
                    for kc in range(NKC):
                        w_ = [ld13k[kc], xs_toks[kc]]
                        m1_ = p.emit("tensor", lambda h, hblk=hblk, kc=kc, b1=b1, xs0=xs0: h.matmul(
                            psum[b1][:, 0:512],
                            wb[:, kc * 1024 + hblk * 128:kc * 1024 + hblk * 128 + 128],
                            aC[:, xs0 + kc * 512:xs0 + kc * 512 + 512],
                            start=(kc == 0), stop=(kc == NKC - 1), skip_group_check=True),
                            waits=w_ + (p.bwait(b1) if kc == 0 else []),
                            inc="pe" if kc == NKC - 1 else None)
                        m3_ = p.emit("tensor", lambda h, hblk=hblk, kc=kc, b3=b3, xs0=xs0: h.matmul(
                            psum[b3][:, 0:512],
                            wb[:, kc * 1024 + 512 + hblk * 128:kc * 1024 + 512 + hblk * 128 + 128],
                            aC[:, xs0 + kc * 512:xs0 + kc * 512 + 512],
                            start=(kc == 0), stop=(kc == NKC - 1), skip_group_check=True),
                            waits=(p.bwait(b3) if kc == 0 else []) + w_,
                            inc="pe" if hblk == 3 or kc == NKC - 1 else None)
                        if hblk == 3:
                            kc_last[kc] = m3_
                    sl = p.emit("scalar", lambda h, hblk=hblk, b1=b1: h.activation(
                        pslab[:, 1024 + (hblk % 2) * 512:1024 + (hblk % 2) * 512 + 512],
                        psum[b1][:, 0:512], AF.Silu),
                        waits=[m1_] + p.bwait(("sil", hblk % 2)), inc="sc")
                    p.bread(b1, sl)
                    ht = p.emit("vector", lambda h, hblk=hblk, b3=b3, e=e: h.tensor_mul(
                        aB[:, e * 2048 + hblk * 512:e * 2048 + hblk * 512 + 512],
                        psum[b3][:, 0:512],
                        pslab[:, 1024 + (hblk % 2) * 512:1024 + (hblk % 2) * 512 + 512]),
                        waits=[m3_, sl], inc="dv")
                    p.bread(b3, ht)
                    p.bread(("sil", hblk % 2), ht)
                    ev_toks.append(ht)
                for kc in range(NKC):
                    p.bread(("w13", kc), kc_last[kc])
                for kc in range(NKC):
                    p.bread(("xs", e % 2, kc), ev_toks[-1])
                h_toks_all += ev_toks
            y_toks = []
            last_mm = {}
            for e in range(8):
                for hk in range(4):
                    par = (e * 4 + hk) % 2
                    ldt = wload(wb[:, 8192 + par * 1024:8192 + par * 1024 + 1024],
                                w2d[e][:, hk, :],
                                waits=p.bwait(("w2s", par)) + (h_toks_all[-1:] if e == 0 and hk == 0 else []))
                    for blk in range(NKC):
                        w_ = [ldt]
                        if e == 0 and hk == 0:
                            w_ += p.bwait(blk) + h_toks_all[-1:]
                        last_mm[blk] = p.emit("tensor", lambda h, blk=blk, e=e, hk=hk, par=par: h.matmul(
                            psum[blk][:, 0:512],
                            wb[:, 8192 + par * 1024 + blk * 128:8192 + par * 1024 + blk * 128 + 128],
                            aB[:, e * 2048 + hk * 512:e * 2048 + hk * 512 + 512],
                            start=(e == 0 and hk == 0), stop=(e == 7 and hk == 3),
                            skip_group_check=True),
                            waits=w_, inc="pe" if blk == NKC - 1 or (e == 7 and hk == 3) else None)
                    p.bread(("w2s", par), last_mm[NKC - 1])
            for blk in range(NKC):
                rt = p.emit("vector", lambda h, blk=blk, t0=t0: h.tensor_add(
                    xres[:, blk, t0:t0 + 512], xres[:, blk, t0:t0 + 512],
                    psum[blk][:, 0:512]), waits=[last_mm[blk]], inc="dv")
                p.bread(blk, rt)
                y_toks.append(rt)
            moe_prev = y_toks + [("dv", p.counts["dv"]), ("sc", p.counts["sc"])]

        moe_done = moe_prev
        if "x1" in dumps:
            moe_done = moe_done + [p.dma("sync", dumps["x1"], xres[:], waits=moe_done)]

        ln2_toks = layernorm(xres, actA, 0, moe_done, actC)
        ln2_done = [("dv", p.counts["dv"])]
        tst = p.dma("sync", misc[64:66, 0:2048], misc[0:1, 0:4096], waits=ln2_done)
        if "xn2" in dumps:
            ln2_done = ln2_done + [p.dma("sync", dumps["xn2"], actA[:], waits=ln2_done)]

        t_pw = []
        t_pw.append(p.dma("sync", misc[:, 2560:2624].rearrange("p (a b) -> p a b", a=16),
                          dramf["conv_w"], waits=ln2_done))
        t_pw.append(p.dma("sync", misc[:, 2624:2640], dramf["conv_b"], waits=ln2_done))
        t_pw.append(p.dma("sync", misc[:, 2640:2656], dramf["W_dt"], waits=ln2_done))
        t_pw.append(p.dma("sync", misc[:, 2656:2672], dramf["b_dt"], waits=ln2_done))
        t_pw.append(p.dma("sync", misc[:, 2672:2688], dramf["D_param"], waits=ln2_done))
        t_pw.append(p.dma("sync", misc[:, 2688:2944].rearrange("p (a b) -> p a b", a=16),
                          dramf["A_log"], waits=ln2_done))
        t_A = p.emit("scalar", lambda h: h.activation(
            misc[:, 2688:2944], misc[:, 2688:2944], AF.Exp), waits=t_pw, inc="sc")
        t_A = p.emit("vector", lambda h: h.tensor_scalar(
            misc[:, 2688:2944], misc[:, 2688:2944], -1.0, None, ALU.mult),
            waits=[t_A], inc="dv")
        ld_wx = wload(wb[:, 9216:9744].rearrange("p (a b) -> p a b", a=16),
                      dramf["W_xT"], waits=ln2_done)

        u_hbm = nc.dram_tensor("u_hbm", [128, NDC, T], BF16, kind="Internal").ap()
        sz_hbm = nc.dram_tensor("sz_hbm", [128, NDC, T], BF16, kind="Internal").ap()

        xn2 = actA
        uu_toks = []
        rows_mm = {}
        for blk in range(NDC):
            slot = blk % 4
            ldt = wload(wb[:, slot * 1024:slot * 1024 + 1024].rearrange(
                "p (a b) -> p a b", a=8),
                dramf["W_inT"][:, :, blk * 128:(blk + 1) * 128],
                waits=p.bwait(("wi", slot)) + (ln2_done if blk < 4 else []))
            xd0 = (blk % 2) * 2048
            ev_toks = []
            for tq in range(4):
                bank = 4 + ((blk + tq) % 4)
                for kc in range(NKC):
                    w_ = [ldt]
                    if kc == 0:
                        w_ += p.bwait(bank) + ln2_done
                    mm = p.emit("tensor", lambda h, kc=kc, tq=tq, bank=bank, slot=slot: h.matmul(
                        psum[bank][:, 0:512],
                        wb[:, slot * 1024 + kc * 128:slot * 1024 + kc * 128 + 128],
                        xn2[:, kc, tq * 512:tq * 512 + 512],
                        start=(kc == 0), stop=(kc == NKC - 1), skip_group_check=True),
                        waits=w_, inc="pe" if kc == NKC - 1 else None)
                ev = p.emit("scalar", lambda h, tq=tq, bank=bank, xd0=xd0: h.copy(
                    aB[:, xd0 + tq * 512:xd0 + tq * 512 + 512],
                    psum[bank][:, 0:512]), waits=[mm] + p.bwait(("xd", blk % 2)), inc="sc")
                p.bread(bank, ev)
                ev_toks.append(ev)
            p.bread(("wi", slot), ev_toks[-1])
            if blk == 15 and "xd15" in dumps:
                xdt = p.dma("sync", dumps["xd15"], aB[:, xd0:xd0 + 2048], waits=ev_toks)
                ev_toks.append(xdt)
            u0 = 4096 + (blk % 2) * 2048
            tu = None
            for hf in range(2):
                hb = hf * 1024
                acc = ((lambda c0, c1: misc[:, 2944 + c0:2944 + c1]) if hf == 0
                       else (lambda c0, c1: fscr[:, c0:c1]))
                prev = None
                for j in range(4):
                    lo = hb - 3 + j
                    srcs = max(lo, 0)
                    do = srcs - lo
                    n = 1024 - do
                    if j == 0:
                        if do > 0:
                            prev = p.emit("vector", lambda h, acc=acc, do=do: h.memset(
                                acc(0, do), 0.0),
                                waits=ev_toks + [t_A] + p.bwait(("cacc", hf)), inc="dv")
                        prev = p.emit("vector", lambda h, xd0=xd0, blk=blk, srcs=srcs, do=do, n=n, acc=acc: h.tensor_scalar(
                            acc(do, do + n), aB[:, xd0 + srcs:xd0 + srcs + n],
                            misc[:, 2560 + blk * 4:2560 + blk * 4 + 1], None, ALU.mult),
                            waits=([prev] if prev else ev_toks + [t_A] + p.bwait(("cacc", hf))),
                            inc="dv")
                    else:
                        prev = p.emit("vector", lambda h, j=j, xd0=xd0, blk=blk, srcs=srcs, do=do, n=n, acc=acc: h.scalar_tensor_tensor(
                            acc(do, do + n), aB[:, xd0 + srcs:xd0 + srcs + n],
                            misc[:, 2560 + blk * 4 + j:2560 + blk * 4 + j + 1],
                            acc(do, do + n), ALU.mult, ALU.add),
                            waits=[prev], inc="dv")
                tu = p.emit("scalar", lambda h, u0=u0, hb=hb, blk=blk, acc=acc: h.activation(
                    aB[:, u0 + hb:u0 + hb + 1024], acc(0, 1024), AF.Silu,
                    bias=misc[:, 2624 + blk:2624 + blk + 1]),
                    waits=[prev] + (p.bwait(("us", blk % 2)) + p.bwait(("usd", blk % 2)) if hf == 0 else []),
                    inc="sc")
                p.bread(("cacc", hf), tu)
            tdu = p.dma("sync", u_hbm[:, blk, :], aB[:, u0:u0 + 2048], waits=[tu])
            uu_toks.append(tdu)
            if "u_ssm" in dumps:
                udt = p.dma("sync", dramf["dump_u_ssm"][:, blk, :],
                            aB[:, u0:u0 + 2048], waits=[tu])
                uu_toks.append(udt)
                p.bread(("usd", blk % 2), udt)
            for tq in range(4):
                w_ = [tu, ld_wx]
                if blk == 0:
                    w_ += p.bwait(tq)
                rows_mm[tq] = p.emit("tensor", lambda h, blk=blk, tq=tq, u0=u0: h.matmul(
                    psum[tq][0:33, 0:512],
                    wb[:, 9216 + blk * 33:9216 + blk * 33 + 33],
                    aB[:, u0 + tq * 512:u0 + tq * 512 + 512],
                    start=(blk == 0), stop=(blk == NDC - 1), skip_group_check=True),
                    waits=w_, inc="pe")
            p.bread(("us", blk % 2), rows_mm[3])
            p.bread(("usd", blk % 2), tdu)
            p.bread(("xd", blk % 2), prev)
        row_toks = []
        for tq in range(4):
            rt = p.emit("scalar", lambda h, tq=tq: h.copy(
                misc[0:33, tq * 512:tq * 512 + 512], psum[tq][0:33, 0:512]),
                waits=[rows_mm[tq]] + p.bwait(("convdead",)), inc="sc")
            p.bread(tq, rt)
            row_toks.append(rt)
        if "ssm_in" in dumps:
            row_toks.append(p.dma("sync", dramf["dump_ssm_in"],
                                  misc[0:64, 0:2048], waits=row_toks))

        sz_toks = []
        for blk in range(NDC):
            slot = blk % 4
            ldt = wload(wb[:, slot * 1024:slot * 1024 + 1024].rearrange(
                "p (a b) -> p a b", a=8),
                dramf["W_inT"][:, :, DI + blk * 128:DI + (blk + 1) * 128],
                waits=p.bwait(("wi", slot)))
            z0 = 8192 + (blk % 2) * 2048
            ev_toks = []
            for tq in range(4):
                bank = 4 + ((blk + tq) % 4)
                for kc in range(NKC):
                    w_ = [ldt]
                    if kc == 0:
                        w_ += p.bwait(bank)
                    mm = p.emit("tensor", lambda h, kc=kc, tq=tq, bank=bank, slot=slot: h.matmul(
                        psum[bank][:, 0:512],
                        wb[:, slot * 1024 + kc * 128:slot * 1024 + kc * 128 + 128],
                        xn2[:, kc, tq * 512:tq * 512 + 512],
                        start=(kc == 0), stop=(kc == NKC - 1), skip_group_check=True),
                        waits=w_, inc="pe" if kc == NKC - 1 else None)
                ev = p.emit("scalar", lambda h, tq=tq, bank=bank, z0=z0: h.activation(
                    aB[:, z0 - 8192 + tq * 512:z0 - 8192 + tq * 512 + 512],
                    psum[bank][:, 0:512], AF.Silu),
                    waits=[mm] + p.bwait(("zs", blk % 2)) +
                    (row_toks + p.bwait(("xd", blk % 2)) if blk < 2 else []), inc="sc")
                p.bread(bank, ev)
                ev_toks.append(ev)
            p.bread(("wi", slot), ev_toks[-1])
            tdz = p.dma("sync", sz_hbm[:, blk, :], aB[:, (blk % 2) * 2048:(blk % 2) * 2048 + 2048],
                        waits=ev_toks)
            p.bread(("zs", blk % 2), tdz)
            sz_toks.append(tdz)

        t_spill = p.dma("sync", x_old_hbm, xres[:], waits=sz_toks + row_toks + uu_toks)

        xf32 = xres.rearrange("p a b -> p (a b)")
        xbf = xres.rearrange("p a b -> p (a b)").bitcast(BF16)
        aA = actA.rearrange("p a b -> p (a b)")
        t_state0 = p.emit("vector", lambda h: h.memset(misc[:, 3456:3712], 0.0),
                          waits=row_toks + [("sc", p.counts["sc"])], inc="dv")
        state_tok = {c: t_state0 for c in range(256)}
        cube_prev = [t_spill, t_state0] + ln2_done
        ssm_y_toks = []
        for tci in range(4):
            t0 = tci * 512
            t_lu = p.dma("sync", aB[:, 0:8192].rearrange("p (a b) -> p a b", a=16),
                         u_hbm[:, :, t0:t0 + 512], waits=cube_prev + p.bwait(("utc",)))
            t_lz = p.dma("sync", aB[:, 8192:16384].rearrange("p (a b) -> p a b", a=16),
                         sz_hbm[:, :, t0:t0 + 512], waits=cube_prev + p.bwait(("sztc",)))
            bc_toks = []
            for which, base in ((1, 0), (17, 8192)):
                for s in range(NS):
                    selr = identf[0:33, which + s:which + s + 1].to_broadcast((33, 128))
                    t_r = p.emit("tensor", lambda h, s=s, selr=selr, t0=t0: h.matmul(
                        psum[6 + s % 2][:, 0:512], selr, misc[0:33, t0:t0 + 512],
                        start=True, stop=True),
                        waits=row_toks + cube_prev + p.bwait(6 + s % 2), inc="pe")
                    t_e = p.emit("scalar", lambda h, s=s, base=base: h.copy(
                        aA[:, base + s * 512:base + s * 512 + 512],
                        psum[6 + s % 2][:, 0:512]),
                        waits=[t_r] + p.bwait(("bc", base, s)), inc="sc")
                    p.bread(6 + s % 2, t_e)
                    bc_toks.append(t_e)
            t_dr = p.emit("tensor", lambda h, t0=t0: h.matmul(
                psum[6][:, 0:512], onesr[:], misc[0:1, t0:t0 + 512],
                start=True, stop=True), waits=row_toks + p.bwait(6), inc="pe")
            t_de = p.emit("scalar", lambda h: h.copy(
                misc[:, 2048:2560], psum[6][:, 0:512]),
                waits=[t_dr] + p.bwait(("dtr",)), inc="sc")
            p.bread(6, t_de)

            wo_mm = {}

            def dprod_a(dc):
                zp = p.emit("vector", lambda h, dc=dc: h.tensor_scalar(
                    fscr[:, 0:512], misc[:, 2048:2560],
                    misc[:, 2640 + dc:2640 + dc + 1], misc[:, 2656 + dc:2656 + dc + 1],
                    ALU.mult, ALU.add),
                    waits=[t_de, t_A] + p.bwait(("zp",)) + p.bwait(("zab",)), inc="dv")
                abw = [zp] + p.bwait(("lns",))
                if tci == 0 and dc == 0:
                    abw = abw + cube_prev + [("pe", p.counts["pe"])]
                ab = p.emit("scalar", lambda h: h.activation(
                    pslab[:, 1536:2048], fscr[:, 0:512], AF.Abs), waits=abw, inc="sc")
                p.bread(("zab",), ab)
                ex = p.emit("scalar", lambda h: h.activation(
                    pslab[:, 1536:2048], pslab[:, 1536:2048], AF.Exp, scale=-1.0),
                    waits=[ab], inc="sc")
                ln_ = p.emit("scalar", lambda h: h.activation(
                    pslab[:, 1536:2048], pslab[:, 1536:2048], AF.Ln,
                    bias=cc(C_ONE)), waits=[ex], inc="sc")
                return zp, ln_

            def dprod_b(dc, zp, ln_):
                dl = p.emit("vector", lambda h: h.scalar_tensor_tensor(
                    misc[:, 2944:3456], fscr[:, 0:512], 0.0, pslab[:, 1536:2048],
                    ALU.max, ALU.add), waits=[ln_] + p.bwait(("dl",)), inc="dv")
                p.bread(("zp",), dl)
                p.bread(("lns",), dl)
                if tci == 0 and dc == 0 and "delta0" in dumps:
                    dtok = p.dma("sync", dramf["dump_delta0"][:, 0:512], misc[:, 2944:3456],
                                 waits=[dl])
                    p.bread(("dl",), dtok)
                tw = p.emit("vector", lambda h, dc=dc: h.tensor_mul(
                    pslab[:, 0:512], misc[:, 2944:3456],
                    aB[:, dc * 512:dc * 512 + 512]), waits=[dl, t_lu] + p.bwait(("wv",)), inc="dv")
                da_toks = []
                for s in range(NS):
                    da_toks.append(p.emit("scalar", lambda h, s=s, dc=dc: h.activation(
                        xf32[:, s * 512:s * 512 + 512], misc[:, 2944:3456], AF.Exp,
                        scale=misc[:, 2688 + dc * 16 + s:2688 + dc * 16 + s + 1]),
                        waits=[dl] + p.bwait(("dA", s)) +
                        (cube_prev if dc == 0 else []), inc="sc"))
                p.bread(("dl",), da_toks[-1])
                x2t = p.emit("vector", lambda h: h.tensor_tensor(
                    xbf[:, 16384:24576].rearrange("p (a b) -> p a b", a=16),
                    pslab[:, 0:512].unsqueeze(1).broadcast_to((128, 16, 512)),
                    aA[:, 0:8192].rearrange("p (a b) -> p a b", a=16),
                    ALU.mult), waits=[tw] + bc_toks, inc="dv")
                p.bread(("wv",), x2t)
                return da_toks, x2t, dl

            zpl = dprod_a(0)
            for dc in range(NDC):
                da_toks, x2t, dl = dprod_b(dc, *zpl)
                if dc + 1 < NDC:
                    zpl = dprod_a(dc + 1)
                sc_t = []
                for s in range(NS):
                    stc = dc * 16 + s
                    sc_t.append(p.emit("vector", lambda h, s=s, stc=stc: h.tensor_tensor_scan(
                        xbf[:, 24576 + s * 512:24576 + s * 512 + 512],
                        xf32[:, s * 512:s * 512 + 512],
                        xbf[:, 16384 + s * 512:16384 + s * 512 + 512],
                        misc[:, 3456 + stc:3456 + stc + 1],
                        ALU.mult, ALU.add),
                        waits=[x2t, da_toks[s], state_tok[stc]] + p.bwait(("H", s)), inc="dv"))
                    p.bread(("dA", s), sc_t[-1])
                st_up = p.emit("vector", lambda h, dc=dc: h.tensor_copy(
                    misc[:, 3456 + dc * 16:3456 + dc * 16 + 16],
                    xbf[:, 24576 + 511:24576 + 511 + 15 * 512 + 1:512] if False else
                    xbf[:, 24576:32768].rearrange("p (a b) -> p a b", a=16)[:, :, 511]),
                    waits=sc_t, inc="dv")
                for s in range(NS):
                    state_tok[dc * 16 + s] = st_up
                ldw = wload(wb[:, 8192 + (dc % 2) * 1024:8192 + (dc % 2) * 1024 + 1024],
                            dramf["W_outT"][:, dc, :], waits=p.bwait(("wo", dc % 2)))
                hct = p.emit("vector", lambda h: h.tensor_tensor(
                    xbf[:, 16384:24576].rearrange("p (a b) -> p a b", a=16),
                    xbf[:, 24576:32768].rearrange("p (a b) -> p a b", a=16),
                    aA[:, 8192:16384].rearrange("p (a b) -> p a b", a=16),
                    ALU.mult), waits=sc_t + [("dv", p.counts["dv"])] + bc_toks, inc="dv")
                tr1 = p.emit("vector", lambda h: h.tensor_add(
                    xbf[:, 24576:28672], xbf[:, 16384:20480], xbf[:, 20480:24576]),
                    waits=[hct], inc="dv")
                tr2 = p.emit("vector", lambda h: h.tensor_add(
                    xbf[:, 16384:18432], xbf[:, 24576:26624], xbf[:, 26624:28672]),
                    waits=[tr1], inc="dv")
                tr3 = p.emit("vector", lambda h: h.tensor_add(
                    xbf[:, 24576:25600], xbf[:, 16384:17408], xbf[:, 17408:18432]),
                    waits=[tr2], inc="dv")
                for s in range(NS):
                    p.bread(("H", s), tr3)
                tr4 = p.emit("vector", lambda h: h.tensor_add(
                    fscr[:, 512:1024], xbf[:, 24576:25088], xbf[:, 25088:25600]),
                    waits=[tr3], inc="dv")
                y1 = p.emit("vector", lambda h, dc=dc: h.scalar_tensor_tensor(
                    pslab[:, 512:1024], aB[:, dc * 512:dc * 512 + 512],
                    misc[:, 2672 + dc:2672 + dc + 1], fscr[:, 512:1024],
                    ALU.mult, ALU.add),
                    waits=[tr4, t_lu], inc="dv")
                yg = p.emit("vector", lambda h, dc=dc: h.tensor_mul(
                    pslab[:, 1024:1536], pslab[:, 512:1024],
                    aB[:, 8192 + dc * 512:8192 + dc * 512 + 512]),
                    waits=[y1, t_lz] + p.bwait(("yg",)), inc="dv")
                for blk in range(NKC):
                    w_ = [ldw, yg]
                    if dc == 0:
                        w_ += p.bwait(blk)
                    wo_mm[blk] = p.emit("tensor", lambda h, dc=dc, blk=blk: h.matmul(
                        psum[blk][:, 0:512],
                        wb[:, 8192 + (dc % 2) * 1024 + blk * 128:8192 + (dc % 2) * 1024 + blk * 128 + 128],
                        pslab[:, 1024:1536],
                        start=(dc == 0), stop=(dc == NDC - 1), skip_group_check=True),
                        waits=w_, inc="pe" if blk == NKC - 1 or dc == NDC - 1 else None)
                p.bread(("wo", dc % 2), wo_mm[NKC - 1])
                p.bread(("yg",), wo_mm[NKC - 1])
            t_lx = p.dma("sync", xf32[:, 0:4096].rearrange("p (a b) -> p a b", a=8),
                         x_old_hbm[:, :, t0:t0 + 512],
                         waits=[t_spill] + [wo_mm[b] for b in range(1)] + p.bwait(("xold",)))
            add_toks = []
            for blk in range(NKC):
                at = p.emit("vector", lambda h, blk=blk: h.tensor_add(
                    xf32[:, 4096 + blk * 512:4096 + blk * 512 + 512],
                    xf32[:, blk * 512:blk * 512 + 512], psum[blk][:, 0:512]),
                    waits=[t_lx, wo_mm[blk]] + p.bwait(("xnew", blk)), inc="dv")
                p.bread(blk, at)
                add_toks.append(at)
            t_sx = p.dma("sync", x_new_hbm[:, :, t0:t0 + 512],
                         xf32[:, 4096:8192].rearrange("p (a b) -> p a b", a=8),
                         waits=add_toks)
            if "y_ssm" in dumps:
                pass
            p.bread(("xold",), add_toks[-1])
            for blk in range(NKC):
                p.bread(("xnew", blk), t_sx)
            p.bread(("utc",), yg)
            p.bread(("sztc",), yg)
            p.bread(("bc", 0, 0), hct)
            for s in range(NS):
                p.bread(("bc", 0, s), hct)
                p.bread(("bc", 8192, s), hct)
            p.bread(("dtr",), dl)
            ssm_y_toks.append(t_sx)
            cube_prev = [t_sx, ("dv", p.counts["dv"]), ("sc", p.counts["sc"])]

        t_lold = p.dma("sync", xres[:], x_old_hbm, waits=cube_prev)
        rec_toks = []
        for tq in range(4):
            t0 = tq * 512
            selm = identf[64:66, 64:65].to_broadcast((2, 128))
            selr = identf[64:66, 65:66].to_broadcast((2, 128))
            tm = p.emit("tensor", lambda h, t0=t0, selm=selm: h.matmul(
                psum[6][:, 0:512], selm, misc[64:66, t0:t0 + 512],
                start=True, stop=True), waits=[tst, t_lold] + p.bwait(6), inc="pe")
            tr_ = p.emit("tensor", lambda h, t0=t0, selr=selr: h.matmul(
                psum[7][:, 0:512], selr, misc[64:66, t0:t0 + 512],
                start=True, stop=True), waits=[tst] + p.bwait(7), inc="pe")
            for kc in range(NKC):
                ts1 = p.emit("vector", lambda h, kc=kc, t0=t0: h.tensor_sub(
                    fscr[:, 0:512], xres[:, kc, t0:t0 + 512], psum[6][:, 0:512]),
                    waits=[tm, t_lold], inc="dv")
                ts2 = p.emit("vector", lambda h, kc=kc, t0=t0: h.tensor_mul(
                    actA[:, kc, t0:t0 + 512], fscr[:, 0:512], psum[7][:, 0:512]),
                    waits=[tr_], inc="dv")
                rec_toks.append(ts2)
            p.bread(6, rec_toks[-1])
            p.bread(7, rec_toks[-1])
        t_rx = p.dma("sync", xres[:], x_new_hbm, waits=rec_toks)
        if "x2" in dumps and False:
            pass
        ssm_done = [t_rx, ("dv", p.counts["dv"]), ("sc", p.counts["sc"])]
        if "y_ssm" in dumps:
            ssm_done = ssm_done + [p.dma("sync", dumps["y_ssm"], xres[:], waits=ssm_done)]
        if "xn2b" in dumps:
            ssm_done = ssm_done + [p.dma("sync", dumps["xn2b"], actA[:], waits=ssm_done)]

        aAx = actA.rearrange("p a b -> p (a b)")

        xh_toks = []
        sq_prev = {}
        for tq in range(4):
            t0 = tq * 512
            for kc in range(NKC):
                par = kc % 2
                w_ = ssm_done + ([sq_prev[par]] if par in sq_prev else [])
                sqt = p.emit("scalar", lambda h, kc=kc, t0=t0, par=par: h.activation(
                    pslab[:, par * 512:par * 512 + 512], actA[:, kc, t0:t0 + 512],
                    AF.Square), waits=w_, inc="sc")
                mmt = p.emit("tensor", lambda h, kc=kc, par=par, tq=tq: h.matmul(
                    psum[4][0:1, tq * 128:tq * 128 + 128] if False else
                    psum[4][0:1, 0:512], onesb[:, 0:1], pslab[:, par * 512:par * 512 + 512],
                    start=(kc == 0), stop=(kc == NKC - 1), skip_group_check=True),
                    waits=[sqt] + (p.bwait(4) if kc == 0 else []), inc="pe")
                sq_prev[par] = mmt
            r1 = p.emit("scalar", lambda h, t0=t0: h.activation(
                fscr[0:1, 0:512], psum[4][0:1, 0:512], AF.Sqrt), waits=[mmt], inc="sc")
            p.bread(4, r1)
            r2 = p.emit("vector", lambda h: h.tensor_scalar(
                fscr[0:1, 0:512], fscr[0:1, 0:512], 1e-12, None, ALU.max),
                waits=[r1], inc="dv")
            r3 = p.emit("vector", lambda h: h.reciprocal(
                fscr[0:1, 512:1024], fscr[0:1, 0:512]), waits=[r2], inc="dv")
            r4 = p.emit("vector", lambda h, t0=t0: h.tensor_scalar(
                misc[0:1, 3584 + t0 // 4 * 0 + 0:3584 + 512] if False else
                fscr[0:1, 512:1024], fscr[0:1, 512:1024], cc(C_SQRTD)[0:1, :], None,
                ALU.mult), waits=[r3], inc="dv")
            rb = p.emit("tensor", lambda h: h.matmul(
                psum[5][:, 0:512], onesr[:], fscr[0:1, 512:1024],
                start=True, stop=True), waits=[r4] + p.bwait(5), inc="pe")
            last = None
            for kc in range(NKC):
                last = p.emit("vector", lambda h, kc=kc, t0=t0: h.tensor_mul(
                    actB[:, kc, t0:t0 + 512], actA[:, kc, t0:t0 + 512],
                    psum[5][:, 0:512]), waits=[rb], inc="dv")
                xh_toks.append(last)
            p.bread(5, last)

        t_rt = p.dma("sync", misc[:, 3968:3976], dramf["q_rt"], waits=ssm_done)
        if "dbg1" in dumps:
            t_rt = p.dma("sync", dumps["dbg1"], misc[:, 3968:3976], waits=[t_rt] + ssm_done)
        if "dbg2" in dumps:
            t_rt = p.dma("sync", dumps["dbg2"], consts[:, :], waits=[t_rt])
        t_rtb = p.emit("vector", lambda h: h.tensor_copy(
            pslab[:, 1024:1032], misc[:, 3968:3976]), waits=[t_rt] + ssm_done, inc="dv")
        if "srow" in dumps:
            dtk = p.dma("sync", dramf["dump_srow"][2:3, 0:1024], misc[:, 3968:3976],
                        waits=[t_rt] + ssm_done)
            t_rtb = p.emit("vector", lambda h: h.tensor_copy(
                pslab[:, 1024:1032], misc[:, 3968:3976]), waits=[dtk], inc="dv")
        smm = None
        s_evs = []
        for tq in range(4):
            for kc in range(NKC):
                smm = p.emit("tensor", lambda h, tq=tq, kc=kc: h.matmul(
                    psum[6][0:1, 0:512],
                    pslab[:, 1024 + kc:1024 + kc + 1],
                    actA[:, kc, tq * 512:tq * 512 + 512],
                    start=(kc == 0), stop=(kc == NKC - 1), skip_group_check=True),
                    waits=[t_rtb] + ssm_done + (p.bwait(6) if kc == 0 else []), inc="pe")
            sev = p.emit("scalar", lambda h, tq=tq: h.copy(
                misc[0:1, 2048 + tq * 512:2048 + tq * 512 + 512], psum[6][0:1, 0:512]),
                waits=[smm], inc="sc")
            p.bread(6, sev)
            s_evs.append(sev)
        s_done = s_evs
        if "dbg3" in dumps:
            s_done = s_done + [p.dma("sync", dramf["dump_dbg3"][0:1, :],
                                     misc[0:1, 2048:4096], waits=s_done)]
        t_s16 = p.dma("sync", misc[0:16, 1024:1152], misc[0:1, 2048:4096], waits=s_done)
        t_smax = p.emit("vector", lambda h: h.reduce_max(
            fscr[0:1, 144:145], misc[0:1, 2048:4096], axis=AX.X), waits=s_done, inc="dv")
        t_na = p.emit("vector", lambda h: h.memset(
            fscr[0:1, 145:146],
            float(np.log(2048.0) - np.log(np.float32(NQ_SEL * 9.0 / 8.0)))),
            waits=s_done, inc="dv")
        tm0 = p.emit("vector", lambda h: h.tensor_tensor(
            fscr[0:1, 146:147], fscr[0:1, 144:145], fscr[0:1, 145:146],
            ALU.min), waits=[t_smax, t_na], inc="dv")
        tm0b = p.emit("vector", lambda h: h.tensor_scalar(
            fscr[0:1, 147:148], fscr[0:1, 146:147], -1.0, None, ALU.mult),
            waits=[tm0], inc="dv")
        trp0 = p.emit("tensor", lambda h: h.matmul(
            psum[7][0:16, 0:3], onesr[0:1, 0:16], fscr[0:1, 145:148],
            start=True, stop=True), waits=[tm0b] + p.bwait(7), inc="pe")
        tre0 = p.emit("scalar", lambda h: h.copy(
            fscr[0:16, 152:155], psum[7][0:16, 0:3]), waits=[trp0], inc="sc")
        p.bread(7, tre0)

        t_gl = p.dma("sync", fscr[:, 8:16], dramf["gamma_light"], waits=ssm_done)
        t_gh = p.dma("sync", fscr[:, 16:24], dramf["gamma_heavy"], waits=ssm_done)
        qk_toks = []
        vl_toks = []
        lql = wload(wb[:, 0:4096].rearrange("p (a b) -> p a b", a=8),
                    dramf["wq_light"], waits=ssm_done + p.bwait(("lw", 0)))
        wsc_toks = []
        for kc in range(NKC):
            wsc_toks.append(p.emit("vector", lambda h, kc=kc: h.tensor_scalar(
                wb[:, kc * 512:kc * 512 + 512], wb[:, kc * 512:kc * 512 + 512],
                fscr[:, 8 + kc:8 + kc + 1], 1.0 / 8.0, ALU.mult, ALU.mult),
                waits=[lql, t_gl], inc="dv"))
        for blk in range(4):
            for tq in range(4):
                bank = (blk + tq) % 4
                for kc in range(NKC):
                    w_ = wsc_toks + xh_toks
                    if kc == 0:
                        w_ = w_ + p.bwait(bank)
                    mmq = p.emit("tensor", lambda h, blk=blk, kc=kc, tq=tq, bank=bank: h.matmul(
                        psum[bank][:, 0:512],
                        wb[:, kc * 512 + blk * 128:kc * 512 + blk * 128 + 128],
                        actB[:, kc, tq * 512:tq * 512 + 512],
                        start=(kc == 0), stop=(kc == NKC - 1), skip_group_check=True),
                        waits=w_, inc="pe" if kc == NKC - 1 else None)
                ev = p.emit("scalar", lambda h, blk=blk, tq=tq, bank=bank: h.copy(
                    aC[:, blk * 2048 + tq * 512:blk * 2048 + tq * 512 + 512],
                    psum[bank][:, 0:512]), waits=[mmq], inc="sc")
                p.bread(bank, ev)
                qk_toks.append(ev)
        p.bread(("lw", 0), qk_toks[-1])
        lkv = wload(wb[:, 0:8192].rearrange("p (a b) -> p a b", a=8),
                    dramf["wkv_light"], waits=p.bwait(("lw", 0)) + ssm_done)
        wsc2 = []
        for kc in range(NKC):
            wsc2.append(p.emit("vector", lambda h, kc=kc: h.tensor_scalar(
                wb[:, kc * 1024:kc * 1024 + 1024], wb[:, kc * 1024:kc * 1024 + 1024],
                fscr[:, 8 + kc:8 + kc + 1], None, ALU.mult),
                waits=[lkv, t_gl], inc="dv"))
        for blk in range(4):
            for tq in range(4):
                bank = (blk + tq) % 4
                for kc in range(NKC):
                    w_ = wsc2 + xh_toks
                    if kc == 0:
                        w_ = w_ + p.bwait(bank)
                    mmk = p.emit("tensor", lambda h, blk=blk, kc=kc, tq=tq, bank=bank: h.matmul(
                        psum[bank][:, 0:512],
                        wb[:, kc * 1024 + blk * 128:kc * 1024 + blk * 128 + 128],
                        actB[:, kc, tq * 512:tq * 512 + 512],
                        start=(kc == 0), stop=(kc == NKC - 1), skip_group_check=True),
                        waits=w_, inc="pe" if kc == NKC - 1 else None)
                ev = p.emit("scalar", lambda h, blk=blk, tq=tq, bank=bank: h.copy(
                    aC[:, 8192 + blk * 2048 + tq * 512:8192 + blk * 2048 + tq * 512 + 512],
                    psum[bank][:, 0:512]), waits=[mmk], inc="sc")
                p.bread(bank, ev)
                qk_toks.append(ev)
        for tt in range(16):
            bank = 4 + tt % 2
            for kc in range(NKC):
                w_ = wsc2 + xh_toks + s_done + [t_s16]
                if kc == 0:
                    w_ = w_ + p.bwait(bank)
                mmv = p.emit("tensor", lambda h, tt=tt, kc=kc, bank=bank: h.matmul(
                    psum[bank][:, 0:512],
                    actB[:, kc, tt * 128:tt * 128 + 128],
                    wb[:, kc * 1024 + 512:kc * 1024 + 1024],
                    start=(kc == 0), stop=(kc == NKC - 1), skip_group_check=True),
                    waits=w_, inc="pe" if kc == NKC - 1 else None)
            evv = p.emit("scalar", lambda h, tt=tt, bank=bank: h.copy(
                aAx[:, tt * 512:tt * 512 + 512], psum[bank][:, 0:512]),
                waits=[mmv], inc="sc")
            p.bread(bank, evv)
            vl_toks.append(evv)
        p.bread(("lw", 0), vl_toks[-1])
        p.bread(("lw", 1), vl_toks[-1])

        t_sxr = p.emit("tensor", lambda h: h.matmul(
            psum[7][0:16, 5:6], onesr[0:1, 0:16], fscr[0:1, 144:145],
            start=True, stop=True), waits=[t_smax] + p.bwait((7, "c")), inc="pe")
        t_sxc = p.emit("scalar", lambda h: h.copy(
            fscr[0:16, 150:151], psum[7][0:16, 5:6]), waits=[t_sxr], inc="sc")
        p.bread((7, "c"), t_sxc)
        light_o_toks = []
        coor_last = [tre0, t_sxc]
        for it in range(50):
            if it > 0:
                ta_ = p.emit("vector", lambda h: h.tensor_scalar(
                    misc[0:16, 1152:1280], misc[0:16, 1024:1152],
                    fscr[0:16, 152:153], None, ALU.min),
                    waits=coor_last + [t_s16], inc="dv")
                te_ = p.emit("scalar", lambda h: h.activation(
                    misc[0:16, 1280:1408], misc[0:16, 1152:1280], AF.Exp,
                    bias=fscr[0:16, 154:155], accum_out=fscr[0:16, 155:156]),
                    waits=[ta_], inc="sc")
                tps = p.emit("tensor", lambda h: h.matmul(
                    psum[7][0:16, 4:5],
                    consts[0:16, C_ONE:C_ONE + 1].to_broadcast((16, 16)),
                    fscr[0:16, 155:156], start=True, stop=True),
                    waits=[te_] + p.bwait((7, "b")), inc="pe")
                tln = p.emit("scalar", lambda h: h.activation(
                    fscr[0:16, 149:150], psum[7][0:16, 4:5], AF.Ln), waits=[tps], inc="sc")
                p.bread((7, "b"), tln)
                t12 = p.emit("vector", lambda h: h.scalar_tensor_tensor(
                    fscr[0:16, 152:153], fscr[0:16, 153:154], fscr[0:16, 149:150],
                    consts[0:16, C_LOGK:C_LOGK + 1], ALU.add, ALU.subtract),
                    waits=[tln], inc="dv")
                t3 = p.emit("vector", lambda h: h.tensor_tensor(
                    fscr[0:16, 153:154], fscr[0:16, 150:151], fscr[0:16, 152:153],
                    ALU.min), waits=[t12], inc="dv")
                coor_last = [p.emit("vector", lambda h: h.tensor_scalar(
                    fscr[0:16, 154:155], fscr[0:16, 153:154], -1.0, None, ALU.mult),
                    waits=[t3], inc="dv")]
            if it < 16:
                wn = it
                wt0 = wn * 128
                for hh in range(H1):
                    hp0 = (hh % 2) * 64
                    hblk = hh // 2
                    g2 = (it * H1 + hh) % 4
                    sb_ = g2
                    fc = 24 + g2 * 4
                    t_s = p.emit("tensor", lambda h, hp0=hp0, hblk=hblk, wt0=wt0, sb_=sb_: h.matmul(
                        psum[sb_][:, 0:128],
                        aC[hp0:hp0 + 64, hblk * 2048 + wt0:hblk * 2048 + wt0 + 128],
                        aC[hp0:hp0 + 64, 8192 + hblk * 2048 + wt0:8192 + hblk * 2048 + wt0 + 128],
                        start=True, stop=True), waits=qk_toks + p.bwait(sb_), inc="pe")
                    t_mx = p.emit("vector", lambda h, sb_=sb_, fc=fc: h.reduce_max(
                        fscr[:, fc:fc + 1], psum[sb_][:, 0:128], axis=AX.X, negate=True),
                        waits=[t_s] + p.bwait(("lmx", g2)), inc="dv")
                    t_ex = p.emit("scalar", lambda h, sb_=sb_, fc=fc, g2=g2: h.activation(
                        pslab[:, g2 * 128:g2 * 128 + 128],
                        psum[sb_][:, 0:128], AF.Exp,
                        bias=fscr[:, fc:fc + 1], accum_out=fscr[:, fc + 1:fc + 2]),
                        waits=[t_mx] + p.bwait(("lps", g2)), inc="sc")
                    p.bread(sb_, t_ex)
                    p.bread(("lmx", g2), t_ex)
                    t_rc = p.emit("vector", lambda h, fc=fc: h.reciprocal(
                        fscr[:, fc + 2:fc + 3], fscr[:, fc + 1:fc + 2]),
                        waits=[t_ex] + p.bwait(("lrc", g2)), inc="dv")
                    t_tp = p.emit("tensor", lambda h, sb_=sb_, g2=g2: h.transpose(
                        psbf[sb_][:, 256:384], pslab[:, g2 * 128:g2 * 128 + 128],
                        ident[:]), waits=[t_ex] + p.bwait((sb_, "pt")), inc="pe")
                    t_pe = p.emit("scalar", lambda h, sb_=sb_, g2=g2: h.copy(
                        pslab[:, 512 + g2 * 128:512 + g2 * 128 + 128],
                        psbf[sb_][:, 256:384]), waits=[t_tp] + p.bwait(("lpt", g2)), inc="sc")
                    p.bread((sb_, "pt"), t_pe)
                    p.bread(("lps", g2), t_pe)
                    t_av = p.emit("tensor", lambda h, wn=wn, hh=hh, sb_=sb_, g2=g2: h.matmul(
                        psum[sb_][:, 128:192],
                        pslab[:, 512 + g2 * 128:512 + g2 * 128 + 128],
                        aAx[:, wn * 512 + hh * 64:wn * 512 + hh * 64 + 64],
                        start=True, stop=True), waits=[t_pe] + vl_toks, inc="pe")
                    p.bread(("lpt", g2), t_av)
                    t_oe = p.emit("scalar", lambda h, sb_=sb_, fc=fc, g2=g2: h.activation(
                        pslab[:, 1024 + g2 * 64:1024 + g2 * 64 + 64],
                        psum[sb_][:, 128:192], AF.Copy,
                        scale=fscr[:, fc + 2:fc + 3]),
                        waits=[t_av, t_rc] + p.bwait(("lo", g2)), inc="sc")
                    p.bread(("lrc", g2), t_oe)
                    t_oc2 = p.emit("vector", lambda h, wn=wn, hh=hh, g2=g2: h.tensor_copy(
                        aAx[:, 8192 + wn * 512 + hh * 64:8192 + wn * 512 + hh * 64 + 64],
                        pslab[:, 1024 + g2 * 64:1024 + g2 * 64 + 64]),
                        waits=[t_oe], inc="dv")
                    p.bread(("lo", g2), t_oc2)
                    light_o_toks.append(t_oc2)
        coor_done = coor_last
        light_done = light_o_toks[-1:]

        lf = [("sc", p.counts["sc"]), ("dv", p.counts["dv"])]

        kx = p.emit("vector", lambda h: h.tensor_scalar(
            misc[0:1, 0:2048], misc[0:1, 2048:4096], fscr[0:1, 152:153], 0.0,
            ALU.subtract, ALU.min), waits=coor_done + s_done, inc="dv")
        t_kT = None
        for c_ in range(16):
            tt_ = p.emit("tensor", lambda h, c_=c_: h.transpose(
                psum[5][:, 0:1], misc[0:1, c_ * 128:c_ * 128 + 128],
                identf[0:1, 0:1]), waits=[kx] + p.bwait(5), inc="pe")
            t_kT = p.emit("scalar", lambda h, c_=c_: h.copy(
                fscr[:, 32 + c_:32 + c_ + 1], psum[5][:, 0:1]), waits=[tt_], inc="sc")
            p.bread(5, t_kT)
        if "dbg3" in dumps:
            kx = p.dma("sync", dramf["dump_dbg3"][1:2, :], misc[0:1, 0:2048], waits=[kx])
        zz_ = p.emit("vector", lambda h: h.memset(fscr[0:1, 512:1024], 0.0),
                     waits=[kx], inc="dv")
        pz_ = p.emit("vector", lambda h: h.memset(misc[0:1, 2048:2049], 0.0),
                     waits=[kx, t_kT] + s_done + [t_s16, t_smax], inc="dv")
        prev_pref = pz_
        for ch in range(4):
            c0 = ch * 512
            n = 512 if ch < 3 else 511
            satc = p.emit("vector", lambda h, c0=c0: h.tensor_scalar(
                fscr[0:1, 0:512], misc[0:1, c0:c0 + 512], 0.0, None, ALU.is_ge),
                waits=[prev_pref], inc="dv")
            prev_pref = p.emit("vector", lambda h, c0=c0, n=n: h.tensor_tensor_scan(
                misc[0:1, 2048 + c0 + 1:2048 + c0 + 1 + n],
                fscr[0:1, 512:512 + n], fscr[0:1, 0:n],
                misc[0:1, 2048 + c0:2048 + c0 + 1],
                ALU.add, ALU.add), waits=[satc, zz_], inc="dv")
        if "dbg3" in dumps:
            prev_pref = p.dma("sync", dramf["dump_dbg3"][2:3, :], misc[0:1, 2048:4096],
                              waits=[prev_pref])
        t_pT = None
        for c_ in range(16):
            tt_ = p.emit("tensor", lambda h, c_=c_: h.transpose(
                psum[5][:, 0:1], misc[0:1, 2048 + c_ * 128:2048 + c_ * 128 + 128],
                identf[0:1, 0:1]), waits=[prev_pref] + p.bwait(5), inc="pe")
            t_pT = p.emit("scalar", lambda h, c_=c_: h.copy(
                fscr[:, 48 + c_:48 + c_ + 1], psum[5][:, 0:1]), waits=[tt_], inc="sc")
            p.bread(5, t_pT)
        cnt_toks = []
        for itile in range(16):
            acc = None
            for ch in range(4):
                t_r = p.emit("tensor", lambda h, ch=ch: h.matmul(
                    psum[4][:, 0:512], onesr[:], misc[0:1, ch * 512:ch * 512 + 512],
                    start=True, stop=True), waits=[kx] + p.bwait(4), inc="pe")
                t_c = p.emit("vector", lambda h, itile=itile, ch=ch: h.tensor_scalar(
                    fscr[:, 64 + ch * 512 // 4:64 + ch * 512 // 4] if False else
                    pslab[:, 0:512].bitcast(F32)[:, 0:256] if False else
                    fscr[:, 512:1024][:, ch * 128:ch * 128 + 128] if False else
                    fscr[:, 512:1024],
                    psum[4][:, 0:512], fscr[:, 32 + itile:32 + itile + 1], None,
                    ALU.is_gt), waits=[t_r, t_kT], inc="dv")
                p.bread(4, t_c)
                acc = p.emit("vector", lambda h, itile=itile, ch=ch: h.reduce_sum(
                    fscr[:, 64 + itile:64 + itile + 1] if ch == 0 else
                    fscr[:, 80 + itile:80 + itile + 1],
                    fscr[:, 512:1024], axis=AX.X), waits=[t_c], inc="dv")
                if ch > 0:
                    acc = p.emit("vector", lambda h, itile=itile: h.tensor_add(
                        fscr[:, 64 + itile:64 + itile + 1],
                        fscr[:, 64 + itile:64 + itile + 1],
                        fscr[:, 80 + itile:80 + itile + 1]), waits=[acc], inc="dv")
            eqi = p.emit("vector", lambda h, itile=itile: h.tensor_scalar(
                fscr[:, 96 + itile:96 + itile + 1] if False else fscr[:, 80 + itile:80 + itile + 1],
                fscr[:, 32 + itile:32 + itile + 1], 0.0, None, ALU.is_ge),
                waits=[acc, t_kT], inc="dv")
            tot = p.emit("vector", lambda h, itile=itile: h.scalar_tensor_tensor(
                fscr[:, 64 + itile:64 + itile + 1],
                fscr[:, 80 + itile:80 + itile + 1],
                fscr[:, 48 + itile:48 + itile + 1],
                fscr[:, 64 + itile:64 + itile + 1], ALU.mult, ALU.add),
                waits=[eqi, t_pT], inc="dv")
            msk = p.emit("vector", lambda h, itile=itile: h.tensor_scalar(
                fscr[:, 96 + itile:96 + itile + 1],
                fscr[:, 64 + itile:64 + itile + 1], float(NQ_SEL), None, ALU.is_lt),
                waits=[tot], inc="dv")
            cnt_toks.append(msk)
        if "mask_q" in dumps:
            cnt_toks.append(p.dma("sync", dumps["mask_q"], fscr[:, 96:112], waits=cnt_toks))
        if "route" in dumps:
            cnt_toks.append(p.dma("sync", dumps["route"], fscr[:, 0:144], waits=cnt_toks))
        if "srow" in dumps:
            cnt_toks.append(p.dma("sync", dramf["dump_srow"][0:1, :], misc[0:1, 2048:4096],
                                  waits=[prev_pref]))
            cnt_toks.append(p.dma("sync", dramf["dump_srow"][1:2, :], misc[0:1, 0:2048],
                                  waits=[kx] + cnt_toks))
        if "light_o" in dumps:
            cnt_toks.append(p.dma("sync",
                dramf["dump_light_o"].rearrange("p a b -> p (a b)"),
                aAx[:, 8192:16384], waits=light_o_toks))

        lo_fm_toks = []
        for tt in range(16):
            for blk in range(4):
                t_t = p.emit("tensor", lambda h, tt=tt, blk=blk: h.transpose(
                    psbf[4][:, 0:128],
                    aAx[:, 8192 + tt * 512 + blk * 128:8192 + tt * 512 + blk * 128 + 128],
                    ident[:]), waits=light_done + lf + p.bwait(4), inc="pe")
                ev = p.emit("scalar", lambda h, tt=tt, blk=blk: h.copy(
                    aAx[:, blk * 2048 + tt * 128:blk * 2048 + tt * 128 + 128],
                    psbf[4][:, 0:128]), waits=[t_t], inc="sc")
                p.bread(4, ev)
                lo_fm_toks.append(ev)
        ld_wol = wload(wb[:, 0:4096].rearrange("p (a b) -> p a b", a=4),
                       dramf["wo_light"], waits=lf + p.bwait(("lw", 0)))
        wol_adds = []
        for tq in range(4):
            last = {}
            for blk in range(NKC):
                for hk in range(4):
                    w_ = [ld_wol] + lo_fm_toks
                    if hk == 0:
                        w_ = w_ + p.bwait(blk)
                    last[blk] = p.emit("tensor", lambda h, blk=blk, hk=hk, tq=tq: h.matmul(
                        psum[blk][:, 0:512],
                        wb[:, hk * 1024 + blk * 128:hk * 1024 + blk * 128 + 128],
                        aAx[:, hk * 2048 + tq * 512:hk * 2048 + tq * 512 + 512],
                        start=(hk == 0), stop=(hk == 3), skip_group_check=True),
                        waits=w_, inc="pe" if hk == 3 else None)
                at = p.emit("vector", lambda h, blk=blk, tq=tq: h.tensor_add(
                    xres[:, blk, tq * 512:tq * 512 + 512],
                    xres[:, blk, tq * 512:tq * 512 + 512],
                    psum[blk][:, 0:512]), waits=[last[blk]], inc="dv")
                p.bread(blk, at)
                wol_adds.append(at)
        p.bread(("lw", 0), wol_adds[-1])

        hv_toks = []
        for wname, gscale, dst, tok_major in (
                ("wq_heavy", "q", 0, False), ("WkT_heavy", "k", 8192, False),
                ("WvT_heavy", "v", None, True)):
            ldh = wload(wb[:, 4096:8192].rearrange("p (a b) -> p a b", a=8),
                        dramf[wname], waits=lf + p.bwait(("hw",)))
            sc_toks2 = []
            for kc in range(NKC):
                if gscale == "q":
                    tsw = p.emit("vector", lambda h, kc=kc: h.tensor_scalar(
                        wb[:, 4096 + kc * 512:4096 + kc * 512 + 512],
                        wb[:, 4096 + kc * 512:4096 + kc * 512 + 512],
                        fscr[:, 16 + kc:16 + kc + 1], 1.0 / 8.0, ALU.mult, ALU.mult),
                        waits=[ldh, t_gh], inc="dv")
                else:
                    tsw = p.emit("vector", lambda h, kc=kc: h.tensor_scalar(
                        wb[:, 4096 + kc * 512:4096 + kc * 512 + 512],
                        wb[:, 4096 + kc * 512:4096 + kc * 512 + 512],
                        fscr[:, 16 + kc:16 + kc + 1], None, ALU.mult),
                        waits=[ldh, t_gh], inc="dv")
                sc_toks2.append(tsw)
            this = []
            if not tok_major:
                for blk in range(4):
                    for tq in range(4):
                        bank = 4 + (blk + tq) % 2
                        for kc in range(NKC):
                            w_ = sc_toks2 + xh_toks
                            if kc == 0:
                                w_ = w_ + p.bwait(bank)
                            mmq = p.emit("tensor", lambda h, blk=blk, kc=kc, tq=tq, bank=bank: h.matmul(
                                psum[bank][:, 0:512],
                                wb[:, 4096 + kc * 512 + blk * 128:4096 + kc * 512 + blk * 128 + 128],
                                actB[:, kc, tq * 512:tq * 512 + 512],
                                start=(kc == 0), stop=(kc == NKC - 1), skip_group_check=True),
                                waits=w_, inc="pe" if kc == NKC - 1 else None)
                        ev = p.emit("scalar", lambda h, dst=dst, blk=blk, tq=tq, bank=bank: h.copy(
                            aC[:, dst + blk * 2048 + tq * 512:dst + blk * 2048 + tq * 512 + 512],
                            psum[bank][:, 0:512]), waits=[mmq] + (wol_adds if dst == 0 else []),
                            inc="sc")
                        p.bread(bank, ev)
                        this.append(ev)
            else:
                for tt in range(16):
                    bank = 4 + tt % 2
                    for kc in range(NKC):
                        w_ = sc_toks2 + xh_toks
                        if kc == 0:
                            w_ = w_ + p.bwait(bank)
                        mmv = p.emit("tensor", lambda h, tt=tt, kc=kc, bank=bank: h.matmul(
                            psum[bank][:, 0:512],
                            actB[:, kc, tt * 128:tt * 128 + 128],
                            wb[:, 4096 + kc * 512:4096 + kc * 512 + 512],
                            start=(kc == 0), stop=(kc == NKC - 1), skip_group_check=True),
                            waits=w_, inc="pe" if kc == NKC - 1 else None)
                    evv = p.emit("scalar", lambda h, tt=tt, bank=bank: h.copy(
                        aAx[:, 8192 + tt * 512:8192 + tt * 512 + 512], psum[bank][:, 0:512]),
                        waits=[mmv] + lo_fm_toks, inc="sc")
                    p.bread(bank, evv)
                    this.append(evv)
            p.bread(("hw",), this[-1])
            hv_toks += this

        def hP_ap(gp_, c0, c1):
            return pslab[:, c0:c1] if gp_ == 0 else actB[:, 0, c0:c1]

        def hT_ap(gp_, c0, c1):
            return actB[:, 1 + gp_, c0:c1]

        ho_toks = []
        hgrp = {}

        def hfront(gi):
            hh = gi // 16
            qt = gi % 16
            gp_ = gi % 2
            b0 = gp_ * 4
            hp0 = (hh % 2) * 64
            hblk = hh // 2
            smm_t = []
            for ch in range(4):
                w_ = hv_toks + p.bwait(b0 + ch)
                smm_t.append(p.emit("tensor", lambda h, ch=ch, hp0=hp0, hblk=hblk, qt=qt, b0=b0: h.matmul(
                    psum[b0 + ch][:, 0:512],
                    aC[hp0:hp0 + 64, hblk * 2048 + qt * 128:hblk * 2048 + qt * 128 + 128],
                    aC[hp0:hp0 + 64, 8192 + hblk * 2048 + ch * 512:8192 + hblk * 2048 + ch * 512 + 512],
                    start=True, stop=True), waits=w_, inc="pe"))
            mx_t = []
            for ch in range(4):
                mx_t.append(p.emit("vector", lambda h, ch=ch, b0=b0, gp_=gp_: h.reduce_max(
                    fscr[:, 112 + gp_ * 8 + ch:112 + gp_ * 8 + ch + 1],
                    psum[b0 + ch][:, 0:512], axis=AX.X, negate=True),
                    waits=[smm_t[ch]], inc="dv"))
            mxf = p.emit("vector", lambda h, gp_=gp_: h.tensor_reduce(
                fscr[:, 128 + gp_:128 + gp_ + 1], fscr[:, 112 + gp_ * 8:112 + gp_ * 8 + 4],
                AX.X, ALU.min), waits=mx_t + p.bwait(("hmxf", gp_)), inc="dv")
            ex_t = []
            for ch in range(4):
                ex_t.append(p.emit("scalar", lambda h, ch=ch, b0=b0, gp_=gp_: h.activation(
                    hP_ap(gp_, ch * 512, ch * 512 + 512), psum[b0 + ch][:, 0:512], AF.Exp,
                    bias=fscr[:, 128 + gp_:128 + gp_ + 1],
                    accum_out=fscr[:, 130 + gp_ * 4 + ch:130 + gp_ * 4 + ch + 1]),
                    waits=[mxf] + p.bwait(("hP", gp_, ch)) +
                    (p.bwait(("hac", gp_)) if ch == 0 else []), inc="sc"))
                p.bread(b0 + ch, ex_t[-1])
            p.bread(("hmxf", gp_), ex_t[-1])
            sm1 = p.emit("vector", lambda h, gp_=gp_: h.reduce_sum(
                fscr[:, 138 + gp_:138 + gp_ + 1], fscr[:, 130 + gp_ * 4:130 + gp_ * 4 + 4],
                axis=AX.X), waits=ex_t, inc="dv")
            p.bread(("hac", gp_), sm1)
            rc1 = p.emit("vector", lambda h, gp_=gp_: h.reciprocal(
                fscr[:, 140 + gp_:140 + gp_ + 1], fscr[:, 138 + gp_:138 + gp_ + 1]),
                waits=[sm1], inc="dv")
            scl = p.emit("vector", lambda h, qt=qt, gp_=gp_: h.tensor_mul(
                fscr[:, 142 + gp_:142 + gp_ + 1], fscr[:, 140 + gp_:140 + gp_ + 1],
                fscr[:, 96 + qt:96 + qt + 1]),
                waits=[rc1] + cnt_toks + p.bwait(("hscl", gp_)), inc="dv")
            hgrp[gi] = (ex_t, scl)

        def hback(gi):
            hh = gi // 16
            qt = gi % 16
            gp_ = gi % 2
            b0 = gp_ * 4
            ex_t, scl = hgrp.pop(gi)
            ev_half = []
            for half in range(2):
                tts = []
                for j in range(8):
                    kvt = half * 8 + j
                    w_ = [ex_t[kvt // 4]]
                    if j == 0:
                        w_ = w_ + p.bwait(b0 + half)
                    tts.append(p.emit("tensor", lambda h, j=j, kvt=kvt, b0=b0, half=half, gp_=gp_: h.transpose(
                        psbf[b0 + half][:, j * 128:j * 128 + 128],
                        hP_ap(gp_, kvt * 128, kvt * 128 + 128), ident[:]),
                        waits=w_, inc="pe"))
                for ch in (2 * half, 2 * half + 1):
                    p.bread(("hP", gp_, ch), tts[-1])
                ev = p.emit("vector", lambda h, b0=b0, half=half, gp_=gp_: h.tensor_copy(
                    hT_ap(gp_, half * 1024, half * 1024 + 1024),
                    psbf[b0 + half][:, 0:1024]),
                    waits=[tts[-1]] + p.bwait(("hT", gp_, half)), inc="dv")
                p.bread(b0 + half, ev)
                ev_half.append(ev)
            av_last = None
            for kvt in range(16):
                half = kvt // 8
                w_ = [ev_half[half]]
                if kvt == 0:
                    w_ = w_ + p.bwait(b0 + 2)
                av_last = p.emit("tensor", lambda h, kvt=kvt, hh=hh, b0=b0, gp_=gp_: h.matmul(
                    psum[b0 + 2][:, 0:64],
                    hT_ap(gp_, kvt * 128, kvt * 128 + 128),
                    aAx[:, 8192 + kvt * 512 + hh * 64:8192 + kvt * 512 + hh * 64 + 64],
                    start=(kvt == 0), stop=(kvt == 15), skip_group_check=True),
                    waits=w_, inc="pe" if kvt in (7, 15) else None)
                if kvt == 7:
                    p.bread(("hT", gp_, 0), av_last)
            p.bread(("hT", gp_, 1), av_last)
            t_oe = p.emit("scalar", lambda h, qt=qt, hh=hh, b0=b0, gp_=gp_: h.activation(
                wb[:, qt * 512 + hh * 64:qt * 512 + hh * 64 + 64],
                psum[b0 + 2][:, 0:64], AF.Copy, scale=fscr[:, 142 + gp_:142 + gp_ + 1]),
                waits=[av_last, scl] + (p.bwait(("lw", 0)) if gi == 0 else []) +
                (p.bwait(("lw", 1)) if gi == 0 else []), inc="sc")
            p.bread(("hscl", gp_), t_oe)
            p.bread(b0 + 2, t_oe)
            ho_toks.append(t_oe)

        for gi in range(129):
            if gi < 128:
                hfront(gi)
            if gi >= 1:
                hback(gi - 1)

        ho_fm_toks = []
        for blk in range(4):
            for tg in range(2):
                bank = 4 + (blk * 2 + tg) % 2
                tts = []
                for j in range(8):
                    tt = tg * 8 + j
                    w_ = ho_toks if (blk == 0 and tg == 0 and j == 0) else []
                    if j == 0:
                        w_ = w_ + p.bwait(bank)
                    tts.append(p.emit("tensor", lambda h, tt=tt, blk=blk, j=j, bank=bank: h.transpose(
                        psbf[bank][:, j * 128:j * 128 + 128],
                        wb[:, tt * 512 + blk * 128:tt * 512 + blk * 128 + 128],
                        ident[:]), waits=w_, inc="pe"))
                ev = p.emit("vector", lambda h, blk=blk, tg=tg, bank=bank: h.tensor_copy(
                    aC[:, blk * 2048 + tg * 1024:blk * 2048 + tg * 1024 + 1024],
                    psbf[bank][:, 0:1024]),
                    waits=[tts[-1]] + ([("sc", p.counts["sc"]), ("dv", p.counts["dv"])]
                                       if blk == 0 and tg == 0 else []), inc="dv")
                p.bread(bank, ev)
                ho_fm_toks.append(ev)
        woh_adds = []
        ld_prev = {}
        for tq in range(4):
            last = {}
            for hk in range(4):
                par = hk % 2
                ldt = wload(wb[:, 8192 + par * 1024:8192 + par * 1024 + 1024],
                            dramf["wo_heavy"][:, hk, :],
                            waits=p.bwait(("whs", par)) + (ho_fm_toks[-1:] if tq == 0 and hk < 2 else []))
                for blk in range(NKC):
                    w_ = [ldt] + ho_fm_toks
                    if hk == 0:
                        w_ = w_ + p.bwait(blk)
                    last[blk] = p.emit("tensor", lambda h, blk=blk, hk=hk, tq=tq, par=par: h.matmul(
                        psum[blk][:, 0:512],
                        wb[:, 8192 + par * 1024 + blk * 128:8192 + par * 1024 + blk * 128 + 128],
                        aC[:, hk * 2048 + tq * 512:hk * 2048 + tq * 512 + 512],
                        start=(hk == 0), stop=(hk == 3), skip_group_check=True),
                        waits=w_, inc="pe" if hk == 3 or blk == NKC - 1 else None)
                p.bread(("whs", par), last[NKC - 1])
            for blk in range(NKC):
                at = p.emit("vector", lambda h, blk=blk, tq=tq: h.tensor_add(
                    xres[:, blk, tq * 512:tq * 512 + 512],
                    xres[:, blk, tq * 512:tq * 512 + 512],
                    psum[blk][:, 0:512]), waits=[last[blk]], inc="dv")
                p.bread(blk, at)
                woh_adds.append(at)
        ffn_done = woh_adds + wol_adds + [("dv", p.counts["dv"]), ("sc", p.counts["sc"])]
        if "x2" in dumps:
            ffn_done = ffn_done + [p.dma("sync", dumps["x2"], xres[:], waits=ffn_done)]

        fin = p.dma("sync", out_fm, xres[:], waits=ffn_done)
        p.wait_only("sync", [fin, ("dq_sync", p.counts["dq_sync"]),
                             ("dq_gpsimd", p.counts["dq_gpsimd"])])

        @block.sync
        def _(h):
            for f in p.ops["sync"]:
                f(h)

        @block.tensor
        def _(h):
            for f in p.ops["tensor"]:
                f(h)

        @block.vector
        def _(h):
            for f in p.ops["vector"]:
                f(h)

        @block.scalar
        def _(h):
            for f in p.ops["scalar"]:
                f(h)

        @block.gpsimd
        def _(h):
            for f in p.ops["gpsimd"]:
                f(h)

    return nc, dumps



def make_consts():
    c = np.zeros((128, 16), np.float32)
    c[:, 0] = 1.0
    c[:, 1] = -1.0
    c[:, 2] = 1e-5
    c[:, 3] = 1e-6
    c[:, 4] = -1e30
    c[:, 5] = 1.0 / 8.0
    c[:, 6] = -1.0 / 16.0
    c[:, 7] = 1.0 / 1024.0
    c[:, 8] = 32.0
    c[:, 9] = 1e-12
    c[:, 10] = float(np.log(np.float32(NQ_SEL * 9.0 / 8.0)))
    return c


def make_tril():
    m = np.zeros((128, 2, CG), BF)
    i = np.arange(128)[:, None]
    j = np.arange(CG)[None, :]
    m[:, 0, :] = np.where(j > i, 1e30, 0.0)
    m[:, 1, :] = np.where(j > i + 128, 1e30, 0.0)
    return m


def fm(w, p=128):
    din, dout = w.shape
    return np.ascontiguousarray(w.reshape(din // p, p, dout).transpose(1, 0, 2))


def vec_fm(v, p=128):
    n = v.shape[0]
    return np.ascontiguousarray(v.reshape(n // p, p).T)


def prep_inputs(kw):
    g = {}
    g["ident"] = np.eye(128, dtype=BF)
    g["identf"] = np.eye(128, dtype=np.float32)
    g["onesb"] = np.ones((128, 8), dtype=BF)
    g["onesr"] = np.ones((1, 128), dtype=np.float32)
    g["onesrb"] = np.ones((1, 128), dtype=BF)
    g["consts"] = make_consts()
    g["trilm"] = make_tril()
    g["Wq"] = fm(np.asarray(kw["L0_attn_Wq"]))
    g["Wk"] = fm(np.asarray(kw["L0_attn_Wk"]))
    g["Wv"] = fm(np.asarray(kw["L0_attn_Wv"]))
    g["Wg"] = fm(np.asarray(kw["L0_attn_Wg"]))
    g["Wo"] = fm(np.asarray(kw["L0_attn_Wo"]))
    g["Wgt"] = fm(np.asarray(kw["L0_attn_Wgt"]))
    g["gate_w"] = fm(np.asarray(kw["L0_ffn_gate_w"]))
    w1s = [fm(w) for w in np.asarray(kw["L0_ffn_w1"])]
    w3s = [fm(w) for w in np.asarray(kw["L0_ffn_w3"])]
    g["w13"] = np.ascontiguousarray(np.stack(
        [np.concatenate([a, b], axis=2) for a, b in zip(w1s, w3s)]))
    g["w2"] = np.ascontiguousarray(np.stack([fm(w) for w in np.asarray(kw["L0_ffn_w2"])]))
    g["W_inT"] = fm(np.ascontiguousarray(np.asarray(kw["L1_attn_W_in"]).T))
    g["conv_w"] = np.ascontiguousarray(
        np.asarray(kw["L1_attn_conv_w"])[:, 0, :].reshape(NDC, 128, 4).transpose(1, 0, 2))
    g["conv_b"] = vec_fm(np.asarray(kw["L1_attn_conv_b"]))
    g["W_xT"] = fm(np.ascontiguousarray(np.asarray(kw["L1_attn_W_x"]).T))
    g["W_dt"] = vec_fm(np.asarray(kw["L1_attn_W_dt"])[:, 0])
    g["b_dt"] = vec_fm(np.asarray(kw["L1_attn_b_dt"]))
    g["A_log"] = np.ascontiguousarray(
        np.asarray(kw["L1_attn_A_log"]).reshape(NDC, 128, NS).transpose(1, 0, 2))
    g["D_param"] = vec_fm(np.asarray(kw["L1_attn_D_param"]))
    g["W_outT"] = fm(np.ascontiguousarray(np.asarray(kw["L1_attn_W_out"]).T))
    g["gamma_light"] = vec_fm(np.asarray(kw["L1_ffn_gamma_light"]))
    g["gamma_heavy"] = vec_fm(np.asarray(kw["L1_ffn_gamma_heavy"]))
    g["wq_light"] = fm(np.asarray(kw["L1_ffn_wq_light"]))
    g["wkv_light"] = fm(np.asarray(kw["L1_ffn_wkv_light"]))
    g["wo_light"] = fm(np.asarray(kw["L1_ffn_wo_light"]))
    g["q_rt"] = vec_fm(np.asarray(kw["L1_ffn_q_routing_token"]))
    wkvh = np.asarray(kw["L1_ffn_wkv_heavy"]).reshape(H1, 2, DH1, D)
    g["WkT_heavy"] = fm(np.ascontiguousarray(
        wkvh[:, 0].transpose(2, 0, 1).reshape(D, HID)))
    g["WvT_heavy"] = fm(np.ascontiguousarray(
        wkvh[:, 1].transpose(2, 0, 1).reshape(D, HID)))
    g["wq_heavy"] = fm(np.asarray(kw["L1_ffn_wq_heavy"]))
    g["wo_heavy"] = fm(np.asarray(kw["L1_ffn_wo_heavy"]))

    x = np.asarray(kw["L0_attn_x"])
    xs = [np.ascontiguousarray(x[c].T.reshape(NKC, 128, T).transpose(1, 0, 2))
          for c in range(8)]
    return g, xs


def run(kw, dump_names=(), cores=8):
    nc, dumps = build_program(dump_names)
    g, xs = prep_inputs(kw)
    in_maps = [dict(g, x_fm=xs[c]) for c in range(cores)]
    res = run_bass_kernel_spmd(nc, in_maps, core_ids=list(range(cores)))
    return res


def kernel(**kw):
    res = run(kw)
    outs = []
    for c in range(8):
        ofm = np.asarray(res.results[c]["out_fm"], dtype=np.float32)
        outs.append(ofm.transpose(1, 0, 2).reshape(D, T).T)
    return np.stack(outs).astype(np.float32)


# version 15
# speedup vs baseline: 1.0821x; 1.0446x over previous
import contextlib
import numpy as np
import ml_dtypes
import concourse.bass as bass
import concourse.mybir as mybir
from concourse.bass_utils import run_bass_kernel_spmd

F32 = mybir.dt.float32
BF16 = mybir.dt.bfloat16
AF = mybir.ActivationFunctionType
ALU = mybir.AluOpType
AX = mybir.AxisListType
BF = ml_dtypes.bfloat16

T = 2048
D = 1024
NKC = D // 128
H0 = 16
DH0 = 64
CG = 256
DI = 2048
NDC = DI // 128
NS = 16
TCS = 512
HID = 512
H1 = 8
DH1 = 64
WIN = 128
NQ_SEL = 1024

ENGINES = ("sync", "tensor", "vector", "scalar", "gpsimd")

DEBUG_DUMPS = []


class Prog:

    def __init__(self, nc):
        self.nc = nc
        self.ops = {e: [] for e in ENGINES}
        self.sems = {}
        self.counts = {}
        self.last_wait = {e: {} for e in ENGINES}

    def new_sem(self, st, name):
        self.sems[name] = st.enter_context(self.nc.semaphore(name))
        self.counts[name] = 0

    def emit(self, eng, fn, waits=(), inc=None, incn=1):
        wl = []
        for s, v in waits:
            if s.startswith("dq_"):
                v = self.counts[s]
            if v > 0 and self.last_wait[eng].get(s, 0) < v:
                self.last_wait[eng][s] = v
                wl.append((s, v))
        tok = None
        if inc is not None:
            self.counts[inc] += incn
            tok = (inc, self.counts[inc])
        sems = self.sems

        def closure(h, wl=wl, fn=fn, inc=inc, incn=incn):
            for s, v in wl:
                h.wait_ge(sems[s], v)
            r = fn(h)
            if inc is not None:
                r.then_inc(sems[inc], incn)
        self.ops[eng].append(closure)
        return tok

    def wait_only(self, eng, waits):
        self.emit(eng, lambda h: None, waits=waits)

    bank_tok = None

    def bwait(self, bank):
        if self.bank_tok is None:
            self.bank_tok = {}
        t = self.bank_tok.get(bank)
        return [t] if t else []

    def bread(self, bank, tok):
        if self.bank_tok is None:
            self.bank_tok = {}
        self.bank_tok[bank] = tok

    def dma(self, eng, out, in_, waits=()):
        sem = "dq_" + eng
        return self.emit(eng, lambda h: h.dma_start(out=out, in_=in_),
                         waits=waits, inc=sem, incn=16)


def ceil_div(a, b):
    return -(-a // b)


def build_program(dump_names=()):
    nc = bass.Bass("TRN2", target_bir_lowering=False, detect_race_conditions=False)
    dramf = {}

    def din(name, shape, dtype=F32):
        dramf[name] = nc.dram_tensor(name, list(shape), dtype, kind="ExternalInput").ap()
        return dramf[name]

    def dout(name, shape, dtype=F32):
        dramf[name] = nc.dram_tensor(name, list(shape), dtype, kind="ExternalOutput").ap()
        return dramf[name]

    x_in = din("x_fm", (128, NKC, T))
    out_fm = dout("out_fm", (128, NKC, T))
    for w in ("Wq", "Wk", "Wv", "Wg", "Wo"):
        din(w, (128, NKC, D))
    din("Wgt", (128, NKC, H0))
    din("gate_w", (128, NKC, 8))
    din("w13", (8, 128, NKC, 1024))
    din("w2", (8, 128, 4, D))
    din("W_inT", (128, NKC, 2 * DI))
    din("conv_w", (128, NDC, 4))
    din("conv_b", (128, NDC))
    din("W_xT", (128, NDC, 33))
    din("W_dt", (128, NDC))
    din("b_dt", (128, NDC))
    din("A_log", (128, NDC, NS))
    din("D_param", (128, NDC))
    din("W_outT", (128, NDC, D))
    din("gamma_light", (128, NKC))
    din("gamma_heavy", (128, NKC))
    din("wq_light", (128, NKC, HID))
    din("wkv_light", (128, NKC, 2 * HID))
    din("wo_light", (128, 4, D))
    din("q_rt", (128, NKC))
    din("WkT_heavy", (128, NKC, HID))
    din("WvT_heavy", (128, NKC, HID))
    din("wq_heavy", (128, NKC, HID))
    din("wo_heavy", (128, 4, D))
    din("ident", (128, 128), BF16)
    din("identf", (128, 128))
    din("onesb", (128, 8), BF16)
    din("onesr", (1, 128))
    din("onesrb", (1, 128), BF16)
    din("consts", (128, 16))
    din("trilm", (128, 2, CG), BF16)
    x_old_hbm = nc.dram_tensor("x_old_hbm", [128, NKC, T], F32, kind="Internal").ap()
    x_new_hbm = nc.dram_tensor("x_new_hbm", [128, NKC, T], F32, kind="Internal").ap()

    dumps = {}

    def register_dump(name, shape, dtype=F32):
        if name in dump_names:
            dumps[name] = dout("dump_" + name, shape, dtype)

    register_dump("xn", (128, NKC, T), BF16)
    register_dump("q_tok", (128, 16, D), BF16)
    register_dump("qraw", (128, 4096), BF16)
    register_dump("wslot", (128, 1024), BF16)
    register_dump("vtok", (128, 4096), BF16)
    register_dump("rows", (128, 2048))
    register_dump("ptile", (128, 1024), BF16)
    register_dump("o_tok", (128, 16, D), BF16)
    register_dump("x1", (128, NKC, T))
    register_dump("moe_xs", (128, NKC, T), BF16)
    register_dump("moe_scale", (8, 512))
    register_dump("xd15", (128, 2048), BF16)
    register_dump("route", (128, 144))
    register_dump("srow", (4, 2048))
    register_dump("xn2b", (128, NKC, T), BF16)
    register_dump("dbg1", (128, 8))
    register_dump("dbg2", (128, 16))
    register_dump("dbg3", (4, 2048))
    register_dump("u_ssm", (128, NDC, T), BF16)
    register_dump("ssm_in", (64, T))
    register_dump("delta0", (128, T))
    register_dump("y_ssm", (128, NKC, T))
    register_dump("xn2", (128, NKC, T), BF16)
    register_dump("mask_q", (128, 16))
    register_dump("light_o", (128, 16, HID), BF16)
    register_dump("x2", (128, NKC, T))

    with contextlib.ExitStack() as st:
        sb = lambda name, shape, dtype: st.enter_context(nc.sbuf_tensor(name, list(shape), dtype))
        xres = sb("xres", (128, NKC, T), F32)
        actA = sb("actA", (128, 8, T), BF16)
        actB = sb("actB", (128, 8, T), BF16)
        actC = sb("actC", (128, 8, T), BF16)
        wbuf = sb("wbuf", (128, 5, T), BF16)
        misc = sb("misc", (128, 4096), F32)
        pslab = sb("pslab", (128, 2048), BF16)
        sscr = pslab
        fscr = sb("fscr", (128, 1024), F32)
        ident = sb("identsb", (128, 128), BF16)
        identf = sb("identfsb", (128, 128), F32)
        onesb = sb("onesbsb", (128, 8), BF16)
        onesr = sb("onesrsb", (1, 128), F32)
        onesrb = sb("onesrbsb", (1, 128), BF16)
        consts = sb("constsb", (128, 16), F32)
        trilm = sb("trilmsb", (128, 2, CG), BF16)
        psum = [st.enter_context(nc.psum_tensor(f"ps{i}", [128, 512], F32)) for i in range(8)]

        p = Prog(nc)
        for s in ("dq_sync", "dq_gpsimd", "pe", "dv", "sc", "gp"):
            p.new_sem(st, s)
        block = st.enter_context(nc.Block())

        C_ONE, C_NEG1, C_EPS_LN, C_EPS_RMS, C_NEG1E30, C_QSCALE, C_NGATE, C_INVD, C_SQRTD, C_EPS12, C_LOGK = range(11)
        cc = lambda i: consts[:, i:i + 1]

        tok_x = p.dma("sync", xres[:], x_in)
        p.dma("sync", ident[:], dramf["ident"])
        p.dma("sync", identf[:], dramf["identf"])
        p.dma("sync", onesb[:], dramf["onesb"])
        p.dma("sync", onesr[:], dramf["onesr"])
        p.dma("sync", onesrb[:], dramf["onesrb"])
        p.dma("sync", consts[:], dramf["consts"])
        p.dma("sync", trilm[:], dramf["trilm"])
        if "srow" in dumps:
            p.dma("sync", dramf["dump_srow"][2:3, 1024:2048], dramf["q_rt"])
        base_loads = [("dq_sync", p.counts["dq_sync"])]

        def layernorm(xsrc_f32, dst_bf16, stat_off, waits, stage_bf16):
            toks = []
            for tc in range(4):
                t0 = tc * 512
                ct = []
                for kc in range(NKC):
                    ct.append(p.emit("vector", lambda h, kc=kc, t0=t0: h.tensor_copy(
                        stage_bf16[:, kc, t0:t0 + 512], xsrc_f32[:, kc, t0:t0 + 512]),
                        waits=waits, inc="dv"))
                sq_tok = [None, None]
                for kc in range(NKC):
                    par = kc % 2
                    s0 = 512 * par
                    w_sq = [sq_tok[par]] if sq_tok[par] else []
                    tsq = p.emit("scalar", lambda h, kc=kc, t0=t0, s0=s0: h.activation(
                        sscr[:, s0:s0 + 512], stage_bf16[:, kc, t0:t0 + 512], AF.Square),
                        waits=[ct[kc]] + w_sq, inc="sc")
                    p.emit("tensor", lambda h, kc=kc, t0=t0: h.matmul(
                        psum[0][0:1, 0:512], onesb[:, 0:1], stage_bf16[:, kc, t0:t0 + 512],
                        start=(kc == 0), stop=(kc == NKC - 1), skip_group_check=True),
                        waits=[ct[kc]], inc="pe")
                    sq_tok[par] = p.emit("tensor", lambda h, kc=kc, s0=s0: h.matmul(
                        psum[1][0:1, 0:512], onesb[:, 0:1], sscr[:, s0:s0 + 512],
                        start=(kc == 0), stop=(kc == NKC - 1), skip_group_check=True),
                        waits=[tsq], inc="pe")
                pe_tok = ("pe", p.counts["pe"])
                m0 = stat_off + t0
                r0 = stat_off + T + t0
                t1_ = p.emit("vector", lambda h, m0=m0: h.tensor_scalar(
                    misc[0:1, m0:m0 + 512], psum[0][0:1, 0:512], 1.0 / D, None, ALU.mult),
                    waits=[pe_tok], inc="dv")
                t2_ = p.emit("vector", lambda h: h.tensor_scalar(
                    fscr[0:1, 0:512], psum[1][0:1, 0:512], 1.0 / D, None, ALU.mult),
                    waits=[pe_tok], inc="dv")
                t3_ = p.emit("scalar", lambda h, m0=m0: h.activation(
                    fscr[0:1, 512:1024], misc[0:1, m0:m0 + 512], AF.Square),
                    waits=[t1_], inc="sc")
                tv = p.emit("vector", lambda h: h.tensor_sub(
                    fscr[0:1, 0:512], fscr[0:1, 0:512], fscr[0:1, 512:1024]),
                    waits=[t2_, t3_], inc="dv")
                tq = p.emit("scalar", lambda h: h.activation(
                    fscr[0:1, 512:1024], fscr[0:1, 0:512], AF.Sqrt,
                    bias=consts[0:1, C_EPS_LN:C_EPS_LN + 1]), waits=[tv], inc="sc")
                tr = p.emit("vector", lambda h, r0=r0: h.reciprocal(
                    misc[0:1, r0:r0 + 512], fscr[0:1, 512:1024]), waits=[tq], inc="dv")
                tbm = p.emit("tensor", lambda h, m0=m0: h.matmul(
                    psum[2][:, 0:512], onesr[:],
                    misc[0:1, m0:m0 + 512], start=True, stop=True),
                    waits=[t1_], inc="pe")
                tbr = p.emit("tensor", lambda h, r0=r0: h.matmul(
                    psum[3][:, 0:512], onesr[:],
                    misc[0:1, r0:r0 + 512], start=True, stop=True),
                    waits=[tr], inc="pe")
                for kc in range(NKC):
                    p.emit("vector", lambda h, kc=kc, t0=t0: h.tensor_sub(
                        fscr[:, 0:512], xsrc_f32[:, kc, t0:t0 + 512], psum[2][:, 0:512]),
                        waits=[tbm], inc="dv")
                    tm_ = p.emit("vector", lambda h, kc=kc, t0=t0: h.tensor_mul(
                        dst_bf16[:, kc, t0:t0 + 512], fscr[:, 0:512], psum[3][:, 0:512]),
                        waits=[tbr], inc="dv")
                    toks.append(tm_)
                waits = list(waits) + [t1_, t2_, ("dv", p.counts["dv"])]
            return toks

        ln1_toks = layernorm(xres, actA, 0, [tok_x] + base_loads, actC)
        ln1_done = [("dv", p.counts["dv"])]

        if "xn" in dumps:
            p.dma("sync", dumps["xn"], actA[:], waits=ln1_done)

        aB = actB.rearrange("p a b -> p (a b)")
        aC = actC.rearrange("p a b -> p (a b)")
        wb = wbuf.rearrange("p a b -> p (a b)")
        S_f = misc
        Sbf = aB
        psbf = [ps.ap().bitcast(BF16) for ps in psum]

        tz0 = p.emit("vector", lambda h: h.memset(misc[:, 0:512], 0.0), inc="dv")
        tz1 = p.emit("vector", lambda h: h.memset(aB[:, 12288:12800], 0.0), inc="dv")
        tz2 = p.emit("vector", lambda h: h.memset(misc[0:16, 3728:3984], 0.0), inc="dv")
        zinit = [tz0, tz1, tz2]

        def wload(dst_ap, src_ap, waits=()):
            return p.dma("gpsimd", dst_ap, src_ap, waits=waits)

        TC0 = 512
        sbf_tok = {hh: tz1 for hh in range(H0)}
        prev_tc_done = ln1_done + zinit
        for tc in range(4):
            t0 = tc * TC0
            dump_toks = []
            xn = actA
            tc_start = prev_tc_done

            def kc_slot(i):
                return wb[:, (i % 8) * 1024:(i % 8) * 1024 + 1024]

            vg_results = {}
            prev_phase_mm = []
            for wname, dcol in (("Wv", None), ("Wg", None)):
                ldtoks = []
                for kc in range(NKC):
                    ldtoks.append(wload(kc_slot(kc), dramf[wname][:, kc, :],
                                        waits=tc_start + prev_phase_mm))
                mm_last = []
                for kc in range(NKC):
                    for tt in range(4):
                        ta = t0 + tt * 128
                        for hf in range(2):
                            bank = tt * 2 + hf
                            w_ = [ldtoks[kc]] + tc_start
                            if kc == 0:
                                w_ += p.bwait(bank)
                            tokmm = p.emit("tensor", lambda h, kc=kc, ta=ta, hf=hf, bank=bank: h.matmul(
                                psum[bank][:, 0:512],
                                xn[:, kc, ta:ta + 128],
                                kc_slot(kc)[:, hf * 512:hf * 512 + 512],
                                start=(kc == 0), stop=(kc == NKC - 1), skip_group_check=True),
                                waits=w_, inc="pe" if kc == NKC - 1 else None)
                            if kc == NKC - 1:
                                mm_last.append(tokmm)
                vg_results[wname] = mm_last
                prev_phase_mm = mm_last
                if wname == "Wv":
                    v_toks = []
                    for tt in range(4):
                        e0 = p.emit("scalar", lambda h, tt=tt: h.copy(
                            pslab[:, 0:512], psum[tt * 2][:, 0:512]),
                            waits=mm_last + v_toks[-1:], inc="sc")
                        e1_ = p.emit("scalar", lambda h, tt=tt: h.copy(
                            pslab[:, 512:1024], psum[tt * 2 + 1][:, 0:512]), waits=mm_last, inc="sc")
                        p.bread(tt * 2, e0)
                        p.bread(tt * 2 + 1, e1_)
                        sq = p.emit("scalar", lambda h: h.activation(
                            pslab[:, 1024:2048], pslab[:, 0:1024], AF.Square,
                            accum_out=fscr[:, 0:1]), waits=[e0, e1_], inc="sc")
                        rs = p.emit("scalar", lambda h: h.activation(
                            fscr[:, 1:2], fscr[:, 0:1], AF.Sqrt, scale=1.0 / D,
                            bias=cc(C_EPS_RMS)), waits=[sq], inc="sc")
                        rr = p.emit("vector", lambda h: h.reciprocal(
                            fscr[:, 2:3], fscr[:, 1:2]), waits=[rs], inc="dv")
                        vt = p.emit("vector", lambda h, tt=tt: h.tensor_scalar(
                            aC[:, 4096 + tt * 1024:4096 + (tt + 1) * 1024],
                            pslab[:, 0:1024], fscr[:, 2:3], None, ALU.mult),
                            waits=[rr], inc="dv")
                        v_toks.append(vt)
                    if tc == 0 and "vtok" in dumps:
                        dump_toks.append(p.dma("sync", dumps["vtok"], aC[:, 4096:8192], waits=v_toks))
                    prev_phase_mm = v_toks
                    prev_phase_mm = mm_last
            g_toks = []
            for tt in range(4):
                for hf in range(2):
                    gtk = p.emit("scalar", lambda h, tt=tt, hf=hf: h.copy(
                        aB[:, 4096 + tt * 1024 + hf * 512:4096 + tt * 1024 + hf * 512 + 512],
                        psum[tt * 2 + hf][:, 0:512]), waits=vg_results["Wg"], inc="sc")
                    p.bread(tt * 2 + hf, gtk)
                    g_toks.append(gtk)

            qk_norm = []
            prev_w_mms = vg_results["Wg"]
            for wname, arena in (("Wq", aB), ("Wk", aC)):
                ldtoks = []
                raw_toks = []
                this_mms = []
                for blk in range(NKC):
                    ldtoks.append(wload(
                        wb[:, blk * 1024:blk * 1024 + 1024].rearrange("p (a b) -> p a b", a=8),
                        dramf[wname][:, :, blk * 128:(blk + 1) * 128],
                        waits=prev_w_mms if blk < 8 else []))
                for blk in range(NKC):
                    bank = blk % 2
                    for kc in range(NKC):
                        w_ = [ldtoks[blk]]
                        if kc == 0:
                            w_ += p.bwait(bank) + g_toks
                        tokmm = p.emit("tensor", lambda h, blk=blk, kc=kc, bank=bank, t0=t0: h.matmul(
                            psum[bank][:, 0:512],
                            wb[:, blk * 1024 + kc * 128:blk * 1024 + kc * 128 + 128],
                            xn[:, kc, t0:t0 + 512],
                            start=(kc == 0), stop=(kc == NKC - 1), skip_group_check=True),
                            waits=w_, inc="pe" if kc == NKC - 1 else None)
                    this_mms.append(tokmm)
                    rt = p.emit("scalar", lambda h, blk=blk, bank=bank, arena=arena: h.copy(
                        arena[:, blk * 512:(blk + 1) * 512], psum[bank][:, 0:512]),
                        waits=[tokmm], inc="sc")
                    p.bread(bank, rt)
                    raw_toks.append(rt)
                if tc == 0 and wname == "Wq" and "qraw" in dumps:
                    dump_toks.append(p.dma("sync", dumps["qraw"], aB[:, 0:4096], waits=raw_toks))
                if tc == 0 and wname == "Wq" and "wslot" in dumps:
                    wt_ = p.dma("sync", dumps["wslot"], wb[:, 0:1024], waits=[ldtoks[0]])
                    this_mms.append(wt_)
                    dump_toks.append(wt_)
                prev_w_mms = this_mms
                sq_prev = {}
                for blk in range(NKC):
                    par = blk % 2
                    w_ = [raw_toks[blk]]
                    if par in sq_prev:
                        w_.append(sq_prev[par])
                    sqt = p.emit("scalar", lambda h, blk=blk, par=par, arena=arena: h.activation(
                        pslab[:, par * 512:par * 512 + 512],
                        arena[:, blk * 512:(blk + 1) * 512], AF.Square),
                        waits=w_, inc="sc")
                    w2_ = [sqt]
                    if blk == 0:
                        w2_ += p.bwait(2)
                    mmt = p.emit("tensor", lambda h, blk=blk, par=par: h.matmul(
                        psum[2][0:1, 0:512], onesb[:, 0:1], pslab[:, par * 512:par * 512 + 512],
                        start=(blk == 0), stop=(blk == NKC - 1), skip_group_check=True),
                        waits=w2_, inc="pe")
                    sq_prev[par] = mmt
                r1 = p.emit("scalar", lambda h: h.activation(
                    fscr[0:1, 0:512], psum[2][0:1, 0:512], AF.Sqrt, scale=1.0 / D,
                    bias=consts[0:1, C_EPS_RMS:C_EPS_RMS + 1]),
                    waits=[sq_prev[0], sq_prev[1]], inc="sc")
                p.bread(2, r1)
                r2 = p.emit("vector", lambda h: h.reciprocal(
                    fscr[0:1, 512:1024], fscr[0:1, 0:512]), waits=[r1], inc="dv")
                if wname == "Wq":
                    r2 = p.emit("vector", lambda h: h.tensor_scalar(
                        fscr[0:1, 512:1024], fscr[0:1, 512:1024], 1.0 / 8.0, None,
                        ALU.mult), waits=[r2], inc="dv")
                r3 = p.emit("tensor", lambda h: h.matmul(
                    psum[3][:, 0:512], onesr[:], fscr[0:1, 512:1024],
                    start=True, stop=True), waits=[r2] + p.bwait(3), inc="pe")
                lastn = None
                for blk in range(NKC):
                    lastn = p.emit("vector", lambda h, blk=blk, arena=arena: h.tensor_mul(
                        arena[:, 8192 + blk * 512:8192 + (blk + 1) * 512],
                        arena[:, blk * 512:(blk + 1) * 512], psum[3][:, 0:512]),
                        waits=[r3], inc="dv")
                    qk_norm.append(lastn)
                p.bread(3, lastn)
            qk_done = [("dv", p.counts["dv"])]

            ldg = wload(wb[:, 0:128].rearrange("p (a b) -> p a b", a=8),
                        dramf["Wgt"], waits=prev_w_mms)
            for kc in range(NKC):
                w_ = [ldg]
                if kc == 0:
                    w_ += p.bwait(4)
                tokmm = p.emit("tensor", lambda h, kc=kc, t0=t0: h.matmul(
                    psum[4][0:16, 0:512], wb[:, kc * 16:kc * 16 + 16],
                    xn[:, kc, t0:t0 + 512],
                    start=(kc == 0), stop=(kc == NKC - 1), skip_group_check=True),
                    waits=w_, inc="pe" if kc == NKC - 1 else None)
            a1 = p.emit("scalar", lambda h: h.activation(
                fscr[0:16, 0:512], psum[4][0:16, 0:512], AF.Abs), waits=[tokmm], inc="sc")
            a2 = p.emit("scalar", lambda h: h.activation(
                fscr[0:16, 512:1024], fscr[0:16, 0:512], AF.Exp, scale=-1.0), waits=[a1], inc="sc")
            a3 = p.emit("scalar", lambda h: h.activation(
                fscr[0:16, 512:1024], fscr[0:16, 512:1024], AF.Ln,
                bias=consts[0:16, C_ONE:C_ONE + 1]), waits=[a2], inc="sc")
            a4 = p.emit("vector", lambda h: h.tensor_scalar(
                fscr[0:16, 0:512], psum[4][0:16, 0:512], -1.0, 0.0, ALU.mult, ALU.max),
                waits=[a2], inc="dv")
            p.bread(4, a4)
            a5 = p.emit("vector", lambda h: h.tensor_add(
                fscr[0:16, 0:512], fscr[0:16, 0:512], fscr[0:16, 512:1024]),
                waits=[a3, a4], inc="dv")
            a6 = p.emit("vector", lambda h: h.tensor_scalar(
                fscr[0:16, 0:512], fscr[0:16, 0:512], -1.0 / 16.0, None, ALU.mult),
                waits=[a5], inc="dv")
            bt = []
            for nn in range(2):
                bt.append(p.emit("vector", lambda h, nn=nn: h.tensor_tensor_scan(
                    misc[0:16, 2048 + nn * 256:2048 + (nn + 1) * 256],
                    misc[0:16, 3728:3984],
                    fscr[0:16, nn * 256:(nn + 1) * 256],
                    0.0, ALU.add, ALU.add), waits=[a6] + zinit, inc="dv"))
            e1t = p.emit("scalar", lambda h: h.activation(
                misc[0:16, 2560:3072], misc[0:16, 2048:2560], AF.Exp), waits=bt, inc="sc")
            e2ts = []
            for nn in range(2):
                w1_ = p.emit("vector", lambda h, nn=nn: h.tensor_scalar(
                    fscr[0:16, 512 + nn * 256:512 + (nn + 1) * 256],
                    misc[0:16, 2048 + nn * 256:2048 + (nn + 1) * 256],
                    misc[0:16, 2048 + nn * 256 + 255:2048 + nn * 256 + 256],
                    None, ALU.subtract), waits=bt, inc="dv")
                e2ts.append(p.emit("scalar", lambda h, nn=nn: h.activation(
                    misc[0:16, 3072 + nn * 256:3072 + (nn + 1) * 256],
                    fscr[0:16, 512 + nn * 256:512 + (nn + 1) * 256],
                    AF.Exp, scale=-1.0), waits=[w1_], inc="sc"))
            bct_toks = []
            for st4 in range(4):
                tr1 = p.emit("tensor", lambda h, st4=st4: h.transpose(
                    psum[5][:, 0:16], misc[0:16, 2048 + st4 * 128:2048 + st4 * 128 + 128],
                    identf[0:16, 0:16]), waits=bt + p.bwait(5), inc="pe")
                ev1 = p.emit("scalar", lambda h, st4=st4: h.copy(
                    misc[:, 3584 + st4 * 16:3584 + st4 * 16 + 16], psum[5][:, 0:16]),
                    waits=[tr1], inc="sc")
                p.bread(5, ev1)
                tr2 = p.emit("tensor", lambda h, st4=st4: h.transpose(
                    psum[5][:, 0:16], misc[0:16, 3072 + st4 * 128:3072 + st4 * 128 + 128],
                    identf[0:16, 0:16]), waits=e2ts + [ev1], inc="pe")
                ev2 = p.emit("scalar", lambda h, st4=st4: h.copy(
                    misc[:, 3648 + st4 * 16:3648 + st4 * 16 + 16], psum[5][:, 0:16]),
                    waits=[tr2], inc="sc")
                p.bread(5, ev2)
                bct_toks += [ev1, ev2]
            rows_done = bct_toks + [e1t] + e2ts

            if tc == 0 and "rows" in dumps:
                dump_toks.append(p.dma("sync", dumps["rows"], misc[:, 2048:4096],
                                       waits=rows_done))
            dump_hold_q = dump_toks
            if "q_tok" in dumps:
                dump_hold_q.append(p.dma(
                    "sync", dumps["q_tok"].rearrange("p a b -> p (a b)")[:, tc * 4096:(tc + 1) * 4096],
                    aB[:, 8192:12288], waits=qk_done))

            for nn in range(2):
                nc0 = nn * 256
                t_tr = p.emit("tensor", lambda h, nn=nn: h.transpose(
                    psum[6][0:1, 0:16],
                    misc[0:16, 2560 + nn * 256 + 255:2560 + nn * 256 + 256],
                    identf[0:16, 0:16]), waits=rows_done + p.bwait(6), inc="pe")
                t_ev = p.emit("scalar", lambda h: h.copy(
                    misc[0:1, 3984:4000], psum[6][0:1, 0:16]), waits=[t_tr], inc="sc")
                p.bread(6, t_ev)
                t_rep = p.emit("tensor", lambda h: h.matmul(
                    psum[6][:, 16:32], onesr[0:1, 0:128], misc[0:1, 3984:4000],
                    start=True, stop=True), waits=[t_ev] + p.bwait(6), inc="pe")
                t_e3 = p.emit("scalar", lambda h: h.copy(
                    misc[:, 3712:3728], psum[6][:, 16:32]), waits=[t_rep], inc="sc")
                p.bread(6, t_e3)

                for blk in range(NKC):
                    qb = 6 + blk % 2
                    t_r1 = None
                    for hh2 in range(2):
                        sel1 = identf[0:16, 2 * blk + hh2:2 * blk + hh2 + 1].to_broadcast((16, 64))
                        t_r1 = p.emit("tensor", lambda h, nn=nn, sel1=sel1, hh2=hh2, qb=qb: h.matmul(
                            psum[qb][hh2 * 64:hh2 * 64 + 64, 32:288], sel1,
                            misc[0:16, 2560 + nn * 256:2560 + (nn + 1) * 256],
                            start=True, stop=True, skip_group_check=True),
                            waits=rows_done + (p.bwait(qb) if hh2 == 0 else []), inc="pe" if hh2 == 1 else None)
                    t_qd = p.emit("vector", lambda h, blk=blk, nc0=nc0, qb=qb: h.tensor_mul(
                        pslab[:, 1536:1792],
                        aB[:, 8192 + blk * 512 + nc0:8192 + blk * 512 + nc0 + 256],
                        psum[qb][:, 32:288]),
                        waits=[t_r1] + qk_done + p.bwait("qd"), inc="dv")
                    p.bread(qb, t_qd)
                    for hh in range(2):
                        hglob = 2 * blk + hh
                        par = hglob % 2
                        hp0 = hh * 64
                        selh = identf[0:16, hglob:hglob + 1].to_broadcast((16, 128))
                        t_rep2 = p.emit("tensor", lambda h, nn=nn, par=par, selh=selh: h.matmul(
                            psum[par][:, 0:256], selh,
                            misc[0:16, 2048 + nn * 256:2048 + (nn + 1) * 256],
                            start=True, stop=True), waits=rows_done + p.bwait(par), inc="pe")
                        for ss in range(2):
                            st4 = nn * 2 + ss
                            t_ea = p.emit("vector", lambda h, par=par, ss=ss, st4=st4, hglob=hglob: h.scalar_tensor_tensor(
                                fscr[:, ss * 256:(ss + 1) * 256],
                                psum[par][:, 0:256],
                                misc[:, 3584 + st4 * 16 + hglob:3584 + st4 * 16 + hglob + 1],
                                trilm[:, ss, :],
                                ALU.subtract, ALU.add),
                                waits=[t_rep2] + p.bwait(("warg", ss)), inc="dv")
                            if ss == 1:
                                p.bread(par, t_ea)
                            t_w = p.emit("scalar", lambda h, ss=ss: h.activation(
                                pslab[:, ss * 256:(ss + 1) * 256], fscr[:, ss * 256:(ss + 1) * 256],
                                AF.Exp, scale=-1.0),
                                waits=[t_ea] + p.bwait(("wtile", ss)), inc="sc")
                            p.bread(("warg", ss), t_w)
                            t_s = p.emit("tensor", lambda h, blk=blk, hp0=hp0, nc0=nc0, ss=ss, par=par: h.matmul(
                                psum[2 + par][:, 0:256],
                                aB[hp0:hp0 + 64, 8192 + blk * 512 + nc0 + ss * 128:
                                   8192 + blk * 512 + nc0 + ss * 128 + 128],
                                aC[hp0:hp0 + 64, 8192 + blk * 512 + nc0:8192 + blk * 512 + nc0 + 256],
                                start=True, stop=True), waits=qk_done + p.bwait(2 + par), inc="pe")
                            t_p = p.emit("vector", lambda h, ss=ss, par=par: h.tensor_mul(
                                pslab[:, 512 + ss * 256:512 + (ss + 1) * 256],
                                psum[2 + par][:, 0:256],
                                pslab[:, ss * 256:(ss + 1) * 256]),
                                waits=[t_w, t_s] + p.bwait(("P", ss)), inc="dv")
                            p.bread(2 + par, t_p)
                            p.bread(("wtile", ss), t_p)
                            ptt = []
                            for kvh in range(2):
                                t_t = p.emit("tensor", lambda h, ss=ss, kvh=kvh: h.transpose(
                                    psbf[4][:, ss * 256 + kvh * 128:ss * 256 + kvh * 128 + 128],
                                    pslab[:, 512 + ss * 256 + kvh * 128:512 + ss * 256 + kvh * 128 + 128],
                                    ident[:]), waits=[t_p] + (p.bwait(4) if kvh == 0 else []), inc="pe")
                                ptt.append(t_t)
                            t_pv = p.emit("vector", lambda h, ss=ss: h.tensor_copy(
                                pslab[:, 1024 + ss * 256:1024 + ss * 256 + 256],
                                psbf[4][:, ss * 256:ss * 256 + 256]),
                                waits=[ptt[-1]] + p.bwait(("PT", ss)), inc="dv")
                            p.bread(4, t_pv)
                            p.bread(("P", ss), t_pv)
                            for kvh in range(2):
                                loc_tt = (nc0 + kvh * 128) // 128
                                w_ = [t_pv] + v_toks
                                if kvh == 0 and ss == 0:
                                    w_ += p.bwait((2 + par, "o"))
                                t_o = p.emit("tensor", lambda h, ss=ss, kvh=kvh, loc_tt=loc_tt, hglob=hglob, par=par: h.matmul(
                                    psum[2 + par][:, 320 + ss * 64:320 + ss * 64 + 64],
                                    pslab[:, 1024 + ss * 256 + kvh * 128:1024 + ss * 256 + kvh * 128 + 128],
                                    aC[:, 4096 + loc_tt * 1024 + hglob * 64:4096 + loc_tt * 1024 + hglob * 64 + 64],
                                    start=(kvh == 0), stop=False, skip_group_check=True),
                                    waits=w_, inc=None)
                            t_oc = p.emit("tensor", lambda h, ss=ss, hp0=hp0, hglob=hglob, par=par: h.matmul(
                                psum[2 + par][:, 320 + ss * 64:320 + ss * 64 + 64],
                                pslab[hp0:hp0 + 64, 1536 + ss * 128:1536 + ss * 128 + 128],
                                Sbf[hp0:hp0 + 64, 12288 + (hglob // 2) * 64:12288 + (hglob // 2) * 64 + 64],
                                start=False, stop=True, skip_group_check=True),
                                waits=[t_qd, sbf_tok[hglob]], inc="pe")
                            p.bread(("PT", ss), t_oc)
                            p.bread("qd", t_oc)
                            oc_last = t_oc
                            if tc == 0 and nn == 0 and hglob == 0 and ss == 0 and "ptile" in dumps:
                                dump_toks.append(p.dma("sync", dumps["ptile"], pslab[:, 0:1024],
                                                       waits=[t_p, t_qd, ("sc", p.counts["sc"])]))
                        kv_mm = None
                        for kvh in range(2):
                            t_kt = p.emit("tensor", lambda h, hp0=hp0, blk=blk, nc0=nc0, kvh=kvh: h.transpose(
                                psbf[5][:, 0:64],
                                aC[hp0:hp0 + 64, 8192 + blk * 512 + nc0 + kvh * 128:
                                   8192 + blk * 512 + nc0 + kvh * 128 + 128],
                                ident[hp0:hp0 + 64, hp0:hp0 + 64]),
                                waits=qk_done + p.bwait(5), inc="pe")
                            t_kd = p.emit("vector", lambda h, nn=nn, kvh=kvh, hglob=hglob: h.tensor_scalar(
                                pslab[:, 1792 + kvh * 64:1792 + kvh * 64 + 64],
                                psbf[5][:, 0:64],
                                misc[:, 3648 + (nn * 2 + kvh) * 16 + hglob:
                                     3648 + (nn * 2 + kvh) * 16 + hglob + 1],
                                None, ALU.mult),
                                waits=[t_kt] + p.bwait(("kd", kvh)), inc="dv")
                            p.bread(5, t_kd)
                        for kvh in range(2):
                            loc_tt = (nc0 + kvh * 128) // 128
                            w_ = [("dv", p.counts["dv"])] if kvh == 0 else []
                            if kvh == 0:
                                w_ += p.bwait(2 + par)
                            kv_mm = p.emit("tensor", lambda h, kvh=kvh, loc_tt=loc_tt, hglob=hglob, par=par, hp0=hp0: h.matmul(
                                psum[2 + par][hp0:hp0 + 64, 256:320],
                                pslab[:, 1792 + kvh * 64:1792 + kvh * 64 + 64],
                                aC[:, 4096 + loc_tt * 1024 + hglob * 64:4096 + loc_tt * 1024 + hglob * 64 + 64],
                                start=(kvh == 0), stop=(kvh == 1), skip_group_check=True),
                                waits=w_, inc="pe" if kvh == 1 else None)
                        scol = (hglob // 2) * 64
                        t_su = p.emit("vector", lambda h, hglob=hglob, par=par, hp0=hp0, scol=scol: h.scalar_tensor_tensor(
                            misc[hp0:hp0 + 64, scol:scol + 64],
                            misc[hp0:hp0 + 64, scol:scol + 64],
                            misc[hp0:hp0 + 64, 3712 + hglob:3712 + hglob + 1],
                            psum[2 + par][hp0:hp0 + 64, 256:320],
                            ALU.mult, ALU.add), waits=[kv_mm, t_e3] + zinit, inc="dv")
                        p.bread(2 + par, t_su)
                        sbf_tok[hglob] = p.emit("vector", lambda h, hglob=hglob, hp0=hp0, scol=scol: h.tensor_copy(
                            aB[hp0:hp0 + 64, 12288 + scol:12288 + scol + 64],
                            misc[hp0:hp0 + 64, scol:scol + 64]), waits=[t_su], inc="dv")
                        loc0 = nc0 // 128
                        t_oe = p.emit("scalar", lambda h, loc0=loc0, hglob=hglob, par=par: h.copy(
                            aC[:, 12288:16384].rearrange("p (x y) -> p x y", x=4)
                            [:, loc0:loc0 + 2, hglob * 64:hglob * 64 + 64],
                            psum[2 + par][:, 320:448].rearrange("p (x y) -> p x y", x=2)),
                            waits=[oc_last, kv_mm, t_su], inc="sc")
                        p.bread((2 + par, "o"), t_oe)
                        p.bread(2 + par, t_oe)
            stageB_done = [("dv", p.counts["dv"]), ("sc", p.counts["sc"])]
            dump_hold_o = []
            if "o_tok" in dumps:
                dump_hold_o.append(p.dma(
                    "sync", dumps["o_tok"].rearrange("p a b -> p (a b)")[:, tc * 4096:(tc + 1) * 4096],
                    aC[:, 12288:16384], waits=stageB_done))

            ot_toks = []
            for tt in range(4):
                sq_toks = []
                for hh in range(H0):
                    sq_toks.append(p.emit("scalar", lambda h, tt=tt, hh=hh: h.activation(
                        pslab[:, 1024 + hh * 64:1024 + hh * 64 + 64],
                        aC[:, 12288 + tt * 1024 + hh * 64:12288 + tt * 1024 + hh * 64 + 64],
                        AF.Square, accum_out=fscr[:, hh:hh + 1]),
                        waits=stageB_done + (ot_toks[-1:] if ot_toks else []), inc="sc"))
                r1 = p.emit("scalar", lambda h: h.activation(
                    fscr[:, 16:32], fscr[:, 0:16], AF.Sqrt, scale=1.0 / DH0,
                    bias=cc(C_EPS_RMS)), waits=sq_toks, inc="sc")
                r2 = p.emit("vector", lambda h: h.reciprocal(
                    fscr[:, 32:48], fscr[:, 16:32]), waits=[r1], inc="dv")
                og_toks = []
                for hh in range(H0):
                    og_toks.append(p.emit("vector", lambda h, tt=tt, hh=hh: h.tensor_scalar(
                        pslab[:, hh * 64:hh * 64 + 64],
                        aC[:, 12288 + tt * 1024 + hh * 64:12288 + tt * 1024 + hh * 64 + 64],
                        fscr[:, 32 + hh:32 + hh + 1], None, ALU.mult), waits=[r2], inc="dv"))
                sg = p.emit("scalar", lambda h, tt=tt: h.activation(
                    pslab[:, 1024:2048], aB[:, 4096 + tt * 1024:4096 + (tt + 1) * 1024],
                    AF.Silu), waits=sq_toks, inc="sc")
                og2 = p.emit("vector", lambda h, tt=tt: h.tensor_mul(
                    aC[:, 12288 + tt * 1024:12288 + (tt + 1) * 1024],
                    pslab[:, 0:1024], pslab[:, 1024:2048]),
                    waits=og_toks + [sg] + dump_hold_o, inc="dv")
                ot_toks.append(og2)
            ogt_toks = []
            for tt in range(4):
                tts = []
                for blk in range(NKC):
                    w_ = [ot_toks[tt]]
                    if blk == 0:
                        w_ = w_ + p.bwait(4)
                    tts.append(p.emit("tensor", lambda h, tt=tt, blk=blk: h.transpose(
                        psbf[4][:, blk * 128:blk * 128 + 128],
                        aC[:, 12288 + tt * 1024 + blk * 128:12288 + tt * 1024 + blk * 128 + 128],
                        ident[:]), waits=w_, inc="pe"))
                ev = p.emit("vector", lambda h, tt=tt: h.tensor_copy(
                    aB[:, 8192 + tt * 128:8192 + tt * 128 + 4096]
                    .rearrange("p (x y) -> p x y", x=8)[:, :, 0:128],
                    psbf[4][:, 0:1024].rearrange("p (x y) -> p x y", x=8)),
                    waits=[tts[-1]] + dump_hold_q, inc="dv")
                p.bread(4, ev)
                ogt_toks.append(ev)
            resid_toks = []
            slot_consumer = {}
            for blk in range(NKC):
                slot = blk % 2
                w_ = list(ogt_toks[-1:])
                if slot in slot_consumer:
                    w_.append(slot_consumer[slot])
                ldt = wload(
                    wb[:, 8192 + slot * 1024:8192 + slot * 1024 + 1024].rearrange(
                        "p (a b) -> p a b", a=8),
                    dramf["Wo"][:, :, blk * 128:(blk + 1) * 128], waits=w_)
                bank = blk % 2
                for kc in range(NKC):
                    w2_ = [ldt] + ogt_toks
                    if kc == 0:
                        w2_ += p.bwait(bank)
                    t_mm = p.emit("tensor", lambda h, blk=blk, kc=kc, bank=bank, slot=slot: h.matmul(
                        psum[bank][:, 0:512],
                        wb[:, 8192 + slot * 1024 + kc * 128:8192 + slot * 1024 + kc * 128 + 128],
                        aB[:, 8192 + kc * 512:8192 + kc * 512 + 512],
                        start=(kc == 0), stop=(kc == NKC - 1), skip_group_check=True),
                        waits=w2_, inc="pe" if kc == NKC - 1 else None)
                slot_consumer[slot] = t_mm
                rt = p.emit("vector", lambda h, blk=blk, bank=bank, t0=t0: h.tensor_add(
                    xres[:, blk, t0:t0 + 512], xres[:, blk, t0:t0 + 512],
                    psum[bank][:, 0:512]), waits=[t_mm], inc="dv")
                p.bread(bank, rt)
                resid_toks.append(rt)

            prev_tc_done = resid_toks + dump_toks + dump_hold_q + dump_hold_o + [
                ("dv", p.counts["dv"]), ("sc", p.counts["sc"])]

        attn_done = prev_tc_done

        moe_prev = attn_done
        w13 = dramf["w13"]
        w2d = dramf["w2"]
        for tc in range(4):
            t0 = tc * 512
            ldgate = wload(wb[:, 0:64].rearrange("p (a b) -> p a b", a=8),
                           dramf["gate_w"], waits=moe_prev)
            for kc in range(NKC):
                w_ = [ldgate] + (moe_prev if kc == 0 else []) + (p.bwait(6) if kc == 0 else [])
                gmm = p.emit("tensor", lambda h, kc=kc, t0=t0: h.matmul(
                    psum[6][0:8, 0:512], wb[:, kc * 8:kc * 8 + 8],
                    actA[:, kc, t0:t0 + 512],
                    start=(kc == 0), stop=(kc == NKC - 1), skip_group_check=True),
                    waits=w_, inc="pe" if kc == NKC - 1 else None)
            gev = p.emit("scalar", lambda h: h.copy(fscr[0:8, 0:512], psum[6][0:8, 0:512]),
                         waits=[gmm], inc="sc")
            p.bread(6, gev)
            sc_toks = []
            for tt in range(4):
                t_t = p.emit("tensor", lambda h, tt=tt: h.transpose(
                    psum[7][:, 0:8], fscr[0:8, tt * 128:tt * 128 + 128],
                    identf[0:8, 0:8]), waits=[gev] + p.bwait(7), inc="pe")
                lev = p.emit("scalar", lambda h, tt=tt: h.copy(
                    fscr[:, 512 + tt * 8:512 + tt * 8 + 8], psum[7][:, 0:8]),
                    waits=[t_t], inc="sc")
                p.bread(7, lev)
                c0 = 512 + tt * 8
                m1 = p.emit("vector", lambda h, c0=c0: h.reduce_max(
                    fscr[:, 544:545], fscr[:, c0:c0 + 8], axis=AX.X, negate=True),
                    waits=[lev], inc="dv")
                ex = p.emit("scalar", lambda h, c0=c0: h.activation(
                    fscr[:, 552 + 0:552 + 8], fscr[:, c0:c0 + 8], AF.Exp,
                    bias=fscr[:, 544:545], accum_out=fscr[:, 545:546]),
                    waits=[m1], inc="sc")
                rc = p.emit("vector", lambda h: h.reciprocal(
                    fscr[:, 546:547], fscr[:, 545:546]), waits=[ex], inc="dv")
                pr = p.emit("vector", lambda h: h.tensor_scalar(
                    fscr[:, 552:560], fscr[:, 552:560], fscr[:, 546:547], None, ALU.mult),
                    waits=[rc], inc="dv")
                mm1 = p.emit("vector", lambda h: h.reduce_max(
                    fscr[:, 544:545], fscr[:, 552:560], axis=AX.X), waits=[pr], inc="dv")
                mk1 = p.emit("vector", lambda h: h.tensor_scalar(
                    fscr[:, 560:568], fscr[:, 552:560], fscr[:, 544:545], None, ALU.is_ge),
                    waits=[mm1], inc="dv")
                p2 = p.emit("vector", lambda h: h.scalar_tensor_tensor(
                    fscr[:, 568:576], fscr[:, 560:568], cc(C_NEG1E30), fscr[:, 552:560],
                    ALU.mult, ALU.add), waits=[mk1], inc="dv")
                mm2 = p.emit("vector", lambda h: h.reduce_max(
                    fscr[:, 545:546], fscr[:, 568:576], axis=AX.X), waits=[p2], inc="dv")
                sel = p.emit("vector", lambda h: h.tensor_scalar(
                    fscr[:, 560:568], fscr[:, 552:560], fscr[:, 545:546], None, ALU.is_ge),
                    waits=[mm2], inc="dv")
                scl = p.emit("vector", lambda h: h.tensor_mul(
                    fscr[:, 552:560], fscr[:, 552:560], fscr[:, 560:568]),
                    waits=[sel], inc="dv")
                t_b = p.emit("tensor", lambda h, tt=tt: h.transpose(
                    psum[7][0:8, 8:136], fscr[:, 552:560], identf[:, 0:128]),
                    waits=[scl] + p.bwait(7), inc="pe")
                sev = p.emit("scalar", lambda h, tt=tt, t0=t0: h.copy(
                    misc[0:8, 2048 + t0 + tt * 128:2048 + t0 + tt * 128 + 128],
                    psum[7][0:8, 8:136]), waits=[t_b], inc="sc")
                p.bread(7, sev)
                sc_toks.append(sev)
            if tc == 0 and "moe_scale" in dumps:
                sc_toks.append(p.dma("sync", dumps["moe_scale"], misc[0:8, 2048:2560],
                                     waits=sc_toks))

            h_toks_all = []
            for e in range(8):
                xs0 = (e % 2) * 4096
                ld13k = []
                for kc in range(NKC):
                    ld13k.append(wload(
                        wb[:, kc * 1024:kc * 1024 + 1024], w13[e][:, kc, :],
                        waits=p.bwait(("w13", kc)) + (sc_toks if e == 0 else [])))
                rb = 6 + e % 2
                sel1 = identf[0:8, e:e + 1].to_broadcast((8, 128))
                t_r = p.emit("tensor", lambda h, t0=t0, sel1=sel1, rb=rb: h.matmul(
                    psum[rb][:, 0:512], sel1,
                    misc[0:8, 2048 + t0:2048 + t0 + 512],
                    start=True, stop=True), waits=sc_toks + p.bwait(rb), inc="pe")
                re_ = p.emit("scalar", lambda h, rb=rb, e=e: h.copy(
                    pslab[:, (e % 2) * 512:(e % 2) * 512 + 512], psum[rb][:, 0:512]),
                    waits=[t_r] + p.bwait(("sr", e % 2)), inc="sc")
                p.bread(rb, re_)
                xs_toks = []
                for kc in range(NKC):
                    xt = p.emit("vector", lambda h, kc=kc, t0=t0, xs0=xs0, e=e: h.tensor_mul(
                        aC[:, xs0 + kc * 512:xs0 + kc * 512 + 512],
                        actA[:, kc, t0:t0 + 512],
                        pslab[:, (e % 2) * 512:(e % 2) * 512 + 512]),
                        waits=[re_] + p.bwait(("xs", e % 2, kc)), inc="dv")
                    xs_toks.append(xt)
                p.bread(("sr", e % 2), xs_toks[-1])
                if tc == 0 and e == 0 and "moe_xs" in dumps:
                    xs_toks.append(p.dma(
                        "sync", dramf["dump_moe_xs"].rearrange("p a b -> p (a b)")[:, 0:4096],
                        aC[:, 0:4096], waits=xs_toks))
                ev_toks = []
                kc_last = {}
                for hblk in range(4):
                    b1 = hblk % 2
                    b3 = 2 + hblk % 2
                    for kc in range(NKC):
                        w_ = [ld13k[kc], xs_toks[kc]]
                        m1_ = p.emit("tensor", lambda h, hblk=hblk, kc=kc, b1=b1, xs0=xs0: h.matmul(
                            psum[b1][:, 0:512],
                            wb[:, kc * 1024 + hblk * 128:kc * 1024 + hblk * 128 + 128],
                            aC[:, xs0 + kc * 512:xs0 + kc * 512 + 512],
                            start=(kc == 0), stop=(kc == NKC - 1), skip_group_check=True),
                            waits=w_ + (p.bwait(b1) if kc == 0 else []),
                            inc="pe" if kc == NKC - 1 else None)
                        m3_ = p.emit("tensor", lambda h, hblk=hblk, kc=kc, b3=b3, xs0=xs0: h.matmul(
                            psum[b3][:, 0:512],
                            wb[:, kc * 1024 + 512 + hblk * 128:kc * 1024 + 512 + hblk * 128 + 128],
                            aC[:, xs0 + kc * 512:xs0 + kc * 512 + 512],
                            start=(kc == 0), stop=(kc == NKC - 1), skip_group_check=True),
                            waits=(p.bwait(b3) if kc == 0 else []) + w_,
                            inc="pe" if hblk == 3 or kc == NKC - 1 else None)
                        if hblk == 3:
                            kc_last[kc] = m3_
                    sl = p.emit("scalar", lambda h, hblk=hblk, b1=b1: h.activation(
                        pslab[:, 1024 + (hblk % 2) * 512:1024 + (hblk % 2) * 512 + 512],
                        psum[b1][:, 0:512], AF.Silu),
                        waits=[m1_] + p.bwait(("sil", hblk % 2)), inc="sc")
                    p.bread(b1, sl)
                    ht = p.emit("vector", lambda h, hblk=hblk, b3=b3, e=e: h.tensor_mul(
                        aB[:, e * 2048 + hblk * 512:e * 2048 + hblk * 512 + 512],
                        psum[b3][:, 0:512],
                        pslab[:, 1024 + (hblk % 2) * 512:1024 + (hblk % 2) * 512 + 512]),
                        waits=[m3_, sl], inc="dv")
                    p.bread(b3, ht)
                    p.bread(("sil", hblk % 2), ht)
                    ev_toks.append(ht)
                for kc in range(NKC):
                    p.bread(("w13", kc), kc_last[kc])
                for kc in range(NKC):
                    p.bread(("xs", e % 2, kc), ev_toks[-1])
                h_toks_all += ev_toks
            y_toks = []
            last_mm = {}
            for e in range(8):
                for hk in range(4):
                    par = (e * 4 + hk) % 2
                    ldt = wload(wb[:, 8192 + par * 1024:8192 + par * 1024 + 1024],
                                w2d[e][:, hk, :],
                                waits=p.bwait(("w2s", par)) + (h_toks_all[-1:] if e == 0 and hk == 0 else []))
                    for blk in range(NKC):
                        w_ = [ldt]
                        if e == 0 and hk == 0:
                            w_ += p.bwait(blk) + h_toks_all[-1:]
                        last_mm[blk] = p.emit("tensor", lambda h, blk=blk, e=e, hk=hk, par=par: h.matmul(
                            psum[blk][:, 0:512],
                            wb[:, 8192 + par * 1024 + blk * 128:8192 + par * 1024 + blk * 128 + 128],
                            aB[:, e * 2048 + hk * 512:e * 2048 + hk * 512 + 512],
                            start=(e == 0 and hk == 0), stop=(e == 7 and hk == 3),
                            skip_group_check=True),
                            waits=w_, inc="pe" if blk == NKC - 1 or (e == 7 and hk == 3) else None)
                    p.bread(("w2s", par), last_mm[NKC - 1])
            for blk in range(NKC):
                rt = p.emit("vector", lambda h, blk=blk, t0=t0: h.tensor_add(
                    xres[:, blk, t0:t0 + 512], xres[:, blk, t0:t0 + 512],
                    psum[blk][:, 0:512]), waits=[last_mm[blk]], inc="dv")
                p.bread(blk, rt)
                y_toks.append(rt)
            moe_prev = y_toks + [("dv", p.counts["dv"]), ("sc", p.counts["sc"])]

        moe_done = moe_prev
        if "x1" in dumps:
            moe_done = moe_done + [p.dma("sync", dumps["x1"], xres[:], waits=moe_done)]

        ln2_toks = layernorm(xres, actA, 0, moe_done, actC)
        ln2_done = [("dv", p.counts["dv"])]
        tst = p.dma("sync", misc[64:66, 0:2048], misc[0:1, 0:4096], waits=ln2_done)
        if "xn2" in dumps:
            ln2_done = ln2_done + [p.dma("sync", dumps["xn2"], actA[:], waits=ln2_done)]

        t_pw = []
        t_pw.append(p.dma("sync", misc[:, 2560:2624].rearrange("p (a b) -> p a b", a=16),
                          dramf["conv_w"], waits=ln2_done))
        t_pw.append(p.dma("sync", misc[:, 2624:2640], dramf["conv_b"], waits=ln2_done))
        t_pw.append(p.dma("sync", misc[:, 2640:2656], dramf["W_dt"], waits=ln2_done))
        t_pw.append(p.dma("sync", misc[:, 2656:2672], dramf["b_dt"], waits=ln2_done))
        t_pw.append(p.dma("sync", misc[:, 2672:2688], dramf["D_param"], waits=ln2_done))
        t_pw.append(p.dma("sync", misc[:, 2688:2944].rearrange("p (a b) -> p a b", a=16),
                          dramf["A_log"], waits=ln2_done))
        t_A = p.emit("scalar", lambda h: h.activation(
            misc[:, 2688:2944], misc[:, 2688:2944], AF.Exp), waits=t_pw, inc="sc")
        t_A = p.emit("vector", lambda h: h.tensor_scalar(
            misc[:, 2688:2944], misc[:, 2688:2944], -1.0, None, ALU.mult),
            waits=[t_A], inc="dv")
        ld_wx = wload(wb[:, 9216:9744].rearrange("p (a b) -> p a b", a=16),
                      dramf["W_xT"], waits=ln2_done)

        u_hbm = nc.dram_tensor("u_hbm", [128, NDC, T], BF16, kind="Internal").ap()
        sz_hbm = nc.dram_tensor("sz_hbm", [128, NDC, T], BF16, kind="Internal").ap()

        xn2 = actA
        uu_toks = []
        rows_mm = {}
        for blk in range(NDC):
            slot = blk % 4
            ldt = wload(wb[:, slot * 1024:slot * 1024 + 1024].rearrange(
                "p (a b) -> p a b", a=8),
                dramf["W_inT"][:, :, blk * 128:(blk + 1) * 128],
                waits=p.bwait(("wi", slot)) + (ln2_done if blk < 4 else []))
            xd0 = (blk % 2) * 2048
            ev_toks = []
            for tq in range(4):
                bank = 4 + ((blk + tq) % 4)
                for kc in range(NKC):
                    w_ = [ldt]
                    if kc == 0:
                        w_ += p.bwait(bank) + ln2_done
                    mm = p.emit("tensor", lambda h, kc=kc, tq=tq, bank=bank, slot=slot: h.matmul(
                        psum[bank][:, 0:512],
                        wb[:, slot * 1024 + kc * 128:slot * 1024 + kc * 128 + 128],
                        xn2[:, kc, tq * 512:tq * 512 + 512],
                        start=(kc == 0), stop=(kc == NKC - 1), skip_group_check=True),
                        waits=w_, inc="pe" if kc == NKC - 1 else None)
                ev = p.emit("scalar", lambda h, tq=tq, bank=bank, xd0=xd0: h.copy(
                    aB[:, xd0 + tq * 512:xd0 + tq * 512 + 512],
                    psum[bank][:, 0:512]), waits=[mm] + p.bwait(("xd", blk % 2)), inc="sc")
                p.bread(bank, ev)
                ev_toks.append(ev)
            p.bread(("wi", slot), ev_toks[-1])
            if blk == 15 and "xd15" in dumps:
                xdt = p.dma("sync", dumps["xd15"], aB[:, xd0:xd0 + 2048], waits=ev_toks)
                ev_toks.append(xdt)
            u0 = 4096 + (blk % 2) * 2048
            tu = None
            for hf in range(2):
                hb = hf * 1024
                acc = ((lambda c0, c1: misc[:, 2944 + c0:2944 + c1]) if hf == 0
                       else (lambda c0, c1: fscr[:, c0:c1]))
                prev = None
                for j in range(4):
                    lo = hb - 3 + j
                    srcs = max(lo, 0)
                    do = srcs - lo
                    n = 1024 - do
                    if j == 0:
                        if do > 0:
                            prev = p.emit("vector", lambda h, acc=acc, do=do: h.memset(
                                acc(0, do), 0.0),
                                waits=ev_toks + [t_A] + p.bwait(("cacc", hf)), inc="dv")
                        prev = p.emit("vector", lambda h, xd0=xd0, blk=blk, srcs=srcs, do=do, n=n, acc=acc: h.tensor_scalar(
                            acc(do, do + n), aB[:, xd0 + srcs:xd0 + srcs + n],
                            misc[:, 2560 + blk * 4:2560 + blk * 4 + 1], None, ALU.mult),
                            waits=([prev] if prev else ev_toks + [t_A] + p.bwait(("cacc", hf))),
                            inc="dv")
                    else:
                        prev = p.emit("vector", lambda h, j=j, xd0=xd0, blk=blk, srcs=srcs, do=do, n=n, acc=acc: h.scalar_tensor_tensor(
                            acc(do, do + n), aB[:, xd0 + srcs:xd0 + srcs + n],
                            misc[:, 2560 + blk * 4 + j:2560 + blk * 4 + j + 1],
                            acc(do, do + n), ALU.mult, ALU.add),
                            waits=[prev], inc="dv")
                tu = p.emit("scalar", lambda h, u0=u0, hb=hb, blk=blk, acc=acc: h.activation(
                    aB[:, u0 + hb:u0 + hb + 1024], acc(0, 1024), AF.Silu,
                    bias=misc[:, 2624 + blk:2624 + blk + 1]),
                    waits=[prev] + (p.bwait(("us", blk % 2)) + p.bwait(("usd", blk % 2)) if hf == 0 else []),
                    inc="sc")
                p.bread(("cacc", hf), tu)
            tdu = p.dma("sync", u_hbm[:, blk, :], aB[:, u0:u0 + 2048], waits=[tu])
            uu_toks.append(tdu)
            if "u_ssm" in dumps:
                udt = p.dma("sync", dramf["dump_u_ssm"][:, blk, :],
                            aB[:, u0:u0 + 2048], waits=[tu])
                uu_toks.append(udt)
                p.bread(("usd", blk % 2), udt)
            for tq in range(4):
                w_ = [tu, ld_wx]
                if blk == 0:
                    w_ += p.bwait(tq)
                rows_mm[tq] = p.emit("tensor", lambda h, blk=blk, tq=tq, u0=u0: h.matmul(
                    psum[tq][0:33, 0:512],
                    wb[:, 9216 + blk * 33:9216 + blk * 33 + 33],
                    aB[:, u0 + tq * 512:u0 + tq * 512 + 512],
                    start=(blk == 0), stop=(blk == NDC - 1), skip_group_check=True),
                    waits=w_, inc="pe")
            p.bread(("us", blk % 2), rows_mm[3])
            p.bread(("usd", blk % 2), tdu)
            p.bread(("xd", blk % 2), prev)
        row_toks = []
        for tq in range(4):
            rt = p.emit("scalar", lambda h, tq=tq: h.copy(
                misc[0:33, tq * 512:tq * 512 + 512], psum[tq][0:33, 0:512]),
                waits=[rows_mm[tq]] + p.bwait(("convdead",)), inc="sc")
            p.bread(tq, rt)
            row_toks.append(rt)
        if "ssm_in" in dumps:
            row_toks.append(p.dma("sync", dramf["dump_ssm_in"],
                                  misc[0:64, 0:2048], waits=row_toks))

        sz_toks = []
        for blk in range(NDC):
            slot = blk % 4
            ldt = wload(wb[:, slot * 1024:slot * 1024 + 1024].rearrange(
                "p (a b) -> p a b", a=8),
                dramf["W_inT"][:, :, DI + blk * 128:DI + (blk + 1) * 128],
                waits=p.bwait(("wi", slot)))
            z0 = 8192 + (blk % 2) * 2048
            ev_toks = []
            for tq in range(4):
                bank = 4 + ((blk + tq) % 4)
                for kc in range(NKC):
                    w_ = [ldt]
                    if kc == 0:
                        w_ += p.bwait(bank)
                    mm = p.emit("tensor", lambda h, kc=kc, tq=tq, bank=bank, slot=slot: h.matmul(
                        psum[bank][:, 0:512],
                        wb[:, slot * 1024 + kc * 128:slot * 1024 + kc * 128 + 128],
                        xn2[:, kc, tq * 512:tq * 512 + 512],
                        start=(kc == 0), stop=(kc == NKC - 1), skip_group_check=True),
                        waits=w_, inc="pe" if kc == NKC - 1 else None)
                ev = p.emit("scalar", lambda h, tq=tq, bank=bank, z0=z0: h.activation(
                    aB[:, z0 - 8192 + tq * 512:z0 - 8192 + tq * 512 + 512],
                    psum[bank][:, 0:512], AF.Silu),
                    waits=[mm] + p.bwait(("zs", blk % 2)) +
                    (row_toks + p.bwait(("xd", blk % 2)) if blk < 2 else []), inc="sc")
                p.bread(bank, ev)
                ev_toks.append(ev)
            p.bread(("wi", slot), ev_toks[-1])
            tdz = p.dma("sync", sz_hbm[:, blk, :], aB[:, (blk % 2) * 2048:(blk % 2) * 2048 + 2048],
                        waits=ev_toks)
            p.bread(("zs", blk % 2), tdz)
            sz_toks.append(tdz)

        t_spill = p.dma("sync", x_old_hbm, xres[:], waits=sz_toks + row_toks + uu_toks)

        xf32 = xres.rearrange("p a b -> p (a b)")
        xbf = xres.rearrange("p a b -> p (a b)").bitcast(BF16)
        aA = actA.rearrange("p a b -> p (a b)")
        t_state0 = p.emit("vector", lambda h: h.memset(misc[:, 3456:3712], 0.0),
                          waits=row_toks + [("sc", p.counts["sc"])], inc="dv")
        state_tok = {c: t_state0 for c in range(256)}
        cube_prev = [t_spill, t_state0] + ln2_done
        ssm_y_toks = []
        for tci in range(4):
            t0 = tci * 512
            t_lu = p.dma("sync", aB[:, 0:8192].rearrange("p (a b) -> p a b", a=16),
                         u_hbm[:, :, t0:t0 + 512], waits=cube_prev + p.bwait(("utc",)))
            t_lz = p.dma("sync", aB[:, 8192:16384].rearrange("p (a b) -> p a b", a=16),
                         sz_hbm[:, :, t0:t0 + 512], waits=cube_prev + p.bwait(("sztc",)))
            bc_toks = []
            for which, base in ((1, 0), (17, 8192)):
                for s in range(NS):
                    selr = identf[0:33, which + s:which + s + 1].to_broadcast((33, 128))
                    t_r = p.emit("tensor", lambda h, s=s, selr=selr, t0=t0: h.matmul(
                        psum[6 + s % 2][:, 0:512], selr, misc[0:33, t0:t0 + 512],
                        start=True, stop=True),
                        waits=row_toks + cube_prev + p.bwait(6 + s % 2), inc="pe")
                    t_e = p.emit("scalar", lambda h, s=s, base=base: h.copy(
                        aA[:, base + s * 512:base + s * 512 + 512],
                        psum[6 + s % 2][:, 0:512]),
                        waits=[t_r] + p.bwait(("bc", base, s)), inc="sc")
                    p.bread(6 + s % 2, t_e)
                    bc_toks.append(t_e)
            t_dr = p.emit("tensor", lambda h, t0=t0: h.matmul(
                psum[6][:, 0:512], onesr[:], misc[0:1, t0:t0 + 512],
                start=True, stop=True), waits=row_toks + p.bwait(6), inc="pe")
            t_de = p.emit("scalar", lambda h: h.copy(
                misc[:, 2048:2560], psum[6][:, 0:512]),
                waits=[t_dr] + p.bwait(("dtr",)), inc="sc")
            p.bread(6, t_de)

            wo_mm = {}

            def dprod_a(dc):
                zp = p.emit("vector", lambda h, dc=dc: h.tensor_scalar(
                    fscr[:, 0:512], misc[:, 2048:2560],
                    misc[:, 2640 + dc:2640 + dc + 1], misc[:, 2656 + dc:2656 + dc + 1],
                    ALU.mult, ALU.add),
                    waits=[t_de, t_A] + p.bwait(("zp",)) + p.bwait(("zab",)), inc="dv")
                abw = [zp] + p.bwait(("lns",))
                if tci == 0 and dc == 0:
                    abw = abw + cube_prev + [("pe", p.counts["pe"])]
                ab = p.emit("scalar", lambda h: h.activation(
                    pslab[:, 1536:2048], fscr[:, 0:512], AF.Abs), waits=abw, inc="sc")
                p.bread(("zab",), ab)
                ex = p.emit("scalar", lambda h: h.activation(
                    pslab[:, 1536:2048], pslab[:, 1536:2048], AF.Exp, scale=-1.0),
                    waits=[ab], inc="sc")
                ln_ = p.emit("scalar", lambda h: h.activation(
                    pslab[:, 1536:2048], pslab[:, 1536:2048], AF.Ln,
                    bias=cc(C_ONE)), waits=[ex], inc="sc")
                return zp, ln_

            def dprod_b(dc, zp, ln_):
                dl = p.emit("vector", lambda h: h.scalar_tensor_tensor(
                    misc[:, 2944:3456], fscr[:, 0:512], 0.0, pslab[:, 1536:2048],
                    ALU.max, ALU.add), waits=[ln_] + p.bwait(("dl",)), inc="dv")
                p.bread(("zp",), dl)
                p.bread(("lns",), dl)
                if tci == 0 and dc == 0 and "delta0" in dumps:
                    dtok = p.dma("sync", dramf["dump_delta0"][:, 0:512], misc[:, 2944:3456],
                                 waits=[dl])
                    p.bread(("dl",), dtok)
                tw = p.emit("vector", lambda h, dc=dc: h.tensor_mul(
                    pslab[:, 0:512], misc[:, 2944:3456],
                    aB[:, dc * 512:dc * 512 + 512]), waits=[dl, t_lu] + p.bwait(("wv",)), inc="dv")
                da_toks = []
                for s in range(NS):
                    da_toks.append(p.emit("scalar", lambda h, s=s, dc=dc: h.activation(
                        xf32[:, s * 512:s * 512 + 512], misc[:, 2944:3456], AF.Exp,
                        scale=misc[:, 2688 + dc * 16 + s:2688 + dc * 16 + s + 1]),
                        waits=[dl] + p.bwait(("dA", s)) +
                        (cube_prev if dc == 0 else []), inc="sc"))
                p.bread(("dl",), da_toks[-1])
                x2t = p.emit("vector", lambda h: h.tensor_tensor(
                    xbf[:, 16384:24576].rearrange("p (a b) -> p a b", a=16),
                    pslab[:, 0:512].unsqueeze(1).broadcast_to((128, 16, 512)),
                    aA[:, 0:8192].rearrange("p (a b) -> p a b", a=16),
                    ALU.mult), waits=[tw] + bc_toks, inc="dv")
                p.bread(("wv",), x2t)
                return da_toks, x2t, dl

            zpl = dprod_a(0)
            for dc in range(NDC):
                da_toks, x2t, dl = dprod_b(dc, *zpl)
                if dc + 1 < NDC:
                    zpl = dprod_a(dc + 1)
                sc_t = []
                for s in range(NS):
                    stc = dc * 16 + s
                    sc_t.append(p.emit("vector", lambda h, s=s, stc=stc: h.tensor_tensor_scan(
                        xbf[:, 24576 + s * 512:24576 + s * 512 + 512],
                        xf32[:, s * 512:s * 512 + 512],
                        xbf[:, 16384 + s * 512:16384 + s * 512 + 512],
                        misc[:, 3456 + stc:3456 + stc + 1],
                        ALU.mult, ALU.add),
                        waits=[x2t, da_toks[s], state_tok[stc]] + p.bwait(("H", s)), inc="dv"))
                    p.bread(("dA", s), sc_t[-1])
                st_up = p.emit("vector", lambda h, dc=dc: h.tensor_copy(
                    misc[:, 3456 + dc * 16:3456 + dc * 16 + 16],
                    xbf[:, 24576 + 511:24576 + 511 + 15 * 512 + 1:512] if False else
                    xbf[:, 24576:32768].rearrange("p (a b) -> p a b", a=16)[:, :, 511]),
                    waits=sc_t, inc="dv")
                for s in range(NS):
                    state_tok[dc * 16 + s] = st_up
                ldw = wload(wb[:, 8192 + (dc % 2) * 1024:8192 + (dc % 2) * 1024 + 1024],
                            dramf["W_outT"][:, dc, :], waits=p.bwait(("wo", dc % 2)))
                hct = p.emit("vector", lambda h: h.tensor_tensor(
                    xbf[:, 16384:24576].rearrange("p (a b) -> p a b", a=16),
                    xbf[:, 24576:32768].rearrange("p (a b) -> p a b", a=16),
                    aA[:, 8192:16384].rearrange("p (a b) -> p a b", a=16),
                    ALU.mult), waits=sc_t + [("dv", p.counts["dv"])] + bc_toks, inc="dv")
                tr1 = p.emit("vector", lambda h: h.tensor_add(
                    xbf[:, 24576:28672], xbf[:, 16384:20480], xbf[:, 20480:24576]),
                    waits=[hct], inc="dv")
                tr2 = p.emit("vector", lambda h: h.tensor_add(
                    xbf[:, 16384:18432], xbf[:, 24576:26624], xbf[:, 26624:28672]),
                    waits=[tr1], inc="dv")
                tr3 = p.emit("vector", lambda h: h.tensor_add(
                    xbf[:, 24576:25600], xbf[:, 16384:17408], xbf[:, 17408:18432]),
                    waits=[tr2], inc="dv")
                for s in range(NS):
                    p.bread(("H", s), tr3)
                tr4 = p.emit("vector", lambda h: h.tensor_add(
                    fscr[:, 512:1024], xbf[:, 24576:25088], xbf[:, 25088:25600]),
                    waits=[tr3], inc="dv")
                y1 = p.emit("vector", lambda h, dc=dc: h.scalar_tensor_tensor(
                    pslab[:, 512:1024], aB[:, dc * 512:dc * 512 + 512],
                    misc[:, 2672 + dc:2672 + dc + 1], fscr[:, 512:1024],
                    ALU.mult, ALU.add),
                    waits=[tr4, t_lu], inc="dv")
                yg = p.emit("vector", lambda h, dc=dc: h.tensor_mul(
                    pslab[:, 1024:1536], pslab[:, 512:1024],
                    aB[:, 8192 + dc * 512:8192 + dc * 512 + 512]),
                    waits=[y1, t_lz] + p.bwait(("yg",)), inc="dv")
                for blk in range(NKC):
                    w_ = [ldw, yg]
                    if dc == 0:
                        w_ += p.bwait(blk)
                    wo_mm[blk] = p.emit("tensor", lambda h, dc=dc, blk=blk: h.matmul(
                        psum[blk][:, 0:512],
                        wb[:, 8192 + (dc % 2) * 1024 + blk * 128:8192 + (dc % 2) * 1024 + blk * 128 + 128],
                        pslab[:, 1024:1536],
                        start=(dc == 0), stop=(dc == NDC - 1), skip_group_check=True),
                        waits=w_, inc="pe" if blk == NKC - 1 or dc == NDC - 1 else None)
                p.bread(("wo", dc % 2), wo_mm[NKC - 1])
                p.bread(("yg",), wo_mm[NKC - 1])
            t_lx = p.dma("sync", xf32[:, 0:4096].rearrange("p (a b) -> p a b", a=8),
                         x_old_hbm[:, :, t0:t0 + 512],
                         waits=[t_spill] + [wo_mm[b] for b in range(1)] + p.bwait(("xold",)))
            add_toks = []
            for blk in range(NKC):
                at = p.emit("vector", lambda h, blk=blk: h.tensor_add(
                    xf32[:, 4096 + blk * 512:4096 + blk * 512 + 512],
                    xf32[:, blk * 512:blk * 512 + 512], psum[blk][:, 0:512]),
                    waits=[t_lx, wo_mm[blk]] + p.bwait(("xnew", blk)), inc="dv")
                p.bread(blk, at)
                add_toks.append(at)
            t_sx = p.dma("sync", x_new_hbm[:, :, t0:t0 + 512],
                         xf32[:, 4096:8192].rearrange("p (a b) -> p a b", a=8),
                         waits=add_toks)
            if "y_ssm" in dumps:
                pass
            p.bread(("xold",), add_toks[-1])
            for blk in range(NKC):
                p.bread(("xnew", blk), t_sx)
            p.bread(("utc",), yg)
            p.bread(("sztc",), yg)
            p.bread(("bc", 0, 0), hct)
            for s in range(NS):
                p.bread(("bc", 0, s), hct)
                p.bread(("bc", 8192, s), hct)
            p.bread(("dtr",), dl)
            ssm_y_toks.append(t_sx)
            cube_prev = [t_sx, ("dv", p.counts["dv"]), ("sc", p.counts["sc"])]

        t_lold = p.dma("sync", xres[:], x_old_hbm, waits=cube_prev)
        rec_toks = []
        for tq in range(4):
            t0 = tq * 512
            selm = identf[64:66, 64:65].to_broadcast((2, 128))
            selr = identf[64:66, 65:66].to_broadcast((2, 128))
            tm = p.emit("tensor", lambda h, t0=t0, selm=selm: h.matmul(
                psum[6][:, 0:512], selm, misc[64:66, t0:t0 + 512],
                start=True, stop=True), waits=[tst, t_lold] + p.bwait(6), inc="pe")
            tr_ = p.emit("tensor", lambda h, t0=t0, selr=selr: h.matmul(
                psum[7][:, 0:512], selr, misc[64:66, t0:t0 + 512],
                start=True, stop=True), waits=[tst] + p.bwait(7), inc="pe")
            for kc in range(NKC):
                ts1 = p.emit("vector", lambda h, kc=kc, t0=t0: h.tensor_sub(
                    fscr[:, 0:512], xres[:, kc, t0:t0 + 512], psum[6][:, 0:512]),
                    waits=[tm, t_lold], inc="dv")
                ts2 = p.emit("vector", lambda h, kc=kc, t0=t0: h.tensor_mul(
                    actA[:, kc, t0:t0 + 512], fscr[:, 0:512], psum[7][:, 0:512]),
                    waits=[tr_], inc="dv")
                rec_toks.append(ts2)
            p.bread(6, rec_toks[-1])
            p.bread(7, rec_toks[-1])
        t_rx = p.dma("sync", xres[:], x_new_hbm, waits=rec_toks)
        if "x2" in dumps and False:
            pass
        ssm_done = [t_rx, ("dv", p.counts["dv"]), ("sc", p.counts["sc"])]
        if "y_ssm" in dumps:
            ssm_done = ssm_done + [p.dma("sync", dumps["y_ssm"], xres[:], waits=ssm_done)]
        if "xn2b" in dumps:
            ssm_done = ssm_done + [p.dma("sync", dumps["xn2b"], actA[:], waits=ssm_done)]

        aAx = actA.rearrange("p a b -> p (a b)")

        xh_toks = []
        sq_prev = {}
        for tq in range(4):
            t0 = tq * 512
            for kc in range(NKC):
                par = kc % 2
                w_ = ssm_done + ([sq_prev[par]] if par in sq_prev else [])
                sqt = p.emit("scalar", lambda h, kc=kc, t0=t0, par=par: h.activation(
                    pslab[:, par * 512:par * 512 + 512], actA[:, kc, t0:t0 + 512],
                    AF.Square), waits=w_, inc="sc")
                mmt = p.emit("tensor", lambda h, kc=kc, par=par, tq=tq: h.matmul(
                    psum[4][0:1, tq * 128:tq * 128 + 128] if False else
                    psum[4][0:1, 0:512], onesb[:, 0:1], pslab[:, par * 512:par * 512 + 512],
                    start=(kc == 0), stop=(kc == NKC - 1), skip_group_check=True),
                    waits=[sqt] + (p.bwait(4) if kc == 0 else []), inc="pe")
                sq_prev[par] = mmt
            r1 = p.emit("scalar", lambda h, t0=t0: h.activation(
                fscr[0:1, 0:512], psum[4][0:1, 0:512], AF.Sqrt), waits=[mmt], inc="sc")
            p.bread(4, r1)
            r2 = p.emit("vector", lambda h: h.tensor_scalar(
                fscr[0:1, 0:512], fscr[0:1, 0:512], 1e-12, None, ALU.max),
                waits=[r1], inc="dv")
            r3 = p.emit("vector", lambda h: h.reciprocal(
                fscr[0:1, 512:1024], fscr[0:1, 0:512]), waits=[r2], inc="dv")
            r4 = p.emit("vector", lambda h, t0=t0: h.tensor_scalar(
                misc[0:1, 3584 + t0 // 4 * 0 + 0:3584 + 512] if False else
                fscr[0:1, 512:1024], fscr[0:1, 512:1024], cc(C_SQRTD)[0:1, :], None,
                ALU.mult), waits=[r3], inc="dv")
            rb = p.emit("tensor", lambda h: h.matmul(
                psum[5][:, 0:512], onesr[:], fscr[0:1, 512:1024],
                start=True, stop=True), waits=[r4] + p.bwait(5), inc="pe")
            last = None
            for kc in range(NKC):
                last = p.emit("vector", lambda h, kc=kc, t0=t0: h.tensor_mul(
                    actB[:, kc, t0:t0 + 512], actA[:, kc, t0:t0 + 512],
                    psum[5][:, 0:512]), waits=[rb], inc="dv")
                xh_toks.append(last)
            p.bread(5, last)

        t_rt = p.dma("sync", misc[:, 3968:3976], dramf["q_rt"], waits=ssm_done)
        if "dbg1" in dumps:
            t_rt = p.dma("sync", dumps["dbg1"], misc[:, 3968:3976], waits=[t_rt] + ssm_done)
        if "dbg2" in dumps:
            t_rt = p.dma("sync", dumps["dbg2"], consts[:, :], waits=[t_rt])
        t_rtb = p.emit("vector", lambda h: h.tensor_copy(
            pslab[:, 1024:1032], misc[:, 3968:3976]), waits=[t_rt] + ssm_done, inc="dv")
        if "srow" in dumps:
            dtk = p.dma("sync", dramf["dump_srow"][2:3, 0:1024], misc[:, 3968:3976],
                        waits=[t_rt] + ssm_done)
            t_rtb = p.emit("vector", lambda h: h.tensor_copy(
                pslab[:, 1024:1032], misc[:, 3968:3976]), waits=[dtk], inc="dv")
        smm = None
        s_evs = []
        for tq in range(4):
            for kc in range(NKC):
                smm = p.emit("tensor", lambda h, tq=tq, kc=kc: h.matmul(
                    psum[6][0:1, 0:512],
                    pslab[:, 1024 + kc:1024 + kc + 1],
                    actA[:, kc, tq * 512:tq * 512 + 512],
                    start=(kc == 0), stop=(kc == NKC - 1), skip_group_check=True),
                    waits=[t_rtb] + ssm_done + (p.bwait(6) if kc == 0 else []), inc="pe")
            sev = p.emit("scalar", lambda h, tq=tq: h.copy(
                misc[0:1, 2048 + tq * 512:2048 + tq * 512 + 512], psum[6][0:1, 0:512]),
                waits=[smm], inc="sc")
            p.bread(6, sev)
            s_evs.append(sev)
        s_done = s_evs
        if "dbg3" in dumps:
            s_done = s_done + [p.dma("sync", dramf["dump_dbg3"][0:1, :],
                                     misc[0:1, 2048:4096], waits=s_done)]
        t_s16 = p.dma("sync", misc[0:16, 1024:1152], misc[0:1, 2048:4096], waits=s_done)
        t_smax = p.emit("vector", lambda h: h.reduce_max(
            fscr[0:1, 144:145], misc[0:1, 2048:4096], axis=AX.X), waits=s_done, inc="dv")
        t_na = p.emit("vector", lambda h: h.memset(
            fscr[0:1, 145:146],
            float(np.log(2048.0) - np.log(np.float32(NQ_SEL * 9.0 / 8.0)))),
            waits=s_done, inc="dv")
        tm0 = p.emit("vector", lambda h: h.tensor_tensor(
            fscr[0:1, 146:147], fscr[0:1, 144:145], fscr[0:1, 145:146],
            ALU.min), waits=[t_smax, t_na], inc="dv")
        tm0b = p.emit("vector", lambda h: h.tensor_scalar(
            fscr[0:1, 147:148], fscr[0:1, 146:147], -1.0, None, ALU.mult),
            waits=[tm0], inc="dv")
        trp0 = p.emit("tensor", lambda h: h.matmul(
            psum[7][0:16, 0:3], onesr[0:1, 0:16], fscr[0:1, 145:148],
            start=True, stop=True), waits=[tm0b] + p.bwait(7), inc="pe")
        tre0 = p.emit("scalar", lambda h: h.copy(
            fscr[0:16, 152:155], psum[7][0:16, 0:3]), waits=[trp0], inc="sc")
        p.bread(7, tre0)

        t_gl = p.dma("sync", fscr[:, 8:16], dramf["gamma_light"], waits=ssm_done)
        t_gh = p.dma("sync", fscr[:, 16:24], dramf["gamma_heavy"], waits=ssm_done)
        qk_toks = []
        vl_toks = []
        lql = wload(wb[:, 0:4096].rearrange("p (a b) -> p a b", a=8),
                    dramf["wq_light"], waits=ssm_done + p.bwait(("lw", 0)))
        wsc_toks = []
        for kc in range(NKC):
            wsc_toks.append(p.emit("vector", lambda h, kc=kc: h.tensor_scalar(
                wb[:, kc * 512:kc * 512 + 512], wb[:, kc * 512:kc * 512 + 512],
                fscr[:, 8 + kc:8 + kc + 1], 1.0 / 8.0, ALU.mult, ALU.mult),
                waits=[lql, t_gl], inc="dv"))
        for blk in range(4):
            for tq in range(4):
                bank = (blk + tq) % 4
                for kc in range(NKC):
                    w_ = wsc_toks + xh_toks
                    if kc == 0:
                        w_ = w_ + p.bwait(bank)
                    mmq = p.emit("tensor", lambda h, blk=blk, kc=kc, tq=tq, bank=bank: h.matmul(
                        psum[bank][:, 0:512],
                        wb[:, kc * 512 + blk * 128:kc * 512 + blk * 128 + 128],
                        actB[:, kc, tq * 512:tq * 512 + 512],
                        start=(kc == 0), stop=(kc == NKC - 1), skip_group_check=True),
                        waits=w_, inc="pe" if kc == NKC - 1 else None)
                ev = p.emit("scalar", lambda h, blk=blk, tq=tq, bank=bank: h.copy(
                    aC[:, blk * 2048 + tq * 512:blk * 2048 + tq * 512 + 512],
                    psum[bank][:, 0:512]), waits=[mmq], inc="sc")
                p.bread(bank, ev)
                qk_toks.append(ev)
        p.bread(("lw", 0), qk_toks[-1])
        lkv = wload(wb[:, 0:8192].rearrange("p (a b) -> p a b", a=8),
                    dramf["wkv_light"], waits=p.bwait(("lw", 0)) + ssm_done)
        wsc2 = []
        for kc in range(NKC):
            wsc2.append(p.emit("vector", lambda h, kc=kc: h.tensor_scalar(
                wb[:, kc * 1024:kc * 1024 + 1024], wb[:, kc * 1024:kc * 1024 + 1024],
                fscr[:, 8 + kc:8 + kc + 1], None, ALU.mult),
                waits=[lkv, t_gl], inc="dv"))
        for blk in range(4):
            for tq in range(4):
                bank = (blk + tq) % 4
                for kc in range(NKC):
                    w_ = wsc2 + xh_toks
                    if kc == 0:
                        w_ = w_ + p.bwait(bank)
                    mmk = p.emit("tensor", lambda h, blk=blk, kc=kc, tq=tq, bank=bank: h.matmul(
                        psum[bank][:, 0:512],
                        wb[:, kc * 1024 + blk * 128:kc * 1024 + blk * 128 + 128],
                        actB[:, kc, tq * 512:tq * 512 + 512],
                        start=(kc == 0), stop=(kc == NKC - 1), skip_group_check=True),
                        waits=w_, inc="pe" if kc == NKC - 1 else None)
                ev = p.emit("scalar", lambda h, blk=blk, tq=tq, bank=bank: h.copy(
                    aC[:, 8192 + blk * 2048 + tq * 512:8192 + blk * 2048 + tq * 512 + 512],
                    psum[bank][:, 0:512]), waits=[mmk], inc="sc")
                p.bread(bank, ev)
                qk_toks.append(ev)
        for tt in range(16):
            bank = 4 + tt % 2
            for kc in range(NKC):
                w_ = wsc2 + xh_toks + s_done + [t_s16]
                if kc == 0:
                    w_ = w_ + p.bwait(bank)
                mmv = p.emit("tensor", lambda h, tt=tt, kc=kc, bank=bank: h.matmul(
                    psum[bank][:, 0:512],
                    actB[:, kc, tt * 128:tt * 128 + 128],
                    wb[:, kc * 1024 + 512:kc * 1024 + 1024],
                    start=(kc == 0), stop=(kc == NKC - 1), skip_group_check=True),
                    waits=w_, inc="pe" if kc == NKC - 1 else None)
            evv = p.emit("scalar", lambda h, tt=tt, bank=bank: h.copy(
                aAx[:, tt * 512:tt * 512 + 512], psum[bank][:, 0:512]),
                waits=[mmv], inc="sc")
            p.bread(bank, evv)
            vl_toks.append(evv)
        p.bread(("lw", 0), vl_toks[-1])
        p.bread(("lw", 1), vl_toks[-1])

        t_sxr = p.emit("tensor", lambda h: h.matmul(
            psum[7][0:16, 5:6], onesr[0:1, 0:16], fscr[0:1, 144:145],
            start=True, stop=True), waits=[t_smax] + p.bwait((7, "c")), inc="pe")
        t_sxc = p.emit("scalar", lambda h: h.copy(
            fscr[0:16, 150:151], psum[7][0:16, 5:6]), waits=[t_sxr], inc="sc")
        p.bread((7, "c"), t_sxc)
        light_o_toks = []
        lo_fm_toks = []
        coor_last = [tre0, t_sxc]
        for it in range(50):
            if 16 <= it < 48:
                for k2 in range(2):
                    li = (it - 16) * 2 + k2
                    tt, blk = li // 4, li % 4
                    t_t = p.emit("tensor", lambda h, tt=tt, blk=blk: h.transpose(
                        psbf[6][:, 0:128],
                        aAx[:, 8192 + tt * 512 + blk * 128:8192 + tt * 512 + blk * 128 + 128],
                        ident[:]), waits=light_o_toks[-1:] + p.bwait(6), inc="pe")
                    ev = p.emit("scalar", lambda h, tt=tt, blk=blk: h.copy(
                        aAx[:, blk * 2048 + tt * 128:blk * 2048 + tt * 128 + 128],
                        psbf[6][:, 0:128]), waits=[t_t], inc="sc")
                    p.bread(6, ev)
                    lo_fm_toks.append(ev)
            if it > 0:
                ta_ = p.emit("vector", lambda h: h.tensor_scalar(
                    misc[0:16, 1152:1280], misc[0:16, 1024:1152],
                    fscr[0:16, 152:153], None, ALU.min),
                    waits=coor_last + [t_s16], inc="dv")
                te_ = p.emit("scalar", lambda h: h.activation(
                    misc[0:16, 1280:1408], misc[0:16, 1152:1280], AF.Exp,
                    bias=fscr[0:16, 154:155], accum_out=fscr[0:16, 155:156]),
                    waits=[ta_], inc="sc")
                tps = p.emit("tensor", lambda h: h.matmul(
                    psum[7][0:16, 4:5],
                    consts[0:16, C_ONE:C_ONE + 1].to_broadcast((16, 16)),
                    fscr[0:16, 155:156], start=True, stop=True),
                    waits=[te_] + p.bwait((7, "b")), inc="pe")
                tln = p.emit("scalar", lambda h: h.activation(
                    fscr[0:16, 149:150], psum[7][0:16, 4:5], AF.Ln), waits=[tps], inc="sc")
                p.bread((7, "b"), tln)
                t12 = p.emit("vector", lambda h: h.scalar_tensor_tensor(
                    fscr[0:16, 152:153], fscr[0:16, 153:154], fscr[0:16, 149:150],
                    consts[0:16, C_LOGK:C_LOGK + 1], ALU.add, ALU.subtract),
                    waits=[tln], inc="dv")
                t3 = p.emit("vector", lambda h: h.tensor_tensor(
                    fscr[0:16, 153:154], fscr[0:16, 150:151], fscr[0:16, 152:153],
                    ALU.min), waits=[t12], inc="dv")
                coor_last = [p.emit("vector", lambda h: h.tensor_scalar(
                    fscr[0:16, 154:155], fscr[0:16, 153:154], -1.0, None, ALU.mult),
                    waits=[t3], inc="dv")]
            if it < 16:
                wn = it
                wt0 = wn * 128
                for hh in range(H1):
                    hp0 = (hh % 2) * 64
                    hblk = hh // 2
                    g2 = (it * H1 + hh) % 4
                    sb_ = g2
                    fc = 24 + g2 * 4
                    t_s = p.emit("tensor", lambda h, hp0=hp0, hblk=hblk, wt0=wt0, sb_=sb_: h.matmul(
                        psum[sb_][:, 0:128],
                        aC[hp0:hp0 + 64, hblk * 2048 + wt0:hblk * 2048 + wt0 + 128],
                        aC[hp0:hp0 + 64, 8192 + hblk * 2048 + wt0:8192 + hblk * 2048 + wt0 + 128],
                        start=True, stop=True), waits=qk_toks + p.bwait(sb_), inc="pe")
                    t_mx = p.emit("vector", lambda h, sb_=sb_, fc=fc: h.reduce_max(
                        fscr[:, fc:fc + 1], psum[sb_][:, 0:128], axis=AX.X, negate=True),
                        waits=[t_s] + p.bwait(("lmx", g2)), inc="dv")
                    t_ex = p.emit("scalar", lambda h, sb_=sb_, fc=fc, g2=g2: h.activation(
                        pslab[:, g2 * 128:g2 * 128 + 128],
                        psum[sb_][:, 0:128], AF.Exp,
                        bias=fscr[:, fc:fc + 1], accum_out=fscr[:, fc + 1:fc + 2]),
                        waits=[t_mx] + p.bwait(("lps", g2)), inc="sc")
                    p.bread(sb_, t_ex)
                    p.bread(("lmx", g2), t_ex)
                    t_rc = p.emit("vector", lambda h, fc=fc: h.reciprocal(
                        fscr[:, fc + 2:fc + 3], fscr[:, fc + 1:fc + 2]),
                        waits=[t_ex] + p.bwait(("lrc", g2)), inc="dv")
                    t_tp = p.emit("tensor", lambda h, sb_=sb_, g2=g2: h.transpose(
                        psbf[sb_][:, 256:384], pslab[:, g2 * 128:g2 * 128 + 128],
                        ident[:]), waits=[t_ex] + p.bwait((sb_, "pt")), inc="pe")
                    t_pe = p.emit("scalar", lambda h, sb_=sb_, g2=g2: h.copy(
                        pslab[:, 512 + g2 * 128:512 + g2 * 128 + 128],
                        psbf[sb_][:, 256:384]), waits=[t_tp] + p.bwait(("lpt", g2)), inc="sc")
                    p.bread((sb_, "pt"), t_pe)
                    p.bread(("lps", g2), t_pe)
                    t_av = p.emit("tensor", lambda h, wn=wn, hh=hh, sb_=sb_, g2=g2: h.matmul(
                        psum[sb_][:, 128:192],
                        pslab[:, 512 + g2 * 128:512 + g2 * 128 + 128],
                        aAx[:, wn * 512 + hh * 64:wn * 512 + hh * 64 + 64],
                        start=True, stop=True), waits=[t_pe] + vl_toks, inc="pe")
                    p.bread(("lpt", g2), t_av)
                    t_oe = p.emit("scalar", lambda h, sb_=sb_, fc=fc, g2=g2: h.activation(
                        pslab[:, 1024 + g2 * 64:1024 + g2 * 64 + 64],
                        psum[sb_][:, 128:192], AF.Copy,
                        scale=fscr[:, fc + 2:fc + 3]),
                        waits=[t_av, t_rc] + p.bwait(("lo", g2)), inc="sc")
                    p.bread(("lrc", g2), t_oe)
                    t_oc2 = p.emit("vector", lambda h, wn=wn, hh=hh, g2=g2: h.tensor_copy(
                        aAx[:, 8192 + wn * 512 + hh * 64:8192 + wn * 512 + hh * 64 + 64],
                        pslab[:, 1024 + g2 * 64:1024 + g2 * 64 + 64]),
                        waits=[t_oe], inc="dv")
                    p.bread(("lo", g2), t_oc2)
                    light_o_toks.append(t_oc2)
        coor_done = coor_last
        light_done = light_o_toks[-1:]

        lf = [("sc", p.counts["sc"]), ("dv", p.counts["dv"])]

        kx = p.emit("vector", lambda h: h.tensor_scalar(
            misc[0:1, 0:2048], misc[0:1, 2048:4096], fscr[0:1, 152:153], 0.0,
            ALU.subtract, ALU.min), waits=coor_done + s_done, inc="dv")
        t_kT = None
        for c_ in range(16):
            tt_ = p.emit("tensor", lambda h, c_=c_: h.transpose(
                psum[5][:, 0:1], misc[0:1, c_ * 128:c_ * 128 + 128],
                identf[0:1, 0:1]), waits=[kx] + p.bwait(5), inc="pe")
            t_kT = p.emit("scalar", lambda h, c_=c_: h.copy(
                fscr[:, 32 + c_:32 + c_ + 1], psum[5][:, 0:1]), waits=[tt_], inc="sc")
            p.bread(5, t_kT)
        if "dbg3" in dumps:
            kx = p.dma("sync", dramf["dump_dbg3"][1:2, :], misc[0:1, 0:2048], waits=[kx])
        zz_ = p.emit("vector", lambda h: h.memset(fscr[0:1, 512:1024], 0.0),
                     waits=[kx], inc="dv")
        pz_ = p.emit("vector", lambda h: h.memset(misc[0:1, 2048:2049], 0.0),
                     waits=[kx, t_kT] + s_done + [t_s16, t_smax], inc="dv")
        prev_pref = pz_
        for ch in range(4):
            c0 = ch * 512
            n = 512 if ch < 3 else 511
            satc = p.emit("vector", lambda h, c0=c0: h.tensor_scalar(
                fscr[0:1, 0:512], misc[0:1, c0:c0 + 512], 0.0, None, ALU.is_ge),
                waits=[prev_pref], inc="dv")
            prev_pref = p.emit("vector", lambda h, c0=c0, n=n: h.tensor_tensor_scan(
                misc[0:1, 2048 + c0 + 1:2048 + c0 + 1 + n],
                fscr[0:1, 512:512 + n], fscr[0:1, 0:n],
                misc[0:1, 2048 + c0:2048 + c0 + 1],
                ALU.add, ALU.add), waits=[satc, zz_], inc="dv")
        if "dbg3" in dumps:
            prev_pref = p.dma("sync", dramf["dump_dbg3"][2:3, :], misc[0:1, 2048:4096],
                              waits=[prev_pref])
        t_pT = None
        for c_ in range(16):
            tt_ = p.emit("tensor", lambda h, c_=c_: h.transpose(
                psum[5][:, 0:1], misc[0:1, 2048 + c_ * 128:2048 + c_ * 128 + 128],
                identf[0:1, 0:1]), waits=[prev_pref] + p.bwait(5), inc="pe")
            t_pT = p.emit("scalar", lambda h, c_=c_: h.copy(
                fscr[:, 48 + c_:48 + c_ + 1], psum[5][:, 0:1]), waits=[tt_], inc="sc")
            p.bread(5, t_pT)
        cnt_toks = []
        for itile in range(16):
            acc = None
            for ch in range(4):
                t_r = p.emit("tensor", lambda h, ch=ch: h.matmul(
                    psum[4][:, 0:512], onesr[:], misc[0:1, ch * 512:ch * 512 + 512],
                    start=True, stop=True), waits=[kx] + p.bwait(4), inc="pe")
                t_c = p.emit("vector", lambda h, itile=itile, ch=ch: h.tensor_scalar(
                    fscr[:, 64 + ch * 512 // 4:64 + ch * 512 // 4] if False else
                    pslab[:, 0:512].bitcast(F32)[:, 0:256] if False else
                    fscr[:, 512:1024][:, ch * 128:ch * 128 + 128] if False else
                    fscr[:, 512:1024],
                    psum[4][:, 0:512], fscr[:, 32 + itile:32 + itile + 1], None,
                    ALU.is_gt), waits=[t_r, t_kT], inc="dv")
                p.bread(4, t_c)
                acc = p.emit("vector", lambda h, itile=itile, ch=ch: h.reduce_sum(
                    fscr[:, 64 + itile:64 + itile + 1] if ch == 0 else
                    fscr[:, 80 + itile:80 + itile + 1],
                    fscr[:, 512:1024], axis=AX.X), waits=[t_c], inc="dv")
                if ch > 0:
                    acc = p.emit("vector", lambda h, itile=itile: h.tensor_add(
                        fscr[:, 64 + itile:64 + itile + 1],
                        fscr[:, 64 + itile:64 + itile + 1],
                        fscr[:, 80 + itile:80 + itile + 1]), waits=[acc], inc="dv")
            eqi = p.emit("vector", lambda h, itile=itile: h.tensor_scalar(
                fscr[:, 96 + itile:96 + itile + 1] if False else fscr[:, 80 + itile:80 + itile + 1],
                fscr[:, 32 + itile:32 + itile + 1], 0.0, None, ALU.is_ge),
                waits=[acc, t_kT], inc="dv")
            tot = p.emit("vector", lambda h, itile=itile: h.scalar_tensor_tensor(
                fscr[:, 64 + itile:64 + itile + 1],
                fscr[:, 80 + itile:80 + itile + 1],
                fscr[:, 48 + itile:48 + itile + 1],
                fscr[:, 64 + itile:64 + itile + 1], ALU.mult, ALU.add),
                waits=[eqi, t_pT], inc="dv")
            msk = p.emit("vector", lambda h, itile=itile: h.tensor_scalar(
                fscr[:, 96 + itile:96 + itile + 1],
                fscr[:, 64 + itile:64 + itile + 1], float(NQ_SEL), None, ALU.is_lt),
                waits=[tot], inc="dv")
            cnt_toks.append(msk)
        if "mask_q" in dumps:
            cnt_toks.append(p.dma("sync", dumps["mask_q"], fscr[:, 96:112], waits=cnt_toks))
        if "route" in dumps:
            cnt_toks.append(p.dma("sync", dumps["route"], fscr[:, 0:144], waits=cnt_toks))
        if "srow" in dumps:
            cnt_toks.append(p.dma("sync", dramf["dump_srow"][0:1, :], misc[0:1, 2048:4096],
                                  waits=[prev_pref]))
            cnt_toks.append(p.dma("sync", dramf["dump_srow"][1:2, :], misc[0:1, 0:2048],
                                  waits=[kx] + cnt_toks))
        if "light_o" in dumps:
            cnt_toks.append(p.dma("sync",
                dramf["dump_light_o"].rearrange("p a b -> p (a b)"),
                aAx[:, 8192:16384], waits=light_o_toks))

        ld_wol = wload(wb[:, 0:4096].rearrange("p (a b) -> p a b", a=4),
                       dramf["wo_light"], waits=p.bwait(("lw", 0)))
        wol_adds = []
        for tq in range(4):
            last = {}
            for blk in range(NKC):
                for hk in range(4):
                    w_ = [ld_wol] + lo_fm_toks
                    if hk == 0:
                        w_ = w_ + p.bwait(blk)
                        if blk in (4, 5, 7):
                            w_ = w_ + cnt_toks[-1:] + coor_done
                    last[blk] = p.emit("tensor", lambda h, blk=blk, hk=hk, tq=tq: h.matmul(
                        psum[blk][:, 0:512],
                        wb[:, hk * 1024 + blk * 128:hk * 1024 + blk * 128 + 128],
                        aAx[:, hk * 2048 + tq * 512:hk * 2048 + tq * 512 + 512],
                        start=(hk == 0), stop=(hk == 3), skip_group_check=True),
                        waits=w_, inc="pe" if hk == 3 else None)
                at = p.emit("vector", lambda h, blk=blk, tq=tq: h.tensor_add(
                    xres[:, blk, tq * 512:tq * 512 + 512],
                    xres[:, blk, tq * 512:tq * 512 + 512],
                    psum[blk][:, 0:512]), waits=[last[blk]], inc="dv")
                p.bread(blk, at)
                wol_adds.append(at)
        p.bread(("lw", 0), wol_adds[-1])

        hv_toks = []
        for wname, gscale, dst, tok_major in (
                ("wq_heavy", "q", 0, False), ("WkT_heavy", "k", 8192, False),
                ("WvT_heavy", "v", None, True)):
            ldh = wload(wb[:, 4096:8192].rearrange("p (a b) -> p a b", a=8),
                        dramf[wname], waits=lf + p.bwait(("hw",)))
            sc_toks2 = []
            for kc in range(NKC):
                if gscale == "q":
                    tsw = p.emit("vector", lambda h, kc=kc: h.tensor_scalar(
                        wb[:, 4096 + kc * 512:4096 + kc * 512 + 512],
                        wb[:, 4096 + kc * 512:4096 + kc * 512 + 512],
                        fscr[:, 16 + kc:16 + kc + 1], 1.0 / 8.0, ALU.mult, ALU.mult),
                        waits=[ldh, t_gh], inc="dv")
                else:
                    tsw = p.emit("vector", lambda h, kc=kc: h.tensor_scalar(
                        wb[:, 4096 + kc * 512:4096 + kc * 512 + 512],
                        wb[:, 4096 + kc * 512:4096 + kc * 512 + 512],
                        fscr[:, 16 + kc:16 + kc + 1], None, ALU.mult),
                        waits=[ldh, t_gh], inc="dv")
                sc_toks2.append(tsw)
            this = []
            if not tok_major:
                for blk in range(4):
                    for tq in range(4):
                        bank = 4 + (blk + tq) % 2
                        for kc in range(NKC):
                            w_ = sc_toks2 + xh_toks
                            if kc == 0:
                                w_ = w_ + p.bwait(bank)
                            mmq = p.emit("tensor", lambda h, blk=blk, kc=kc, tq=tq, bank=bank: h.matmul(
                                psum[bank][:, 0:512],
                                wb[:, 4096 + kc * 512 + blk * 128:4096 + kc * 512 + blk * 128 + 128],
                                actB[:, kc, tq * 512:tq * 512 + 512],
                                start=(kc == 0), stop=(kc == NKC - 1), skip_group_check=True),
                                waits=w_, inc="pe" if kc == NKC - 1 else None)
                        ev = p.emit("scalar", lambda h, dst=dst, blk=blk, tq=tq, bank=bank: h.copy(
                            aC[:, dst + blk * 2048 + tq * 512:dst + blk * 2048 + tq * 512 + 512],
                            psum[bank][:, 0:512]), waits=[mmq] + (wol_adds if dst == 0 else []),
                            inc="sc")
                        p.bread(bank, ev)
                        this.append(ev)
            else:
                for tt in range(16):
                    bank = 4 + tt % 2
                    for kc in range(NKC):
                        w_ = sc_toks2 + xh_toks
                        if kc == 0:
                            w_ = w_ + p.bwait(bank)
                        mmv = p.emit("tensor", lambda h, tt=tt, kc=kc, bank=bank: h.matmul(
                            psum[bank][:, 0:512],
                            actB[:, kc, tt * 128:tt * 128 + 128],
                            wb[:, 4096 + kc * 512:4096 + kc * 512 + 512],
                            start=(kc == 0), stop=(kc == NKC - 1), skip_group_check=True),
                            waits=w_, inc="pe" if kc == NKC - 1 else None)
                    evv = p.emit("scalar", lambda h, tt=tt, bank=bank: h.copy(
                        aAx[:, 8192 + tt * 512:8192 + tt * 512 + 512], psum[bank][:, 0:512]),
                        waits=[mmv] + lo_fm_toks, inc="sc")
                    p.bread(bank, evv)
                    this.append(evv)
            p.bread(("hw",), this[-1])
            hv_toks += this

        def hP_ap(gp_, c0, c1):
            return pslab[:, c0:c1] if gp_ == 0 else actB[:, 0, c0:c1]

        def hT_ap(gp_, c0, c1):
            return actB[:, 1 + gp_, c0:c1]

        ho_toks = []
        hgrp = {}

        def hfront(gi):
            hh = gi // 16
            qt = gi % 16
            gp_ = gi % 2
            b0 = gp_ * 4
            hp0 = (hh % 2) * 64
            hblk = hh // 2
            smm_t = []
            for ch in range(4):
                w_ = hv_toks + p.bwait(b0 + ch)
                smm_t.append(p.emit("tensor", lambda h, ch=ch, hp0=hp0, hblk=hblk, qt=qt, b0=b0: h.matmul(
                    psum[b0 + ch][:, 0:512],
                    aC[hp0:hp0 + 64, hblk * 2048 + qt * 128:hblk * 2048 + qt * 128 + 128],
                    aC[hp0:hp0 + 64, 8192 + hblk * 2048 + ch * 512:8192 + hblk * 2048 + ch * 512 + 512],
                    start=True, stop=True), waits=w_, inc="pe"))
            mx_t = []
            for ch in range(4):
                mx_t.append(p.emit("vector", lambda h, ch=ch, b0=b0, gp_=gp_: h.reduce_max(
                    fscr[:, 112 + gp_ * 8 + ch:112 + gp_ * 8 + ch + 1],
                    psum[b0 + ch][:, 0:512], axis=AX.X, negate=True),
                    waits=[smm_t[ch]], inc="dv"))
            mxf = p.emit("vector", lambda h, gp_=gp_: h.tensor_reduce(
                fscr[:, 128 + gp_:128 + gp_ + 1], fscr[:, 112 + gp_ * 8:112 + gp_ * 8 + 4],
                AX.X, ALU.min), waits=mx_t + p.bwait(("hmxf", gp_)), inc="dv")
            ex_t = []
            for ch in range(4):
                ex_t.append(p.emit("scalar", lambda h, ch=ch, b0=b0, gp_=gp_: h.activation(
                    hP_ap(gp_, ch * 512, ch * 512 + 512), psum[b0 + ch][:, 0:512], AF.Exp,
                    bias=fscr[:, 128 + gp_:128 + gp_ + 1],
                    accum_out=fscr[:, 130 + gp_ * 4 + ch:130 + gp_ * 4 + ch + 1]),
                    waits=[mxf] + p.bwait(("hP", gp_, ch)) +
                    (p.bwait(("hac", gp_)) if ch == 0 else []), inc="sc"))
                p.bread(b0 + ch, ex_t[-1])
            p.bread(("hmxf", gp_), ex_t[-1])
            hgrp[gi] = ex_t

        def hfront2(gi):
            qt = gi % 16
            gp_ = gi % 2
            ex_t = hgrp[gi]
            sm1 = p.emit("vector", lambda h, gp_=gp_: h.reduce_sum(
                fscr[:, 138 + gp_:138 + gp_ + 1], fscr[:, 130 + gp_ * 4:130 + gp_ * 4 + 4],
                axis=AX.X), waits=ex_t, inc="dv")
            p.bread(("hac", gp_), sm1)
            rc1 = p.emit("vector", lambda h, gp_=gp_: h.reciprocal(
                fscr[:, 140 + gp_:140 + gp_ + 1], fscr[:, 138 + gp_:138 + gp_ + 1]),
                waits=[sm1], inc="dv")
            scl = p.emit("vector", lambda h, qt=qt, gp_=gp_: h.tensor_mul(
                fscr[:, 142 + gp_:142 + gp_ + 1], fscr[:, 140 + gp_:140 + gp_ + 1],
                fscr[:, 96 + qt:96 + qt + 1]),
                waits=[rc1] + cnt_toks + p.bwait(("hscl", gp_)), inc="dv")
            hgrp[gi] = (ex_t, scl)

        def hback(gi):
            hh = gi // 16
            qt = gi % 16
            gp_ = gi % 2
            b0 = gp_ * 4
            ex_t, scl = hgrp.pop(gi)
            ev_half = []
            for half in range(2):
                tts = []
                for j in range(8):
                    kvt = half * 8 + j
                    w_ = [ex_t[kvt // 4]]
                    if j == 0:
                        w_ = w_ + p.bwait(b0 + half)
                    tts.append(p.emit("tensor", lambda h, j=j, kvt=kvt, b0=b0, half=half, gp_=gp_: h.transpose(
                        psbf[b0 + half][:, j * 128:j * 128 + 128],
                        hP_ap(gp_, kvt * 128, kvt * 128 + 128), ident[:]),
                        waits=w_, inc="pe"))
                for ch in (2 * half, 2 * half + 1):
                    p.bread(("hP", gp_, ch), tts[-1])
                ev = p.emit("vector", lambda h, b0=b0, half=half, gp_=gp_: h.tensor_copy(
                    hT_ap(gp_, half * 1024, half * 1024 + 1024),
                    psbf[b0 + half][:, 0:1024]),
                    waits=[tts[-1]] + p.bwait(("hT", gp_, half)), inc="dv")
                p.bread(b0 + half, ev)
                ev_half.append(ev)
            av_last = None
            for kvt in range(16):
                half = kvt // 8
                w_ = [ev_half[half]]
                if kvt == 0:
                    w_ = w_ + p.bwait(b0 + 2)
                av_last = p.emit("tensor", lambda h, kvt=kvt, hh=hh, b0=b0, gp_=gp_: h.matmul(
                    psum[b0 + 2][:, 0:64],
                    hT_ap(gp_, kvt * 128, kvt * 128 + 128),
                    aAx[:, 8192 + kvt * 512 + hh * 64:8192 + kvt * 512 + hh * 64 + 64],
                    start=(kvt == 0), stop=(kvt == 15), skip_group_check=True),
                    waits=w_, inc="pe" if kvt in (7, 15) else None)
                if kvt == 7:
                    p.bread(("hT", gp_, 0), av_last)
            p.bread(("hT", gp_, 1), av_last)
            t_oe = p.emit("scalar", lambda h, qt=qt, hh=hh, b0=b0, gp_=gp_: h.activation(
                wb[:, qt * 512 + hh * 64:qt * 512 + hh * 64 + 64],
                psum[b0 + 2][:, 0:64], AF.Copy, scale=fscr[:, 142 + gp_:142 + gp_ + 1]),
                waits=[av_last, scl] + (p.bwait(("lw", 0)) if gi == 0 else []) +
                (p.bwait(("lw", 1)) if gi == 0 else []), inc="sc")
            p.bread(("hscl", gp_), t_oe)
            p.bread(b0 + 2, t_oe)
            ho_toks.append(t_oe)

        for gi in range(129):
            if gi < 128:
                hfront(gi)
            if gi >= 1:
                hback(gi - 1)
            if gi < 128:
                hfront2(gi)

        ho_fm_toks = []
        for blk in range(4):
            for tg in range(2):
                bank = 4 + (blk * 2 + tg) % 2
                tts = []
                for j in range(8):
                    tt = tg * 8 + j
                    w_ = ho_toks if (blk == 0 and tg == 0 and j == 0) else []
                    if j == 0:
                        w_ = w_ + p.bwait(bank)
                    tts.append(p.emit("tensor", lambda h, tt=tt, blk=blk, j=j, bank=bank: h.transpose(
                        psbf[bank][:, j * 128:j * 128 + 128],
                        wb[:, tt * 512 + blk * 128:tt * 512 + blk * 128 + 128],
                        ident[:]), waits=w_, inc="pe"))
                ev = p.emit("vector", lambda h, blk=blk, tg=tg, bank=bank: h.tensor_copy(
                    aC[:, blk * 2048 + tg * 1024:blk * 2048 + tg * 1024 + 1024],
                    psbf[bank][:, 0:1024]),
                    waits=[tts[-1]] + ([("sc", p.counts["sc"]), ("dv", p.counts["dv"])]
                                       if blk == 0 and tg == 0 else []), inc="dv")
                p.bread(bank, ev)
                ho_fm_toks.append(ev)
        woh_adds = []
        ld_prev = {}
        for tq in range(4):
            last = {}
            for hk in range(4):
                par = hk % 2
                ldt = wload(wb[:, 8192 + par * 1024:8192 + par * 1024 + 1024],
                            dramf["wo_heavy"][:, hk, :],
                            waits=p.bwait(("whs", par)) + (ho_fm_toks[-1:] if tq == 0 and hk < 2 else []))
                for blk in range(NKC):
                    w_ = [ldt] + ho_fm_toks
                    if hk == 0:
                        w_ = w_ + p.bwait(blk)
                    last[blk] = p.emit("tensor", lambda h, blk=blk, hk=hk, tq=tq, par=par: h.matmul(
                        psum[blk][:, 0:512],
                        wb[:, 8192 + par * 1024 + blk * 128:8192 + par * 1024 + blk * 128 + 128],
                        aC[:, hk * 2048 + tq * 512:hk * 2048 + tq * 512 + 512],
                        start=(hk == 0), stop=(hk == 3), skip_group_check=True),
                        waits=w_, inc="pe" if hk == 3 or blk == NKC - 1 else None)
                p.bread(("whs", par), last[NKC - 1])
            for blk in range(NKC):
                at = p.emit("vector", lambda h, blk=blk, tq=tq: h.tensor_add(
                    xres[:, blk, tq * 512:tq * 512 + 512],
                    xres[:, blk, tq * 512:tq * 512 + 512],
                    psum[blk][:, 0:512]), waits=[last[blk]], inc="dv")
                p.bread(blk, at)
                woh_adds.append(at)
        ffn_done = woh_adds + wol_adds + [("dv", p.counts["dv"]), ("sc", p.counts["sc"])]
        if "x2" in dumps:
            ffn_done = ffn_done + [p.dma("sync", dumps["x2"], xres[:], waits=ffn_done)]

        fin = p.dma("sync", out_fm, xres[:], waits=ffn_done)
        p.wait_only("sync", [fin, ("dq_sync", p.counts["dq_sync"]),
                             ("dq_gpsimd", p.counts["dq_gpsimd"])])

        @block.sync
        def _(h):
            for f in p.ops["sync"]:
                f(h)

        @block.tensor
        def _(h):
            for f in p.ops["tensor"]:
                f(h)

        @block.vector
        def _(h):
            for f in p.ops["vector"]:
                f(h)

        @block.scalar
        def _(h):
            for f in p.ops["scalar"]:
                f(h)

        @block.gpsimd
        def _(h):
            for f in p.ops["gpsimd"]:
                f(h)

    return nc, dumps



def make_consts():
    c = np.zeros((128, 16), np.float32)
    c[:, 0] = 1.0
    c[:, 1] = -1.0
    c[:, 2] = 1e-5
    c[:, 3] = 1e-6
    c[:, 4] = -1e30
    c[:, 5] = 1.0 / 8.0
    c[:, 6] = -1.0 / 16.0
    c[:, 7] = 1.0 / 1024.0
    c[:, 8] = 32.0
    c[:, 9] = 1e-12
    c[:, 10] = float(np.log(np.float32(NQ_SEL * 9.0 / 8.0)))
    return c


def make_tril():
    m = np.zeros((128, 2, CG), BF)
    i = np.arange(128)[:, None]
    j = np.arange(CG)[None, :]
    m[:, 0, :] = np.where(j > i, 1e30, 0.0)
    m[:, 1, :] = np.where(j > i + 128, 1e30, 0.0)
    return m


def fm(w, p=128):
    din, dout = w.shape
    return np.ascontiguousarray(w.reshape(din // p, p, dout).transpose(1, 0, 2))


def vec_fm(v, p=128):
    n = v.shape[0]
    return np.ascontiguousarray(v.reshape(n // p, p).T)


def prep_inputs(kw):
    g = {}
    g["ident"] = np.eye(128, dtype=BF)
    g["identf"] = np.eye(128, dtype=np.float32)
    g["onesb"] = np.ones((128, 8), dtype=BF)
    g["onesr"] = np.ones((1, 128), dtype=np.float32)
    g["onesrb"] = np.ones((1, 128), dtype=BF)
    g["consts"] = make_consts()
    g["trilm"] = make_tril()
    g["Wq"] = fm(np.asarray(kw["L0_attn_Wq"]))
    g["Wk"] = fm(np.asarray(kw["L0_attn_Wk"]))
    g["Wv"] = fm(np.asarray(kw["L0_attn_Wv"]))
    g["Wg"] = fm(np.asarray(kw["L0_attn_Wg"]))
    g["Wo"] = fm(np.asarray(kw["L0_attn_Wo"]))
    g["Wgt"] = fm(np.asarray(kw["L0_attn_Wgt"]))
    g["gate_w"] = fm(np.asarray(kw["L0_ffn_gate_w"]))
    w1s = [fm(w) for w in np.asarray(kw["L0_ffn_w1"])]
    w3s = [fm(w) for w in np.asarray(kw["L0_ffn_w3"])]
    g["w13"] = np.ascontiguousarray(np.stack(
        [np.concatenate([a, b], axis=2) for a, b in zip(w1s, w3s)]))
    g["w2"] = np.ascontiguousarray(np.stack([fm(w) for w in np.asarray(kw["L0_ffn_w2"])]))
    g["W_inT"] = fm(np.ascontiguousarray(np.asarray(kw["L1_attn_W_in"]).T))
    g["conv_w"] = np.ascontiguousarray(
        np.asarray(kw["L1_attn_conv_w"])[:, 0, :].reshape(NDC, 128, 4).transpose(1, 0, 2))
    g["conv_b"] = vec_fm(np.asarray(kw["L1_attn_conv_b"]))
    g["W_xT"] = fm(np.ascontiguousarray(np.asarray(kw["L1_attn_W_x"]).T))
    g["W_dt"] = vec_fm(np.asarray(kw["L1_attn_W_dt"])[:, 0])
    g["b_dt"] = vec_fm(np.asarray(kw["L1_attn_b_dt"]))
    g["A_log"] = np.ascontiguousarray(
        np.asarray(kw["L1_attn_A_log"]).reshape(NDC, 128, NS).transpose(1, 0, 2))
    g["D_param"] = vec_fm(np.asarray(kw["L1_attn_D_param"]))
    g["W_outT"] = fm(np.ascontiguousarray(np.asarray(kw["L1_attn_W_out"]).T))
    g["gamma_light"] = vec_fm(np.asarray(kw["L1_ffn_gamma_light"]))
    g["gamma_heavy"] = vec_fm(np.asarray(kw["L1_ffn_gamma_heavy"]))
    g["wq_light"] = fm(np.asarray(kw["L1_ffn_wq_light"]))
    g["wkv_light"] = fm(np.asarray(kw["L1_ffn_wkv_light"]))
    g["wo_light"] = fm(np.asarray(kw["L1_ffn_wo_light"]))
    g["q_rt"] = vec_fm(np.asarray(kw["L1_ffn_q_routing_token"]))
    wkvh = np.asarray(kw["L1_ffn_wkv_heavy"]).reshape(H1, 2, DH1, D)
    g["WkT_heavy"] = fm(np.ascontiguousarray(
        wkvh[:, 0].transpose(2, 0, 1).reshape(D, HID)))
    g["WvT_heavy"] = fm(np.ascontiguousarray(
        wkvh[:, 1].transpose(2, 0, 1).reshape(D, HID)))
    g["wq_heavy"] = fm(np.asarray(kw["L1_ffn_wq_heavy"]))
    g["wo_heavy"] = fm(np.asarray(kw["L1_ffn_wo_heavy"]))

    x = np.asarray(kw["L0_attn_x"])
    xs = [np.ascontiguousarray(x[c].T.reshape(NKC, 128, T).transpose(1, 0, 2))
          for c in range(8)]
    return g, xs


def run(kw, dump_names=(), cores=8):
    nc, dumps = build_program(dump_names)
    g, xs = prep_inputs(kw)
    in_maps = [dict(g, x_fm=xs[c]) for c in range(cores)]
    res = run_bass_kernel_spmd(nc, in_maps, core_ids=list(range(cores)))
    return res


def kernel(**kw):
    res = run(kw)
    outs = []
    for c in range(8):
        ofm = np.asarray(res.results[c]["out_fm"], dtype=np.float32)
        outs.append(ofm.transpose(1, 0, 2).reshape(D, T).T)
    return np.stack(outs).astype(np.float32)
